# version 02
# baseline (speedup 1.0000x reference)
import math
import numpy as np
import ml_dtypes
import concourse.bass as bass
import concourse.mybir as mybir
from concourse.bass_utils import run_bass_kernel_spmd
from concourse.bass_core_rust import IndirectOffsetOnAxis

F32 = mybir.dt.float32
BF16 = mybir.dt.bfloat16
I32 = mybir.dt.int32
AF = mybir.ActivationFunctionType
ALU = mybir.AluOpType
AX = mybir.AxisListType

D = 1024
T = 4096
NT = T // 128
DC = D // 128
H0, G0, HD0 = 16, 4, 64
WIN = 256
L1_NH, L1_DK, L1_DC, L1_DHR = 8, 128, 32, 32
L1_SCALE = 1.0 / math.sqrt(L1_DK + L1_DHR)
L0_SCALE = 1.0 / math.sqrt(HD0)
FFH = 4096
NE = 8

BF = ml_dtypes.bfloat16

HEADS_A = [0, 1, 2, 3, 8, 9, 10, 11]
HEADS_B = [4, 5, 6, 7, 12, 13, 14, 15]
Q_PERM = np.concatenate([
    np.concatenate([np.arange(HEADS_A[c] * 64, HEADS_A[c] * 64 + 64),
                    np.arange(HEADS_B[c] * 64, HEADS_B[c] * 64 + 64)])
    for c in range(8)])


def head_slot(h):
    half = (h // 4) % 2
    c = (HEADS_A if half == 0 else HEADS_B).index(h)
    return c, half



def host_consts():
    c = {}
    c["ident_b"] = np.eye(128, dtype=BF)
    c["ident_f"] = np.eye(128, dtype=np.float32)
    p64 = np.zeros((64, 64), np.float32)
    for i in range(32):
        p64[2 * i, 2 * i + 1] = -1.0
        p64[2 * i + 1, 2 * i] = 1.0
    p128 = np.zeros((128, 128), np.float32)
    p128[0:64, 0:64] = p64
    p128[64:128, 64:128] = p64
    c["rope_lhsT"] = np.ascontiguousarray(p128.T).astype(BF)
    inv_freq = (1.0 / (10000.0 ** (np.arange(0, HD0, 2, dtype=np.float32) / HD0))).astype(np.float32)
    t = np.arange(T, dtype=np.float32)
    freqs = t[:, None] * inv_freq[None, :]
    emb = np.concatenate([freqs, freqs], axis=-1)
    sin64 = np.sin(emb).astype(np.float32)
    cos64 = np.cos(emb).astype(np.float32)
    c["cosk"] = np.ascontiguousarray(np.vstack([cos64.T, cos64.T])).astype(BF)
    c["sink"] = np.ascontiguousarray(np.vstack([sin64.T, sin64.T])).astype(BF)
    c["cos_tok"] = np.ascontiguousarray(np.hstack([cos64, cos64])).astype(BF)
    c["sin_tok"] = np.ascontiguousarray(np.hstack([sin64, sin64])).astype(BF)
    p = np.arange(128)
    c["winp2a"] = np.where(p[:, None] - p[None, :] >= 0, 0.0, -1e30).astype(BF)
    c["winm2a"] = np.where(p[:, None] - p[None, :] <= 0, 0.0, -1e30).astype(BF)
    c["ltm4"] = np.ascontiguousarray(np.tril(np.ones((4, 4), np.float32), -1)).astype(BF)
    return c


def host_layout(inp):
    a = {}
    cc = np.ascontiguousarray
    a["tg"] = inp["L0_attn_global_positions"].astype(np.int32).reshape(4, 1)
    a["wq"] = cc(np.asarray(inp["L0_attn_Wq"])[:, Q_PERM])
    a["bqt"] = cc(np.asarray(inp["L0_attn_bq"])[Q_PERM].reshape(8, 128).T)
    a["wk"] = cc(inp["L0_attn_Wk"])
    a["bkt"] = cc(np.asarray(inp["L0_attn_bk"]).reshape(2, 128).T)
    a["wv"] = cc(inp["L0_attn_Wv"])
    a["bvb"] = cc(np.broadcast_to(np.asarray(inp["L0_attn_bv"]), (128, 256)))
    a["wo_ext"] = cc(np.concatenate([np.asarray(inp["L0_attn_Wo"])[Q_PERM, :],
                                     np.asarray(inp["L0_attn_bo"])[None, :]], axis=0))
    a["wfcT"] = cc(np.asarray(inp["L0_ffn_W_fc"]).T)
    a["wprojT"] = cc(np.asarray(inp["L0_ffn_W_proj"]).T)
    a["lnwT"] = cc(np.asarray(inp["L0_ffn_ln_w"]).reshape(32, 128).T)
    a["lnbT"] = cc(np.asarray(inp["L0_ffn_ln_b"]).reshape(32, 128).T)
    a["wdq"] = cc(inp["L1_attn_W_DQ"])
    a["bdq"] = cc(np.asarray(inp["L1_attn_b_DQ"]).reshape(32, 1))
    a["wuq"] = cc(inp["L1_attn_W_UQ"])
    a["buqt"] = cc(np.asarray(inp["L1_attn_b_UQ"]).reshape(8, 128).T)
    a["wdkv"] = cc(inp["L1_attn_W_DKV"])
    a["bdkv"] = cc(np.asarray(inp["L1_attn_b_DKV"]).reshape(32, 1))
    a["wkr"] = cc(inp["L1_attn_W_KR"])
    a["bkr"] = cc(np.asarray(inp["L1_attn_b_KR"]).reshape(32, 1))
    wuk = np.asarray(inp["L1_attn_W_UK"])
    a["wukT"] = cc(np.stack([wuk[:, h * 128:(h + 1) * 128].T for h in range(8)]))
    a["bukt"] = cc(np.asarray(inp["L1_attn_b_UK"]).reshape(8, 128).T)
    wuv = np.asarray(inp["L1_attn_W_UV"])
    a["wuvT"] = cc(np.stack([wuv[:, h * 128:(h + 1) * 128].T for h in range(8)]))
    a["buvt"] = cc(np.asarray(inp["L1_attn_b_UV"]).reshape(8, 128).T)
    a["wqr"] = cc(inp["L1_attn_W_QR"])
    a["bqrt"] = cc(np.asarray(inp["L1_attn_b_QR"]).reshape(8, 32).T)
    a["bdkvb"] = cc(np.broadcast_to(np.asarray(inp["L1_attn_b_DKV"]), (128, 32)))
    a["wo1"] = cc(inp["L1_attn_W_O"])
    a["bo1"] = cc(np.asarray(inp["L1_attn_b_O"]).reshape(1, 1024))
    a["gateT"] = cc(np.asarray(inp["L1_ffn_gate_w"]).T)
    a["wexT"] = cc(np.stack([np.asarray(inp["L1_ffn_expert_w"][e]).T for e in range(NE)]))
    out = {}
    for k, v in a.items():
        v = np.asarray(v)
        if v.dtype == np.float64:
            v = v.astype(np.float32)
        out[k] = v
    return out



ENGS = ["qs", "qg", "qa", "pe", "dv", "ac", "gp"]
XENGS = ["dvx", "acx", "gpx", "pex"]


class Ctx:
    def __init__(self, name="prog"):
        import os
        drc = os.environ.get("BASS_DRC", "1") == "1"
        self.nc = bass.Bass("TRN2", target_bir_lowering=False, detect_race_conditions=drc)
        self.name = name
        self._stack = []
        self.sems = {}
        self.base = {}
        for s in ENGS:
            self.sems[s] = self._enter(self.nc.semaphore(f"sem_{s}"))
            self.base[s] = 0
        self.xcnt = {}
        for s in XENGS:
            self.sems[s] = self._enter(self.nc.semaphore(f"sem_{s}"))
            self.xcnt[s] = 0
        self.consts = {}
        self.pending_sw = []

    def swsem(self):
        if not hasattr(self, "_swsems"):
            self._swsems = []
        n = len(self._swsems)
        h = self._enter(self.nc.semaphore(f"sem_sw{n}"))
        self._swsems.append(h)
        self.pending_sw.append((h, 16))
        return h

    def settle(self, e, key, instr):
        self.xcnt[key] += 1
        instr.then_inc(self.sems[key], 1)
        e.wait_ge(self.sems[key], self.xcnt[key])

    def _enter(self, cm):
        v = cm.__enter__()
        self._stack.append(cm)
        return v

    def sbuf(self, name, shape, dtype):
        ar = getattr(self, "arena", None)
        if ar is not None:
            return ar.sbuf(name, shape, dtype)
        return self._enter(self.nc.sbuf_tensor(name, shape, dtype))

    def psum(self, name, shape, dtype=F32):
        ar = getattr(self, "arena", None)
        if ar is not None:
            return ar.psum(name, shape, dtype)
        return self._enter(self.nc.psum_tensor(name, shape, dtype))

    def scope(self):
        return Scope(self)

    def close(self):
        while self._stack:
            self._stack.pop().__exit__(None, None, None)


class Scope:

    def __init__(self, C):
        self.C = C
        self.cms = []

    def _enter(self, cm):
        v = cm.__enter__()
        self.cms.append(cm)
        return v

    def sbuf(self, name, shape, dtype):
        return self._enter(self.C.nc.sbuf_tensor(name, shape, dtype))

    def psum(self, name, shape, dtype=F32):
        return self._enter(self.C.nc.psum_tensor(name, shape, dtype))

    def close(self):
        while self.cms:
            self.cms.pop().__exit__(None, None, None)


def emit_fence(C, engines=None, queues=None):
    targets = [(C.sems[s], v) for s, v in C.base.items() if v > 0]
    targets += list(C.pending_sw)
    C.pending_sw = []
    with C.nc.Block() as blk:
        def mk(e):
            for h, v in targets:
                e.wait_ge(h, v)
        blk.tensor(lambda e: mk(e))
        blk.vector(lambda e: mk(e))
        blk.scalar(lambda e: mk(e))
        blk.gpsimd(lambda e: mk(e))
        blk.sync(lambda e: mk(e))


class Sched:

    def __init__(self, C, tiles, table, pre=None):
        self.C = C
        self.tiles = tiles
        self.table = {k: list(v) for k, v in table.items()}
        self.pre = dict(pre or {})
        self.base0 = dict(C.base)
        self.b = {}
        for e in table:
            step = 16 if e in ("qs", "qg", "qa") else 1
            self.b[e] = C.base[e] + self.pre.get(e, 0) * step

    def step(self, eng):
        return 16 if eng in ("qs", "qg", "qa") else 1

    def per(self, eng):
        return len(self.table[eng]) * self.step(eng)

    def t(self, eng, i, name):
        idx = self.table[eng].index(name) + 1
        return self.b[eng] + i * self.per(eng) + idx * self.step(eng)

    def pre_t(self, eng, k=None):
        n = self.pre.get(eng, 0) if k is None else k
        return self.base0[eng] + n * self.step(eng)

    def done(self):
        for e in self.table:
            self.C.base[e] = self.b[e] + self.tiles * self.per(e)
        return self



ALL_CONSTS = [("ident_f", "c_identf", [128, 128], F32, "ident_f"),
              ("ident_b", "c_identb", [128, 128], BF16, "ident_b"),
              ("rope", "c_rope", [128, 128], BF16, "rope_lhsT"),
              ("cosk", "c_cosk", [128, T], BF16, "cosk"),
              ("sink", "c_sink", [128, T], BF16, "sink"),
              ("winp2a", "c_winp2a", [128, 128], BF16, "winp2a"),
              ("winm2a", "c_winm2a", [128, 128], BF16, "winm2a")]


def emit_consts(C, hb, names=None):
    nc = C.nc
    K = C.consts
    sel = [t for t in ALL_CONSTS if names is None or t[0] in names]
    for key, sbn, shape, dt, _src in sel:
        K[key] = C.sbuf(sbn, shape, dt)
    derived = "iotaq" not in K
    if derived:
        K["iotaq"] = C.sbuf("c_iotaq", [128, NT], F32)
        K["ones_r"] = C.sbuf("c_onesr", [1, 128], BF16)
        K["ones_rf"] = C.sbuf("c_onesrf", [1, 128], F32)
        K["cvals"] = C.sbuf("c_cvals", [128, 4], F32)
    with nc.Block() as blk:
        @blk.sync
        def _(e):
            for key, _sbn, _shape, _dt, srcnm in sel:
                e.dma_start(out=K[key][:, :], in_=hb[srcnm][:, :]).then_inc(C.sems["qs"], 16)

        if derived:
            @blk.gpsimd
            def _(e):
                e.iota(K["iotaq"][:, :], [[128, NT]], base=0, channel_multiplier=1,
                       allow_small_or_imprecise_dtypes=True)
                e.memset(K["ones_r"][:, :], 1.0)
                e.memset(K["ones_rf"][:, :], 1.0)
                e.memset(K["cvals"][:, 0:1], 1e-5).then_inc(C.sems["gp"], 1)
    C.base["qs"] += len(sel) * 16
    if derived:
        C.base["gp"] += 1
    C.consts_ready = (C.base["qs"], C.base["gp"])


def wait_consts(C, e):
    e.wait_ge(C.sems["qs"], C.consts_ready[0])
    e.wait_ge(C.sems["gp"], C.consts_ready[1])



def emit_phase_a(C, x_hbm, xnt_hbm, layer, gate=None, write_xnt_hbm=True, dbg=None):
    nc = C.nc
    g = gate is not None
    xnt_sb = C.sbuf(f"xnt_sb_{layer}", [128, DC * T], BF16)
    sc = C.scope()
    xb = sc.sbuf(f"pa_x_{layer}", [128, 2 * D], F32)
    xnf = sc.sbuf(f"pa_xn_{layer}", [128, 2 * D], F32)
    xsq = sc.sbuf(f"pa_sq_{layer}", [128, D], F32)
    st = sc.sbuf(f"pa_st_{layer}", [128, 8], F32)
    ps_t = sc.psum(f"pa_pt_{layer}", [128, 2 * D], F32)
    if g:
        gf32 = sc.sbuf(f"pa_gf32_{layer}", [128, 2 * D], F32)
        ohsb = sc.sbuf(f"pa_oh_{layer}", [128, 2 * 8], BF16)
        xtokb = sc.sbuf(f"pa_xtok_{layer}", [128, 2 * D], BF16)
        gatew = sc.sbuf(f"pa_gatew_{layer}", [128, 8 * DC], F32)
        ps_g = sc.psum(f"pa_pg_{layer}", [128, 2 * 512], F32)

    tbl = {
        "qs": ["x"],
        "dv": (["var", "rstd", "xn"] + (["gmax", "oh"] if g else [])),
        "ac": (["sq", "sqrt", "gf32", "xnt", "xtok"] if g
               else ["sq", "sqrt", "xnt"]),
        "pe": (["tr", "gmm"] if g else ["tr"]),
        "qa": (["oh", "xtok"] if g else []),
    }
    S = Sched(C, NT, tbl, pre={"qs": 1 if g else 0})

    with nc.Block() as blk:
        @blk.sync
        def _(e):
            if g:
                e.dma_start(out=gatew.ap().rearrange("p (c f) -> p c f", c=DC),
                            in_=gate["gateT_hbm"].rearrange("(c p) f -> p c f", p=128)
                            ).then_inc(C.sems["qs"], 16)
            for i in range(NT):
                if i >= 2:
                    e.wait_ge(C.sems["dv"], S.t("dv", i - 2, "xn"))
                    e.wait_ge(C.sems["ac"], S.t("ac", i - 2, "sq"))
                e.dma_start(out=xb[:, (i % 2) * D:(i % 2 + 1) * D],
                            in_=x_hbm.rearrange("(n p) m -> n p m", p=128)[i, :, :]
                            ).then_inc(C.sems["qs"], 16)
            if write_xnt_hbm:
                e.wait_ge(C.sems["ac"], S.t("ac", NT - 1, "xnt"))
                for c in range(DC):
                    e.dma_start(out=xnt_hbm.rearrange("(c p) t -> c p t", p=128)[c, :, :],
                                in_=xnt_sb[:, c * T:(c + 1) * T]).then_inc(C.sems["qs"], 16)

        @blk.vector
        def _(e):
            for i in range(NT):
                buf = xb[:, (i % 2) * D:(i % 2 + 1) * D]
                nb = xnf[:, (i % 2) * D:(i % 2 + 1) * D]
                e.wait_ge(C.sems["qs"], S.t("qs", min(i + 1, NT - 1), "x"))
                negmu = st[:, 0:1]; ex2 = st[:, 1:2]; var = st[:, 2:3]
                rstd = st[:, 3:4]; sq = st[:, 5:6]
                e.reduce_sum(out=negmu, in_=buf, axis=AX.X, negate=True)
                e.wait_ge(C.sems["ac"], S.t("ac", i, "sq"))
                ins = e.reduce_sum(out=ex2, in_=xsq[:, :], axis=AX.X)
                C.settle(e, "dvx", ins)
                e.tensor_scalar(out=negmu, in0=negmu, scalar1=1.0 / D, scalar2=None, op0=ALU.mult)
                ins = e.tensor_scalar(out=ex2, in0=ex2, scalar1=1.0 / D, scalar2=None, op0=ALU.mult)
                C.settle(e, "dvx", ins)
                ins = e.tensor_tensor(out=var, in0=negmu, in1=negmu, op=ALU.mult)
                C.settle(e, "dvx", ins)
                e.tensor_sub(out=var, in0=ex2, in1=var).then_inc(C.sems["dv"], 1)
                e.wait_ge(C.sems["ac"], S.t("ac", i, "sqrt"))
                e.reciprocal(out=rstd, in_=sq).then_inc(C.sems["dv"], 1)
                e.wait_ge(C.sems["dv"], S.t("dv", i, "rstd"))
                if i >= 2:
                    e.wait_ge(C.sems["pe"], S.t("pe", i - 2, "tr"))
                    if g:
                        e.wait_ge(C.sems["ac"], S.t("ac", i - 2, "xtok"))
                e.tensor_scalar(out=nb, in0=buf, scalar1=negmu, scalar2=rstd,
                                op0=ALU.add, op1=ALU.mult).then_inc(C.sems["dv"], 1)
                if g:
                    e.wait_ge(C.sems["pe"], S.t("pe", i, "gmm"))
                    gm = st[:, 6:7]
                    psg = ps_g[:, (i % 2) * 512:(i % 2) * 512 + 8]
                    e.reduce_max(out=gm, in_=psg, axis=AX.X).then_inc(C.sems["dv"], 1)
                    e.wait_ge(C.sems["dv"], S.t("dv", i, "gmax"))
                    if i >= 2:
                        e.wait_ge(C.sems["qa"], S.t("qa", i - 2, "oh"))
                    e.tensor_scalar(out=ohsb[:, (i % 2) * 8:(i % 2 + 1) * 8], in0=psg,
                                    scalar1=gm, scalar2=None,
                                    op0=ALU.is_ge).then_inc(C.sems["dv"], 1)

        @blk.scalar
        def _(e):
            wait_consts(C, e)
            for i in range(NT):
                buf = xb[:, (i % 2) * D:(i % 2 + 1) * D]
                nb = xnf[:, (i % 2) * D:(i % 2 + 1) * D]
                e.wait_ge(C.sems["qs"], S.t("qs", min(i + 1, NT - 1), "x"))
                if i >= 1:
                    e.wait_ge(C.sems["dv"], S.t("dv", i - 1, "var"))
                e.activation(out=xsq[:, :], in_=buf, func=AF.Square).then_inc(C.sems["ac"], 1)
                e.wait_ge(C.sems["dv"], S.t("dv", i, "var"))
                e.activation(out=st[:, 5:6], in_=st[:, 2:3], func=AF.Sqrt,
                             bias=C.consts["cvals"][:, 0:1]).then_inc(C.sems["ac"], 1)
                e.wait_ge(C.sems["pe"], S.t("pe", i, "tr"))
                if g:
                    if i >= 2:
                        e.wait_ge(C.sems["pe"], S.t("pe", i - 2, "gmm"))
                    e.activation(out=gf32[:, (i % 2) * D:(i % 2 + 1) * D],
                                 in_=ps_t[:, (i % 2) * D:(i % 2 + 1) * D],
                                 func=AF.Copy).then_inc(C.sems["ac"], 1)
                e.activation(
                    out=xnt_sb.ap().rearrange("p (c t) -> p c t", c=DC)[:, :, i * 128:(i + 1) * 128],
                    in_=ps_t.ap().rearrange("p (c t) -> p c t", c=2 * DC)[:, (i % 2) * DC:(i % 2 + 1) * DC, :],
                    func=AF.Copy).then_inc(C.sems["ac"], 1)
                if g:
                    if i >= 2:
                        e.wait_ge(C.sems["qa"], S.t("qa", i - 2, "xtok"))
                    e.wait_ge(C.sems["dv"], S.t("dv", i, "xn"))
                    e.activation(out=xtokb[:, (i % 2) * D:(i % 2 + 1) * D], in_=nb,
                                 func=AF.Copy).then_inc(C.sems["ac"], 1)
                    e.wait_ge(C.sems["ac"], S.t("ac", i, "xtok"))
                    e.wait_ge(C.sems["dv"], S.t("dv", i, "oh"))
                    e.dma_start(out=gate["oh_hbm"][i * 128:(i + 1) * 128, :],
                                in_=ohsb[:, (i % 2) * 8:(i % 2 + 1) * 8]).then_inc(C.sems["qa"], 16)
                    e.dma_start(out=gate["xtok_hbm"][i * 128:(i + 1) * 128, :],
                                in_=xtokb[:, (i % 2) * D:(i % 2 + 1) * D]).then_inc(C.sems["qa"], 16)

        @blk.tensor
        def _(e):
            wait_consts(C, e)
            for i in range(NT):
                nb = xnf[:, (i % 2) * D:(i % 2 + 1) * D]
                e.wait_ge(C.sems["dv"], S.t("dv", i, "xn"))
                if i >= 2:
                    e.wait_ge(C.sems["ac"], S.t("ac", i - 2, "xnt"))
                for c in range(DC):
                    ins = e.transpose(ps_t[:, (i % 2) * D + c * 128:(i % 2) * D + (c + 1) * 128],
                                      nb[:, c * 128:(c + 1) * 128], C.consts["ident_f"][:, :])
                ins.then_inc(C.sems["pe"], 1)
                if g:
                    e.wait_ge(C.sems["ac"], S.t("ac", i, "gf32"))
                    e.wait_ge(C.sems["qs"], S.pre_t("qs"))
                    if i >= 2:
                        e.wait_ge(C.sems["dv"], S.t("dv", i - 2, "oh"))
                    for c in range(DC):
                        ins = e.matmul(ps_g[:, (i % 2) * 512:(i % 2) * 512 + 8],
                                       gf32[:, (i % 2) * D + c * 128:(i % 2) * D + (c + 1) * 128],
                                       gatew[:, c * 8:(c + 1) * 8],
                                       start=(c == 0), stop=(c == DC - 1))
                    ins.then_inc(C.sems["pe"], 1)

        if dbg is not None:
            @blk.gpsimd
            def _(e):
                for i in range(NT):
                    e.wait_ge(C.sems["dv"], S.t("dv", i, "xn"))
                    e.dma_start(out=dbg[i * 128:(i + 1) * 128, :],
                                in_=xnf[:, (i % 2) * D:(i % 2 + 1) * D]).then_inc(C.sems["qg"], 16)
                C.base["qg"] += NT * 16


    S.done()
    if write_xnt_hbm:
        C.base["qs"] += DC * 16
    emit_fence(C)
    sc.close()
    return xnt_sb



def emit_l0_kv(C, xnt_sb, hb, parts=3):
    nc = C.nc
    if not hasattr(C, "stage"):
        C.stage = C.sbuf("stage", [128, 4096], F32)
    kt_sb = C.sbuf("l0_kt", [128, 2 * T], BF16)
    vext = [C.sbuf(f"l0_vext{g}", [128, NT * 65], BF16) for g in range(G0)]
    wq_sb = C.sbuf("l0_wq", [128, 8192], BF16)
    bkt_sb = C.sbuf("l0_bkt", [128, 2], F32)
    bqt_sb = C.sbuf("l0_bqt", [128, 8], F32)
    bvb_sb = C.sbuf("l0_bvb", [128, 256], F32)
    wk_sb = C.sbuf("l0_wk", [128, 2048], BF16)
    wv_sb = C.sbuf("l0_wv", [128, 2048], BF16)
    sc = C.scope()
    kb_sb = sc.sbuf("l0_kb", [128, 2 * 512], BF16)
    rot_sb = sc.sbuf("l0_rot", [128, 2 * 512], BF16)
    t1_sb = sc.sbuf("l0_t1", [128, 512], BF16)
    pk = sc.psum("l0_pk", [128, 2 * 512], F32)
    pr = sc.psum("l0_pr", [128, 2 * 512], F32)
    pv = sc.psum("l0_pv", [128, 2 * 512], F32)
    C.l0 = dict(kt=kt_sb, vext=vext, wq=wq_sb, wk=wk_sb, wv=wv_sb,
                bkt=bkt_sb, bqt=bqt_sb, bvb=bvb_sb)

    Sp = Serial(C)
    Sp.add("sy", lambda e: e.dma_start(out=bkt_sb[:, :], in_=hb["bkt"][:, :]), dma=True)
    Sp.add("sy", lambda e: e.dma_start(out=bqt_sb[:, :], in_=hb["bqt"][:, :]), dma=True)
    Sp.add("sy", lambda e: e.dma_start(out=bvb_sb[:, :], in_=hb["bvb"][:, :]), dma=True)
    for hh in range(2):
        Sp.add("sy", lambda e, hh=hh: e.dma_start(
            out=C.stage.ap().rearrange("p (c f) -> p c f", c=4),
            in_=hb["wq"].rearrange("(g c p) f -> g p c f", p=128, c=4)[hh, :, :, :]), dma=True)
        Sp.add("gp", lambda e, hh=hh: e.tensor_copy(out=wq_sb[:, hh * 4096:(hh + 1) * 4096],
                                                    in_=C.stage[:, :]))
    Sp.add("sy", lambda e: e.dma_start(out=C.stage[:, 0:2048].rearrange("p (c f) -> p c f", c=DC),
                                       in_=hb["wk"].rearrange("(c p) f -> p c f", p=128)), dma=True)
    Sp.add("gp", lambda e: e.tensor_copy(out=wk_sb[:, :], in_=C.stage[:, 0:2048]))
    Sp.add("sy", lambda e: e.dma_start(out=C.stage[:, 0:2048].rearrange("p (c f) -> p c f", c=DC),
                                       in_=hb["wv"].rearrange("(c p) f -> p c f", p=128)), dma=True)
    Sp.add("gp", lambda e: e.tensor_copy(out=wv_sb[:, :], in_=C.stage[:, 0:2048]))
    def _memsets(e):
        for g in range(G0):
            e.memset(vext[g][:, :], 1.0)
        return e.memset(kt_sb[:, :], 0.0)
    Sp.add("gp", _memsets)
    Sp.emit()
    emit_fence(C)

    NS = 16 if parts in (2, 3) else 0
    Sk = Sched(C, NS, {
        "pe": ["kmm", "rot"],
        "dv": ["kb", "kcomb"],
        "ac": ["rote"],
    })

    with nc.Block() as blk:
        @blk.tensor
        def _(e):
            for s in range(NS):
                c2, sl = s // 8, s % 8
                pkb = pk[:, (s % 2) * 512:(s % 2 + 1) * 512]
                if s >= 2:
                    e.wait_ge(C.sems["dv"], Sk.t("dv", s - 2, "kb"))
                for dch in range(DC):
                    ins = e.matmul(pkb,
                                   wk_sb[:, dch * 256 + c2 * 128:dch * 256 + (c2 + 1) * 128],
                                   xnt_sb[:, dch * T + sl * 512:dch * T + (sl + 1) * 512],
                                   start=(dch == 0), stop=(dch == DC - 1))
                ins.then_inc(C.sems["pe"], 1)
                e.wait_ge(C.sems["dv"], Sk.t("dv", s, "kb"))
                if s >= 2:
                    e.wait_ge(C.sems["ac"], Sk.t("ac", s - 2, "rote"))
                ins = e.matmul(pr[:, (s % 2) * 512:(s % 2 + 1) * 512],
                               C.consts["rope"][:, :],
                               kb_sb[:, (s % 2) * 512:(s % 2 + 1) * 512],
                               start=True, stop=True)
                ins.then_inc(C.sems["pe"], 1)

        @blk.scalar
        def _(e):
            for s in range(NS):
                e.wait_ge(C.sems["pe"], Sk.t("pe", s, "rot"))
                if s >= 2:
                    e.wait_ge(C.sems["dv"], Sk.t("dv", s - 2, "kcomb"))
                e.activation(out=rot_sb[:, (s % 2) * 512:(s % 2 + 1) * 512],
                             in_=pr[:, (s % 2) * 512:(s % 2 + 1) * 512],
                             func=AF.Copy).then_inc(C.sems["ac"], 1)

        @blk.vector
        def _(e):
            for s in range(NS):
                c2, sl = s // 8, s % 8
                cs = slice(sl * 512, (sl + 1) * 512)
                e.wait_ge(C.sems["pe"], Sk.t("pe", s, "kmm"))
                if s >= 2:
                    e.wait_ge(C.sems["pe"], Sk.t("pe", s - 2, "rot"))
                e.tensor_scalar(out=kb_sb[:, (s % 2) * 512:(s % 2 + 1) * 512],
                                in0=pk[:, (s % 2) * 512:(s % 2 + 1) * 512],
                                scalar1=bkt_sb[:, c2:c2 + 1], scalar2=None,
                                op0=ALU.add).then_inc(C.sems["dv"], 1)
                e.wait_ge(C.sems["ac"], Sk.t("ac", s, "rote"))
                e.tensor_tensor(out=t1_sb[:, :], in0=kb_sb[:, (s % 2) * 512:(s % 2 + 1) * 512],
                                in1=C.consts["cosk"][:, cs], op=ALU.mult)
                e.tensor_tensor(out=kt_sb[:, c2 * T:][:, cs], in0=rot_sb[:, (s % 2) * 512:(s % 2 + 1) * 512],
                                in1=C.consts["sink"][:, cs], op=ALU.mult)
                e.tensor_tensor(out=kt_sb[:, c2 * T:][:, cs], in0=kt_sb[:, c2 * T:][:, cs],
                                in1=t1_sb[:, :], op=ALU.add).then_inc(C.sems["dv"], 1)
    Sk.done()

    NV = NT if parts in (3, 4) else 0
    Sv = Sched(C, NV, {"pe": ["vmm"], "dv": ["vb"]})
    with nc.Block() as blk:
        @blk.tensor
        def _(e):
            for i in range(NV):
                pvb = pv[:, (i % 2) * 512:(i % 2) * 512 + 256]
                if i >= 2:
                    e.wait_ge(C.sems["dv"], Sv.t("dv", i - 2, "vb"))
                for dch in range(DC):
                    ins = e.matmul(pvb,
                                   xnt_sb[:, dch * T + i * 128:dch * T + (i + 1) * 128],
                                   wv_sb[:, dch * 256:(dch + 1) * 256],
                                   start=(dch == 0), stop=(dch == DC - 1))
                ins.then_inc(C.sems["pe"], 1)

        @blk.vector
        def _(e):
            for i in range(NV):
                e.wait_ge(C.sems["pe"], Sv.t("pe", i, "vmm"))
                for g in range(G0):
                    ins = e.tensor_tensor(out=vext[g][:, i * 65:i * 65 + 64],
                                          in0=pv[:, (i % 2) * 512 + g * 64:(i % 2) * 512 + (g + 1) * 64],
                                          in1=bvb_sb[:, g * 64:(g + 1) * 64], op=ALU.add)
                ins.then_inc(C.sems["dv"], 1)
    Sv.done()
    emit_fence(C)
    sc.close()
    return C.l0



class Serial:

    def __init__(self, C):
        self.C = C
        self.steps = []

    def add(self, eng, fn, dma=False, sw=False):
        self.steps.append((eng, fn, dma, sw))

    def emit(self, extra_waits=None):
        C = self.C
        if "sr" not in C.sems:
            C.sems["sr"] = C._enter(C.nc.semaphore("sem_sr"))
            C.base["sr"] = 0
        targets = []
        prev = None
        for eng, fn, dma, sw in self.steps:
            start = prev
            if sw:
                h = C.swsem()
                prev = (h, 16)
            else:
                C.base["sr"] += 16 if dma else 1
                prev = (C.sems["sr"], C.base["sr"])
            targets.append((eng, fn, dma, sw, start, prev))
        engines = {"pe": "tensor", "dv": "vector", "ac": "scalar", "gp": "gpsimd", "sy": "sync"}
        with C.nc.Block() as blk:
            for ename, bname in engines.items():
                mine = [t for t in targets if t[0] == ename]
                if not mine:
                    continue

                def body(e, mine=mine):
                    if extra_waits:
                        for s, val in extra_waits:
                            e.wait_ge(C.sems[s], val)
                    for _, fn, dma, sw, start, end in mine:
                        if start is not None:
                            e.wait_ge(start[0], start[1])
                        ins = fn(e)
                        ins.then_inc(end[0], 16 if dma else 1)
                getattr(blk, bname)(body)
        self.final = prev



def emit_l0_globals_pre(C, hb, x_hbm):
    nc = C.nc
    g = {}
    g["tgi"] = C.sbuf("g_tgi", [4, 1], I32)
    g["tgb"] = C.sbuf("g_tgb", [128, 4], F32)
    g["nodupb"] = C.sbuf("g_nodupb", [128, 4], F32)
    g["xngt"] = C.sbuf("g_xngt", [128, 32], BF16)
    g["kgt"] = C.sbuf("g_kgt", [128, 8], BF16)
    g["vg"] = C.sbuf("g_vg", [4, 260], BF16)
    g["onesall"] = C.sbuf("g_ones", [128, 128], BF16)
    sc = C.scope()
    g["misc"] = sc.sbuf("g_misc", [128, 16], F32)
    g["eq"] = sc.sbuf("g_eq", [4, 8], F32)
    g["ltm4"] = sc.sbuf("g_ltm4", [4, 4], BF16)
    g["xg"] = sc.sbuf("g_xg", [4, 1024], F32)
    g["xgsq"] = sc.sbuf("g_xgsq", [4, 1024], F32)
    g["xng"] = sc.sbuf("g_xng", [4, 1024], F32)
    g["ctok"] = sc.sbuf("g_ctok", [4, 256], BF16)
    g["cosg"] = sc.sbuf("g_cosg", [128, 8], BF16)
    g["kgb"] = sc.sbuf("g_kgb", [128, 8], BF16)
    ps = sc.psum("g_ps", [128, 512], F32)
    psb = sc.psum("g_psb", [128, 128], BF16)
    S = Serial(C)
    K = C.consts

    S.add("sy", lambda e: e.dma_start(out=g["tgi"][:, :], in_=hb["tg"][:, :]), dma=True)
    S.add("gp", lambda e: e.memset(g["vg"][:, :], 1.0))
    S.add("gp", lambda e: e.memset(g["onesall"][:, :], 1.0))
    S.add("sy", lambda e: e.dma_start(out=g["ltm4"][:, :], in_=hb["ltm4"][:, :]), dma=True)
    S.add("dv", lambda e: e.tensor_copy(out=g["misc"][0:4, 0:1], in_=g["tgi"][:, :]))
    S.add("pe", lambda e: e.transpose(ps[0:1, 0:4], g["misc"][0:4, 0:1], K["ident_f"][0:4, 0:4]))
    S.add("dv", lambda e: e.tensor_copy(out=g["misc"][0:1, 4:8], in_=ps[0:1, 0:4]))
    S.add("pe", lambda e: e.matmul(ps[:, 0:4], K["ones_rf"][0:1, :], g["misc"][0:1, 4:8],
                                   start=True, stop=True))
    S.add("dv", lambda e: e.tensor_copy(out=g["tgb"][:, :], in_=ps[:, 0:4]))
    S.add("pe", lambda e: e.matmul(ps[0:4, 4:8], K["ones_rf"][0:1, 0:4], g["misc"][0:1, 4:8],
                                   start=True, stop=True))
    S.add("dv", lambda e: e.tensor_scalar(out=g["eq"][:, 0:4], in0=ps[0:4, 4:8],
                                          scalar1=g["misc"][0:4, 0:1], scalar2=None,
                                          op0=ALU.subtract))
    S.add("dv", lambda e: e.tensor_single_scalar(out=g["eq"][:, 4:8], in_=g["eq"][0:4, 0:4],
                                                 scalar=0.0, op=ALU.is_equal))
    S.add("dv", lambda e: e.tensor_tensor(out=g["ltm4"][:, :], in0=g["eq"][0:4, 4:8],
                                          in1=g["ltm4"][:, :], op=ALU.mult))
    S.add("pe", lambda e: e.matmul(ps[0:4, 12:16], g["ltm4"][:, :], g["onesall"][0:4, 0:4],
                                   start=True, stop=True))
    S.add("dv", lambda e: e.tensor_single_scalar(out=g["misc"][0:4, 1:2], in_=ps[0:4, 12:13],
                                                 scalar=0.5, op=ALU.is_lt))
    S.add("pe", lambda e: e.transpose(ps[0:1, 0:4], g["misc"][0:4, 1:2], K["ident_f"][0:4, 0:4]))
    S.add("dv", lambda e: e.tensor_copy(out=g["misc"][0:1, 8:12], in_=ps[0:1, 0:4]))
    S.add("pe", lambda e: e.matmul(ps[:, 0:4], K["ones_rf"][0:1, :], g["misc"][0:1, 8:12],
                                   start=True, stop=True))
    S.add("dv", lambda e: e.tensor_copy(out=g["nodupb"][:, :], in_=ps[:, 0:4]))

    S.add("gp", lambda e: e.indirect_dma_start(
        out=g["xg"][:, :], out_offset=None, in_=x_hbm[:, :],
        in_offset=IndirectOffsetOnAxis(ap=g["tgi"][:, :], axis=0)), dma=True, sw=True)
    S.add("ac", lambda e: e.activation(out=g["xgsq"][:, :], in_=g["xg"][:, :], func=AF.Square))
    S.add("dv", lambda e: e.reduce_sum(out=g["misc"][0:4, 2:3], in_=g["xg"][:, :], axis=AX.X, negate=True))
    S.add("dv", lambda e: e.reduce_sum(out=g["misc"][0:4, 3:4], in_=g["xgsq"][:, :], axis=AX.X))
    S.add("dv", lambda e: e.tensor_scalar(out=g["misc"][0:4, 2:3], in0=g["misc"][0:4, 2:3],
                                          scalar1=1.0 / D, scalar2=None, op0=ALU.mult))
    S.add("dv", lambda e: e.tensor_scalar(out=g["misc"][0:4, 3:4], in0=g["misc"][0:4, 3:4],
                                          scalar1=1.0 / D, scalar2=None, op0=ALU.mult))
    S.add("dv", lambda e: e.tensor_tensor(out=g["misc"][0:4, 4:5], in0=g["misc"][0:4, 2:3],
                                          in1=g["misc"][0:4, 2:3], op=ALU.mult))
    S.add("dv", lambda e: e.tensor_sub(out=g["misc"][0:4, 4:5], in0=g["misc"][0:4, 3:4],
                                       in1=g["misc"][0:4, 4:5]))
    S.add("ac", lambda e: e.activation(out=g["misc"][0:4, 5:6], in_=g["misc"][0:4, 4:5],
                                       func=AF.Sqrt, bias=K["cvals"][0:4, 0:1]))
    S.add("dv", lambda e: e.reciprocal(out=g["misc"][0:4, 6:7], in_=g["misc"][0:4, 5:6]))
    S.add("dv", lambda e: e.tensor_scalar(out=g["xng"][:, :], in0=g["xg"][:, :],
                                          scalar1=g["misc"][0:4, 2:3], scalar2=g["misc"][0:4, 6:7],
                                          op0=ALU.add, op1=ALU.mult))
    def xtr(e):
        for c in range(DC):
            ins = e.transpose(ps[:, c * 4:(c + 1) * 4], g["xng"][0:4, c * 128:(c + 1) * 128],
                              K["ident_f"][0:4, 0:4])
        return ins
    S.add("pe", xtr)
    S.add("ac", lambda e: e.activation(out=g["xngt"][:, :], in_=ps[:, 0:32], func=AF.Copy))

    S.add("gp", lambda e: e.indirect_dma_start(
        out=g["ctok"][:, 0:128], out_offset=None, in_=hb["cos_tok"][:, :],
        in_offset=IndirectOffsetOnAxis(ap=g["tgi"][:, :], axis=0)), dma=True, sw=True)
    S.add("gp", lambda e: e.indirect_dma_start(
        out=g["ctok"][:, 128:256], out_offset=None, in_=hb["sin_tok"][:, :],
        in_offset=IndirectOffsetOnAxis(ap=g["tgi"][:, :], axis=0)), dma=True, sw=True)
    S.add("pe", lambda e: e.transpose(psb[:, 0:4], g["ctok"][0:4, 0:128], K["ident_b"][0:4, 0:4]))
    S.add("pe", lambda e: e.transpose(psb[:, 4:8], g["ctok"][0:4, 128:256], K["ident_b"][0:4, 0:4]))
    S.add("ac", lambda e: e.activation(out=g["cosg"][:, :], in_=psb[:, 0:8], func=AF.Copy))

    wk = C.l0["wk"]; wv = C.l0["wv"]; bkt = C.l0["bkt"]; bvb = C.l0["bvb"]
    for c2 in range(2):
        def kmm(e, c2=c2):
            for dch in range(DC):
                ins = e.matmul(ps[:, 64 + c2 * 4:64 + (c2 + 1) * 4],
                               wk[:, dch * 256 + c2 * 128:dch * 256 + (c2 + 1) * 128],
                               g["xngt"][:, dch * 4:(dch + 1) * 4],
                               start=(dch == 0), stop=(dch == DC - 1))
            return ins
        S.add("pe", kmm)
    S.add("dv", lambda e: e.tensor_scalar(out=g["kgb"][:, 0:4], in0=ps[:, 64:68],
                                          scalar1=bkt[:, 0:1], scalar2=None, op0=ALU.add))
    S.add("dv", lambda e: e.tensor_scalar(out=g["kgb"][:, 4:8], in0=ps[:, 68:72],
                                          scalar1=bkt[:, 1:2], scalar2=None, op0=ALU.add))
    S.add("pe", lambda e: e.matmul(ps[:, 72:80], C.consts["rope"][:, :], g["kgb"][:, :],
                                   start=True, stop=True))
    S.add("dv", lambda e: e.tensor_copy(out=g["kgt"][:, :], in_=ps[:, 72:80]))
    def kcomb1(e):
        e.tensor_tensor(out=g["kgt"][:, 0:4], in0=g["kgt"][:, 0:4], in1=g["cosg"][:, 4:8], op=ALU.mult)
        e.tensor_tensor(out=g["kgt"][:, 4:8], in0=g["kgt"][:, 4:8], in1=g["cosg"][:, 4:8], op=ALU.mult)
        e.tensor_tensor(out=g["kgb"][:, 0:4], in0=g["kgb"][:, 0:4], in1=g["cosg"][:, 0:4], op=ALU.mult)
        return e.tensor_tensor(out=g["kgb"][:, 4:8], in0=g["kgb"][:, 4:8], in1=g["cosg"][:, 0:4], op=ALU.mult)
    S.add("dv", kcomb1)
    S.add("dv", lambda e: e.tensor_tensor(out=g["kgt"][:, :], in0=g["kgt"][:, :],
                                          in1=g["kgb"][:, :], op=ALU.add))
    def vgmm(e):
        for dch in range(DC):
            ins = e.matmul(ps[0:4, 256:512], g["xngt"][:, dch * 4:(dch + 1) * 4],
                           wv[:, dch * 256:(dch + 1) * 256],
                           start=(dch == 0), stop=(dch == DC - 1))
        return ins
    S.add("pe", vgmm)
    def vgbias(e):
        for gg in range(G0):
            ins = e.tensor_tensor(out=g["vg"][:, gg * 65:gg * 65 + 64],
                                  in0=ps[0:4, 256 + gg * 64:256 + (gg + 1) * 64],
                                  in1=bvb[0:4, gg * 64:(gg + 1) * 64], op=ALU.add)
        return ins
    S.add("dv", vgbias)
    S.emit(extra_waits=[("qs", C.consts_ready[0]), ("gp", C.consts_ready[1])])
    emit_fence(C)
    sc.close()
    C.l0g = g
    return g



def _win_range(qt):
    t0 = qt * 128
    w0 = max(0, t0 - WIN)
    w1 = min(T, t0 + 128 + WIN)
    return w0, w1


def emit_l0_attn(C, xnt_sb, hb, s_hbm, dbg=None, dbg_it=(0, 5)):
    nc = C.nc
    K = C.consts
    g = C.l0g
    kt = C.l0["kt"]; vext = C.l0["vext"]; wq = C.l0["wq"]; bqt = C.l0["bqt"]
    if not hasattr(C, "stage"):
        C.stage = C.sbuf("stage", [128, 4096], F32)
    wo_sb = C.sbuf("l0_wo", [128, 8192], BF16)
    wo_row = C.sbuf("l0_worow", [1, 1024], BF16)
    sc = C.scope()
    qt_sb = sc.sbuf("a_qt", [128, 4096], BF16)
    qb_sb = sc.sbuf("a_qb", [128, 2 * 512], BF16)
    qrotb = sc.sbuf("a_qrot", [128, 2 * 512], BF16)
    qt1 = sc.sbuf("a_qt1", [128, 512], BF16)
    ext_sb = sc.sbuf("a_ext", [128, 16], BF16)
    ext_f = sc.sbuf("a_extf", [128, 4], F32)
    p_sb = sc.sbuf("a_p", [128, 2 * 656], BF16)
    pt_sb = sc.sbuf("a_pt", [128, 768], BF16)
    st2 = sc.sbuf("a_st2", [128, 4], F32)
    ot_sb = sc.sbuf("a_ot", [128, 4096], BF16)
    s_tile = sc.sbuf("a_stile", [128, 1024], F32)
    ps_s = sc.psum("a_pss", [128, 2048], F32)
    ps_pt = sc.psum("a_pspt", [128, 768], BF16)
    ps_o = sc.psum("a_pso", [128, 256], F32)
    ps_q = sc.psum("a_psq", [128, 1024], F32)

    NG = 8
    NI = 64

    cnt = {k: C.base[k] for k in ENGS}
    Tg = {}

    def res(eng, key):
        cnt[eng] += 16 if eng in ("qs", "qg", "qa") else 1
        Tg[key] = cnt[eng]

    for g4 in range(NG):
        res("pe", ("qmm", g4, 0))
        for oc in range(1, DC):
            res("pe", ("qmm", g4, oc))
            res("pe", ("qrot", g4, oc - 1))
        res("pe", ("qrot", g4, DC - 1))
        res("pe", ("smm", g4, 0))
        for i in range(1, NI):
            res("pe", ("smm", g4, i))
            res("pe", ("tr", g4, i - 1))
            res("pe", ("pv", g4, i - 1))
        res("pe", ("tr", g4, NI - 1))
        res("pe", ("pv", g4, NI - 1))
        for j in range(4):
            res("pe", ("sproj", g4, j))
        for j in range(4):
            res("dv", ("ext01", g4, j))
        for oc in range(DC):
            res("dv", ("qb", g4, oc))
            if oc >= 1:
                res("dv", ("qcomb", g4, oc - 1))
        res("dv", ("qcomb", g4, DC - 1))
        for i in range(NI):
            res("dv", ("max", g4, i))
            res("dv", ("nrm", g4, i))
        for oc in range(DC):
            res("ac", ("qre", g4, oc))
        for i in range(NI):
            res("ac", ("exp", g4, i))
            res("ac", ("pte", g4, i))
            res("ac", ("oev", g4, i))
        for j in range(4):
            res("ac", ("spe", g4, j))
        for j in range(4):
            res("qs", ("sst", g4, j))
    if dbg is not None:
        for nm in ["qt", "p", "st2"]:
            res("qs", ("dbg", nm))

    def tg(key, default=0):
        return Tg.get(key, default)

    def it_of(i):
        h, j = i // 4, i % 4
        return h, j

    Sw = Serial(C)
    for hh in range(2):
        Sw.add("sy", lambda e, hh=hh: e.dma_start(
            out=C.stage.ap().rearrange("p (c f) -> p c f", c=4),
            in_=hb["wo_ext"][0:1024, :].rearrange("(g c p) f -> g p c f", p=128, c=4)[hh, :, :, :]), dma=True)
        Sw.add("gp", lambda e, hh=hh: e.tensor_copy(out=wo_sb[:, hh * 4096:(hh + 1) * 4096],
                                                    in_=C.stage[:, :]))
    Sw.add("sy", lambda e: e.dma_start(out=C.stage[0:1, 0:1024], in_=hb["wo_ext"][1024:1025, :]), dma=True)
    Sw.add("gp", lambda e: e.tensor_copy(out=wo_row[:, :], in_=C.stage[0:1, 0:1024]))
    Sw.emit()
    emit_fence(C)

    with nc.Block() as blk:
        @blk.tensor
        def _(e):
            for g4 in range(NG):
                def emit_qmm(oc):
                    if oc >= 1:
                        e.wait_ge(C.sems["dv"], tg(("qb", g4, oc - 1)))
                    elif g4 >= 1:
                        e.wait_ge(C.sems["ac"], tg(("spe", g4 - 1, 3)))
                    for dch in range(DC):
                        ins = e.matmul(ps_q[:, 0:512],
                                       wq[:, dch * 1024 + oc * 128:dch * 1024 + (oc + 1) * 128],
                                       xnt_sb[:, dch * T + g4 * 512:dch * T + (g4 + 1) * 512],
                                       start=(dch == 0), stop=(dch == DC - 1))
                    ins.then_inc(C.sems["pe"], 1)

                def emit_qrot(oc):
                    e.wait_ge(C.sems["dv"], tg(("qb", g4, oc)))
                    if oc >= 1:
                        e.wait_ge(C.sems["ac"], tg(("qre", g4, oc - 1)))
                    e.matmul(ps_q[:, 512:1024], K["rope"][:, :],
                             qb_sb[:, (oc % 2) * 512:(oc % 2 + 1) * 512],
                             start=True, stop=True).then_inc(C.sems["pe"], 1)

                emit_qmm(0)
                for oc in range(1, DC):
                    emit_qmm(oc)
                    emit_qrot(oc - 1)
                emit_qrot(DC - 1)

                def emit_smm(i):
                    h, j = it_of(i)
                    qt = g4 * 4 + j
                    git = g4 * NI + i
                    oc, hh = head_slot(h)
                    g0 = h // 4
                    c2, gg = g0 // 2, g0 % 2
                    w0, w1 = _win_range(qt)
                    w = w1 - w0
                    nblk = w // 128
                    if git >= 2:
                        pg4, pi = (git - 2) // NI, (git - 2) % NI
                        e.wait_ge(C.sems["ac"], tg(("exp", pg4, pi)))
                    e.wait_ge(C.sems["dv"], tg(("qcomb", g4, oc)))
                    e.wait_ge(C.sems["dv"], tg(("ext01", g4, j)))
                    b = (git % 2) * 1024
                    lhs = qt_sb[hh * 64:(hh + 1) * 64, oc * 512 + j * 128:oc * 512 + (j + 1) * 128]
                    rhs0 = kt[gg * 64:(gg + 1) * 64, c2 * T + w0:c2 * T + w1]
                    n1 = min(512, w)
                    e.matmul(ps_s[:, b:b + n1], lhs, rhs0[:, 0:n1], start=True, stop=(w <= 512))
                    if w > 512:
                        e.matmul(ps_s[:, b + 512:b + w], lhs, rhs0[:, 512:w], start=True, stop=True)
                    qb_idx = qt
                    if qb_idx - 2 >= 0:
                        e.matmul(ps_s[:, b:b + 128], K["ident_b"][:, :], K["winm2a"][:, :],
                                 start=False, stop=True, skip_group_check=True)
                    if qb_idx + 2 <= NT - 1:
                        e.matmul(ps_s[:, b + w - 128:b + w], K["ident_b"][:, :], K["winp2a"][:, :],
                                 start=False, stop=True, skip_group_check=True)
                    e.matmul(ps_s[:, b + w:b + w + 4], lhs, g["kgt"][gg * 64:(gg + 1) * 64, c2 * 4:(c2 + 1) * 4],
                             start=True, stop=False)
                    e.matmul(ps_s[:, b + w:b + w + 4], K["ident_b"][:, :], ext_sb[:, j * 4:(j + 1) * 4],
                             start=False, stop=True,
                             skip_group_check=True).then_inc(C.sems["pe"], 1)

                def emit_tr(i):
                    h, j = it_of(i)
                    qt = g4 * 4 + j
                    git = g4 * NI + i
                    w0, w1 = _win_range(qt)
                    w = w1 - w0
                    nblk = w // 128
                    b2 = (git % 2) * 656
                    e.wait_ge(C.sems["dv"], tg(("nrm", g4, i)))
                    if git >= 1:
                        pg4, pi = (git - 1) // NI, (git - 1) % NI
                        e.wait_ge(C.sems["ac"], tg(("pte", pg4, pi)))
                    for c in range(nblk):
                        e.transpose(ps_pt[:, c * 128:(c + 1) * 128],
                                    p_sb[:, b2 + c * 128:b2 + (c + 1) * 128], K["ident_b"][:, :])
                    e.transpose(ps_pt[0:4, nblk * 128:(nblk + 1) * 128],
                                p_sb[:, b2 + w:b2 + w + 4],
                                K["ident_b"][:, :]).then_inc(C.sems["pe"], 1)

                def emit_pv(i):
                    h, j = it_of(i)
                    qt = g4 * 4 + j
                    git = g4 * NI + i
                    g0 = h // 4
                    _, hh = head_slot(h)
                    w0, w1 = _win_range(qt)
                    w = w1 - w0
                    nblk = w // 128
                    kb0 = w0 // 128
                    e.wait_ge(C.sems["ac"], tg(("pte", g4, i)))
                    if git >= 2:
                        pg4, pi = (git - 2) // NI, (git - 2) % NI
                        e.wait_ge(C.sems["ac"], tg(("oev", pg4, pi)))
                    ob = (git % 2) * 128
                    for c in range(nblk):
                        e.matmul(ps_o[hh * 64:(hh + 1) * 64, ob:ob + 128],
                                 vext[g0][:, (kb0 + c) * 65:(kb0 + c) * 65 + 64],
                                 pt_sb[:, c * 128:(c + 1) * 128],
                                 start=(c == 0), stop=False)
                    e.matmul(ps_o[hh * 64:(hh + 1) * 64, ob:ob + 128],
                             g["vg"][0:4, g0 * 65:g0 * 65 + 64],
                             pt_sb[0:4, nblk * 128:(nblk + 1) * 128],
                             start=False, stop=True).then_inc(C.sems["pe"], 1)

                emit_smm(0)
                for i in range(1, NI):
                    emit_smm(i)
                    emit_tr(i - 1)
                    emit_pv(i - 1)
                emit_tr(NI - 1)
                emit_pv(NI - 1)

                for j in range(4):
                    e.wait_ge(C.sems["ac"], tg(("oev", g4, 60 + j)))
                    if j == 0:
                        e.wait_ge(C.sems["dv"], tg(("qb", g4, DC - 1)))
                        e.wait_ge(C.sems["ac"], tg(("qre", g4, DC - 1)))
                    else:
                        e.wait_ge(C.sems["ac"], tg(("spe", g4, j - 1)))
                    for half in range(2):
                        for oc in range(DC):
                            e.matmul(ps_q[:, half * 512:(half + 1) * 512],
                                     ot_sb[:, (j * 8 + oc) * 128:(j * 8 + oc + 1) * 128],
                                     wo_sb[:, oc * 1024 + half * 512:oc * 1024 + (half + 1) * 512],
                                     start=(oc == 0), stop=False)
                        ins = e.matmul(ps_q[:, half * 512:(half + 1) * 512], K["ones_r"][0:1, :],
                                       wo_row[0:1, half * 512:(half + 1) * 512],
                                       start=False, stop=True)
                    ins.then_inc(C.sems["pe"], 1)

        @blk.vector
        def _(e):
            def qcomb(g4, oc):
                cs = slice(g4 * 512, (g4 + 1) * 512)
                e.wait_ge(C.sems["ac"], tg(("qre", g4, oc)))
                if g4 >= 1:
                    e.wait_ge(C.sems["pe"], tg(("smm", g4 - 1, NI - 1)))
                e.tensor_tensor(out=qt1[:, :], in0=qb_sb[:, (oc % 2) * 512:(oc % 2 + 1) * 512],
                                in1=K["cosk"][:, cs], op=ALU.mult)
                e.tensor_tensor(out=qt_sb[:, oc * 512:(oc + 1) * 512],
                                in0=qrotb[:, (oc % 2) * 512:(oc % 2 + 1) * 512],
                                in1=K["sink"][:, cs], op=ALU.mult)
                e.tensor_tensor(out=qt_sb[:, oc * 512:(oc + 1) * 512],
                                in0=qt_sb[:, oc * 512:(oc + 1) * 512],
                                in1=qt1[:, :], op=ALU.add).then_inc(C.sems["dv"], 1)

            for g4 in range(NG):
                for j in range(4):
                    qt = g4 * 4 + j
                    if g4 >= 1:
                        e.wait_ge(C.sems["pe"], tg(("smm", g4 - 1, 60 + j)))
                    i1 = e.tensor_scalar(out=ext_f[:, :], in0=g["tgb"][:, :],
                                         scalar1=K["iotaq"][:, qt:qt + 1], scalar2=None,
                                         op0=ALU.subtract)
                    C.settle(e, "dvx", i1)
                    i2 = e.tensor_tensor(out=ext_f[:, :], in0=ext_f[:, :],
                                         in1=ext_f[:, :], op=ALU.mult)
                    C.settle(e, "dvx", i2)
                    i3 = e.tensor_single_scalar(out=ext_f[:, :], in_=ext_f[:, :],
                                                scalar=(float(WIN) + 0.5) ** 2, op=ALU.is_gt)
                    C.settle(e, "dvx", i3)
                    i4 = e.tensor_tensor(out=ext_f[:, :], in0=ext_f[:, :],
                                         in1=g["nodupb"][:, :], op=ALU.mult)
                    C.settle(e, "dvx", i4)
                    e.tensor_scalar(out=ext_sb[:, j * 4:(j + 1) * 4], in0=ext_f[:, :],
                                    scalar1=1.0, scalar2=1e30, op0=ALU.subtract,
                                    op1=ALU.mult).then_inc(C.sems["dv"], 1)
                for oc in range(DC):
                    e.wait_ge(C.sems["pe"], tg(("qmm", g4, oc)))
                    if oc >= 2:
                        e.wait_ge(C.sems["pe"], tg(("qrot", g4, oc - 2)))
                    e.tensor_scalar(out=qb_sb[:, (oc % 2) * 512:(oc % 2 + 1) * 512],
                                    in0=ps_q[:, 0:512], scalar1=bqt[:, oc:oc + 1],
                                    scalar2=None, op0=ALU.add).then_inc(C.sems["dv"], 1)
                    if oc >= 1:
                        qcomb(g4, oc - 1)
                qcomb(g4, DC - 1)
                for i in range(NI):
                    h, j = it_of(i)
                    qt = g4 * 4 + j
                    git = g4 * NI + i
                    w0, w1 = _win_range(qt)
                    w = w1 - w0
                    b = (git % 2) * 1024
                    b2 = (git % 2) * 656
                    e.wait_ge(C.sems["pe"], tg(("smm", g4, i)))
                    i1 = e.reduce_max(out=st2[:, 0:1], in_=ps_s[:, b:b + w + 4],
                                      axis=AX.X, negate=True)
                    C.settle(e, "dvx", i1)
                    e.tensor_scalar(out=st2[:, 1:2], in0=st2[:, 0:1], scalar1=L0_SCALE,
                                    scalar2=None, op0=ALU.mult).then_inc(C.sems["dv"], 1)
                    e.wait_ge(C.sems["ac"], tg(("exp", g4, i)))
                    i2 = e.reciprocal(out=st2[:, 3:4], in_=st2[:, 2:3])
                    C.settle(e, "dvx", i2)
                    e.tensor_scalar(out=p_sb[:, b2:b2 + w + 4], in0=p_sb[:, b2:b2 + w + 4],
                                    scalar1=st2[:, 3:4], scalar2=None,
                                    op0=ALU.mult).then_inc(C.sems["dv"], 1)

        @blk.scalar
        def _(e):
            for g4 in range(NG):
                for oc in range(DC):
                    e.wait_ge(C.sems["pe"], tg(("qrot", g4, oc)))
                    if oc >= 2:
                        e.wait_ge(C.sems["dv"], tg(("qcomb", g4, oc - 2)))
                    e.activation(out=qrotb[:, (oc % 2) * 512:(oc % 2 + 1) * 512],
                                 in_=ps_q[:, 512:1024], func=AF.Copy).then_inc(C.sems["ac"], 1)
                for i in range(NI):
                    h, j = it_of(i)
                    qt = g4 * 4 + j
                    git = g4 * NI + i
                    w0, w1 = _win_range(qt)
                    w = w1 - w0
                    nblk = w // 128
                    b = (git % 2) * 1024
                    b2 = (git % 2) * 656
                    oc, hh = head_slot(h)
                    e.wait_ge(C.sems["dv"], tg(("max", g4, i)))
                    if git >= 2:
                        pg4, pi = (git - 2) // NI, (git - 2) % NI
                        e.wait_ge(C.sems["pe"], tg(("tr", pg4, pi)))
                    e.activation(out=p_sb[:, b2:b2 + w + 4], in_=ps_s[:, b:b + w + 4],
                                 func=AF.Exp, scale=L0_SCALE, bias=st2[:, 1:2],
                                 accum_out=st2[:, 2:3]).then_inc(C.sems["ac"], 1)
                    e.wait_ge(C.sems["pe"], tg(("tr", g4, i)))
                    if git >= 1:
                        pg4, pi = (git - 1) // NI, (git - 1) % NI
                        e.wait_ge(C.sems["pe"], tg(("pv", pg4, pi)))
                    e.activation(out=pt_sb[:, 0:(nblk + 1) * 128], in_=ps_pt[:, 0:(nblk + 1) * 128],
                                 func=AF.Copy).then_inc(C.sems["ac"], 1)
                    e.wait_ge(C.sems["pe"], tg(("pv", g4, i)))
                    if g4 >= 1:
                        e.wait_ge(C.sems["pe"], tg(("sproj", g4 - 1, 3)))
                    e.activation(out=ot_sb[hh * 64:(hh + 1) * 64, (j * 8 + oc) * 128:(j * 8 + oc + 1) * 128],
                                 in_=ps_o[hh * 64:(hh + 1) * 64, (git % 2) * 128:(git % 2) * 128 + 128],
                                 func=AF.Copy).then_inc(C.sems["ac"], 1)
                for j in range(4):
                    e.wait_ge(C.sems["pe"], tg(("sproj", g4, j)))
                    if g4 >= 1 or j >= 1:
                        pj = (g4, j - 1) if j >= 1 else (g4 - 1, 3)
                        e.wait_ge(C.sems["qs"], tg(("sst",) + pj))

                    e.activation(out=s_tile[:, :],
                                 in_=ps_q[:, 0:1024], func=AF.Copy).then_inc(C.sems["ac"], 1)

        @blk.sync
        def _(e):
            if dbg is not None:
                dg4, di = dbg_it
                git = dg4 * NI + di
                e.wait_ge(C.sems["dv"], tg(("qcomb", 0, DC - 1)))
                e.dma_start(out=dbg["qt"], in_=qt_sb[:, :]).then_inc(C.sems["qs"], 16)
                e.wait_ge(C.sems["dv"], tg(("nrm", dg4, di)))
                e.dma_start(out=dbg["p"], in_=p_sb[:, (git % 2) * 656:(git % 2 + 1) * 656]).then_inc(C.sems["qs"], 16)
                e.dma_start(out=dbg["st2"], in_=st2[:, :]).then_inc(C.sems["qs"], 16)
            for g4 in range(NG):
                for j in range(4):
                    qt = g4 * 4 + j
                    e.wait_ge(C.sems["ac"], tg(("spe", g4, j)))
                    e.dma_start(out=s_hbm[qt * 128:(qt + 1) * 128, :],
                                in_=s_tile[:, :]).then_inc(C.sems["qs"], 16)

    C.base.update({k: cnt[k] for k in cnt})
    emit_fence(C)
    sc.close()
    return dict(wo=wo_sb, wo_row=wo_row)




def emit_l0_fixup(C, hb, sg_hbm, wo, wo_row):
    nc = C.nc
    K = C.consts
    g = C.l0g
    kt = C.l0["kt"]; vext = C.l0["vext"]; wq = C.l0["wq"]; bqt = C.l0["bqt"]
    sc = C.scope()
    qgt = sc.sbuf("f_qgt", [128, 32], BF16)
    qgb = sc.sbuf("f_qgb", [128, 32], BF16)
    cosgt = sc.sbuf("f_cosgt", [128, 64], BF16)
    sgf = sc.sbuf("f_sg", [4, T], F32)
    pgb = sc.sbuf("f_pg", [4, T], BF16)
    st4 = sc.sbuf("f_st4", [4, 4], F32)
    ptg = sc.sbuf("f_ptg", [128, 128], BF16)
    otg = sc.sbuf("f_otg", [128, 32], BF16)
    sgo = sc.sbuf("f_sgo", [4, 1024], F32)
    ps1 = sc.psum("f_ps1", [4, 2048], F32)
    ps2 = sc.psum("f_ps2", [128, 512], F32)
    psb = sc.psum("f_psb", [128, 128], BF16)
    S = Serial(C)

    g2 = {}
    g2["ctok"] = sc.sbuf("f_ctok", [4, 256], BF16)
    S.add("gp", lambda e: e.indirect_dma_start(
        out=g2["ctok"][:, 0:128], out_offset=None, in_=hb["cos_tok"][:, :],
        in_offset=IndirectOffsetOnAxis(ap=g["tgi"][:, :], axis=0)), dma=True, sw=True)
    S.add("gp", lambda e: e.indirect_dma_start(
        out=g2["ctok"][:, 128:256], out_offset=None, in_=hb["sin_tok"][:, :],
        in_offset=IndirectOffsetOnAxis(ap=g["tgi"][:, :], axis=0)), dma=True, sw=True)
    S.add("pe", lambda e: e.transpose(psb[:, 0:4], g2["ctok"][0:4, 0:128], K["ident_b"][0:4, 0:4]))
    S.add("pe", lambda e: e.transpose(psb[:, 4:8], g2["ctok"][0:4, 128:256], K["ident_b"][0:4, 0:4]))
    def ctile2(e):
        for oc in range(DC):
            e.activation(out=cosgt[:, oc * 4:(oc + 1) * 4], in_=psb[:, 0:4], func=AF.Copy)
            ins = e.activation(out=cosgt[:, 32 + oc * 4:32 + (oc + 1) * 4], in_=psb[:, 4:8], func=AF.Copy)
        return ins
    S.add("ac", ctile2)

    def qmm(e):
        for oc in range(DC):
            for dch in range(DC):
                ins = e.matmul(ps2[:, oc * 4:(oc + 1) * 4],
                               wq[:, dch * 1024 + oc * 128:dch * 1024 + (oc + 1) * 128],
                               g["xngt"][:, dch * 4:(dch + 1) * 4],
                               start=(dch == 0), stop=(dch == DC - 1))
        return ins
    S.add("pe", qmm)
    def qbias(e):
        for oc in range(DC):
            ins = e.tensor_scalar(out=qgb[:, oc * 4:(oc + 1) * 4], in0=ps2[:, oc * 4:(oc + 1) * 4],
                                  scalar1=bqt[:, oc:oc + 1], scalar2=None, op0=ALU.add)
        return ins
    S.add("dv", qbias)
    S.add("pe", lambda e: e.matmul(ps2[:, 64:96], K["rope"][:, :], qgb[:, :], start=True, stop=True))
    S.add("dv", lambda e: e.tensor_copy(out=qgt[:, :], in_=ps2[:, 64:96]))
    S.add("dv", lambda e: e.tensor_tensor(out=qgt[:, :], in0=qgt[:, :], in1=cosgt[:, 32:64], op=ALU.mult))
    S.add("dv", lambda e: e.tensor_tensor(out=qgb[:, :], in0=qgb[:, :], in1=cosgt[:, 0:32], op=ALU.mult))
    S.add("dv", lambda e: e.tensor_tensor(out=qgt[:, :], in0=qgt[:, :], in1=qgb[:, :], op=ALU.add))

    for h in range(H0):
        oc, hh = head_slot(h)
        g0 = h // 4
        c2, gg = g0 // 2, g0 % 2
        for half in range(2):
            def smm(e, oc=oc, hh=hh, c2=c2, gg=gg, half=half):
                for sl in range(4):
                    ins = e.matmul(ps1[0:4, sl * 512:(sl + 1) * 512],
                                   qgt[hh * 64:(hh + 1) * 64, oc * 4:(oc + 1) * 4],
                                   kt[gg * 64:(gg + 1) * 64,
                                      c2 * T + half * 2048 + sl * 512:c2 * T + half * 2048 + (sl + 1) * 512],
                                   start=True, stop=True)
                return ins
            S.add("pe", smm)
            S.add("dv", lambda e, half=half: e.tensor_copy(out=sgf[:, half * 2048:(half + 1) * 2048],
                                                           in_=ps1[0:4, 0:2048]))
        S.add("dv", lambda e: e.reduce_max(out=st4[:, 0:1], in_=sgf[:, :], axis=AX.X, negate=True))
        S.add("dv", lambda e: e.tensor_scalar(out=st4[:, 1:2], in0=st4[:, 0:1], scalar1=L0_SCALE,
                                              scalar2=None, op0=ALU.mult))
        S.add("ac", lambda e: e.activation(out=pgb[:, :], in_=sgf[:, :], func=AF.Exp,
                                           scale=L0_SCALE, bias=st4[0:4, 1:2],
                                           accum_out=st4[0:4, 2:3]))
        S.add("dv", lambda e: e.reciprocal(out=st4[:, 3:4], in_=st4[0:4, 2:3]))
        S.add("dv", lambda e: e.tensor_scalar(out=pgb[:, :], in0=pgb[:, :],
                                              scalar1=st4[0:4, 3:4], scalar2=None, op0=ALU.mult))
        def ptr(e):
            for c in range(NT):
                ins = e.transpose(psb[:, c * 4:(c + 1) * 4], pgb[0:4, c * 128:(c + 1) * 128],
                                  K["ident_b"][0:4, 0:4])
            return ins
        S.add("pe", ptr)
        S.add("ac", lambda e: e.activation(out=ptg[:, :], in_=psb[:, :], func=AF.Copy))
        def pv(e, g0=g0, hh=hh):
            for c in range(NT):
                ins = e.matmul(ps2[hh * 64:(hh + 1) * 64, 0:4],
                               vext[g0][:, c * 65:c * 65 + 64],
                               ptg[:, c * 4:(c + 1) * 4],
                               start=(c == 0), stop=(c == NT - 1))
            return ins
        S.add("pe", pv)
        S.add("ac", lambda e, oc=oc, hh=hh: e.activation(
            out=otg[hh * 64:(hh + 1) * 64, oc * 4:(oc + 1) * 4],
            in_=ps2[hh * 64:(hh + 1) * 64, 0:4], func=AF.Copy))

    def sproj(e):
        for half in range(2):
            for oc in range(DC):
                e.matmul(ps1[0:4, half * 512:(half + 1) * 512], otg[:, oc * 4:(oc + 1) * 4],
                         wo[:, oc * 1024 + half * 512:oc * 1024 + (half + 1) * 512],
                         start=(oc == 0), stop=False)
            ins = e.matmul(ps1[0:4, half * 512:(half + 1) * 512], K["ones_r"][0:1, 0:4],
                           wo_row[0:1, half * 512:(half + 1) * 512], start=False, stop=True)
        return ins
    S.add("pe", sproj)
    S.add("ac", lambda e: e.activation(out=sgo[:, :], in_=ps1[0:4, 0:1024], func=AF.Copy))
    S.add("sy", lambda e: e.dma_start(out=sg_hbm[:, :], in_=sgo[:, :]), dma=True)
    S.emit()
    emit_fence(C)
    sc.close()



def emit_l0_ternarize(C, hb, wfc_t_hbm, wproj_t_hbm):
    nc = C.nc
    K = C.consts
    C.lnbrow = C.sbuf("lnbrow", [1, 1024], BF16)
    sc = C.scope()
    wbuf = sc.sbuf("t_w", [128, 2 * 4096], F32)
    habs = sc.sbuf("t_abs", [128, 4096], F32)
    sgn = sc.sbuf("t_sgn", [128, 4096], BF16)
    mag = sc.sbuf("t_mag", [128, 4096], F32)
    tern = sc.sbuf("t_tern", [128, 2 * 4096], BF16)
    acc = sc.sbuf("t_acc", [128, 4], F32)
    thr = sc.sbuf("t_thr", [128, 4], F32)
    onesf = sc.sbuf("t_onesf", [128, 1], F32)
    lnw = sc.sbuf("t_lnw", [128, 32], F32)
    lnbf = sc.sbuf("t_lnbf", [128, 32], F32)
    lnbb = sc.sbuf("t_lnbb", [128, 32], BF16)
    ps = sc.psum("t_ps", [128, 512], F32)
    ps_lb = sc.psum("t_pslb", [1, 1024], F32)
    tiles = [("fc", r) for r in range(8)] + [("pj", r) for r in range(8)]
    NTL = len(tiles)

    def src(m, r):
        if m == "fc":
            return hb["wfcT"].rearrange("(n p) f -> n p f", p=128)[r, :, :]
        return hb["wprojT"].rearrange("(a n p) f -> a p n f", p=128, n=4)[r, :, :, :]

    def dst(m, r):
        if m == "fc":
            return wfc_t_hbm.rearrange("(n p) f -> n p f", p=128)[r, :, :]
        return wproj_t_hbm.rearrange("(a n p) f -> a p n f", p=128, n=4)[r, :, :, :]

    def buf3(t, ti):
        return t.ap().rearrange("p (b n f) -> p b n f", b=2, n=4)[:, ti % 2, :, :]

    S1 = Sched(C, NTL, {"qs": ["w"], "dv": ["acc"]}, pre={"qs": 2})
    with nc.Block() as blk:
        @blk.sync
        def _(e):
            e.dma_start(out=lnw[:, :], in_=hb["lnwT"][:, :]).then_inc(C.sems["qs"], 16)
            e.dma_start(out=lnbf[:, :], in_=hb["lnbT"][:, :]).then_inc(C.sems["qs"], 16)
            for ti, (m, r) in enumerate(tiles):
                if ti >= 2:
                    e.wait_ge(C.sems["dv"], S1.t("dv", ti - 2, "acc"))
                e.dma_start(out=wbuf[:, (ti % 2) * 4096:(ti % 2 + 1) * 4096],
                            in_=src(m, r)).then_inc(C.sems["qs"], 16)

        @blk.gpsimd
        def _(e):
            e.memset(acc[:, :], 0.0)
            e.memset(onesf[:, :], 1.0)
            e.wait_ge(C.sems["qs"], S1.pre_t("qs"))
            e.tensor_copy(out=lnbb[:, :], in_=lnbf[:, :]).then_inc(C.sems["gp"], 1)

        @blk.vector
        def _(e):
            e.wait_ge(C.sems["gp"], C.base["gp"] + 1)
            for ti, (m, r) in enumerate(tiles):
                col = 0 if m == "fc" else 1
                e.wait_ge(C.sems["qs"], S1.t("qs", min(ti + 1, NTL - 1), "w"))
                if ti >= 1:
                    e.wait_ge(C.sems["dv"], S1.t("dv", ti - 1, "acc"))
                i1 = e.reduce_sum(out=acc[:, 2:3], in_=wbuf[:, (ti % 2) * 4096:(ti % 2 + 1) * 4096],
                                  axis=AX.X, apply_absolute_value=True)
                C.settle(e, "dvx", i1)
                e.tensor_add(out=acc[:, col:col + 1], in0=acc[:, col:col + 1],
                             in1=acc[:, 2:3]).then_inc(C.sems["dv"], 1)
    S1.done()
    C.base["gp"] += 1

    Sf = Serial(C)
    Sf.add("pe", lambda e: e.matmul(ps[0:1, 0:2], onesf[:, 0:1], acc[:, 0:2], start=True, stop=True))
    Sf.add("dv", lambda e: e.tensor_scalar(out=acc[0:1, 0:2], in0=ps[0:1, 0:2],
                                           scalar1=1.0 / (4096 * 1024), scalar2=None, op0=ALU.mult))
    Sf.add("dv", lambda e: e.tensor_scalar(out=acc[0:1, 0:2], in0=acc[0:1, 0:2], scalar1=1e-6,
                                           scalar2=-0.5, op0=ALU.add, op1=ALU.mult))
    Sf.add("pe", lambda e: e.matmul(ps[:, 4:6], K["ones_rf"][0:1, :], acc[0:1, 0:2],
                                    start=True, stop=True))
    Sf.add("dv", lambda e: e.tensor_copy(out=thr[:, 0:2], in_=ps[:, 4:6]))
    Sf.add("dv", lambda e: e.tensor_tensor(out=thr[:, 2:4], in0=thr[:, 0:2], in1=thr[:, 0:2],
                                           op=ALU.mult))
    Sf.add("dv", lambda e: e.tensor_scalar(out=thr[:, 2:4], in0=thr[:, 2:4], scalar1=-1.0,
                                           scalar2=None, op0=ALU.mult))
    Sf.emit()
    sr_done = C.base["sr"]
    qs_after_s1 = C.base["qs"]

    S2 = Sched(C, NTL, {"qs": ["w", "out"], "dv": ["tern", "scaled"], "ac": ["sgn", "cmp"], "pe": ["lbmm"]})
    with nc.Block() as blk:
        @blk.sync
        def _(e):
            e.wait_ge(C.sems["sr"], sr_done)
            for ti, (m, r) in enumerate(tiles):
                if ti >= 2:
                    e.wait_ge(C.sems["dv"], S2.t("dv", ti - 2, "scaled"))
                e.dma_start(out=(buf3(wbuf, ti) if m == "pj" else wbuf[:, (ti % 2) * 4096:(ti % 2 + 1) * 4096]),
                            in_=src(m, r)).then_inc(C.sems["qs"], 16)
                e.wait_ge(C.sems["dv"], S2.t("dv", ti, "scaled"))
                e.dma_start(out=dst(m, r),
                            in_=(buf3(tern, ti) if m == "pj" else tern[:, (ti % 2) * 4096:(ti % 2 + 1) * 4096])
                            ).then_inc(C.sems["qs"], 16)

        @blk.scalar
        def _(e):
            e.wait_ge(C.sems["sr"], sr_done)
            for ti, (m, r) in enumerate(tiles):
                col = 0 if m == "fc" else 1
                wb = wbuf[:, (ti % 2) * 4096:(ti % 2 + 1) * 4096]
                e.wait_ge(C.sems["qs"], S2.t("qs", ti, "w"))
                if ti >= 1:
                    e.wait_ge(C.sems["dv"], S2.t("dv", ti - 1, "scaled"))
                i1 = e.activation(out=habs[:, :], in_=wb, func=AF.Square)
                e.activation(out=sgn[:, :], in_=wb, func=AF.Sign).then_inc(C.sems["ac"], 1)
                C.settle(e, "acx", i1)
                e.activation(out=mag[:, :], in_=habs[:, :], func=AF.Sign,
                             bias=thr[:, 2 + col:3 + col]).then_inc(C.sems["ac"], 1)

        @blk.tensor
        def _(e):
            for ti, (m, r) in enumerate(tiles):
                if m != "pj":
                    e.matmul(ps[0:1, 100:101], onesf[0:1, 0:1], onesf[0:1, 0:1],
                             start=True, stop=True).then_inc(C.sems["pe"], 1)
                    continue
                e.wait_ge(C.sems["dv"], S2.t("dv", ti, "tern"))
                tb = tern[:, (ti % 2) * 4096:(ti % 2 + 1) * 4096]
                first = (ti == 8)
                for n in range(4):
                    for half in range(2):
                        ins = e.matmul(ps_lb[0:1, half * 512:(half + 1) * 512],
                                       lnbb[:, (4 * r + n):(4 * r + n) + 1],
                                       tb[:, n * 1024 + half * 512:n * 1024 + (half + 1) * 512],
                                       start=(first and n == 0), stop=(ti == NTL - 1 and n == 3),
                                       skip_group_check=True)
                ins.then_inc(C.sems["pe"], 1)

        @blk.vector
        def _(e):
            e.wait_ge(C.sems["sr"], sr_done)
            e.wait_ge(C.sems["qs"], qs_after_s1)
            for ti, (m, r) in enumerate(tiles):
                e.wait_ge(C.sems["ac"], S2.t("ac", ti, "cmp"))
                if ti >= 2:
                    e.wait_ge(C.sems["qs"], S2.t("qs", ti - 1, "w"))
                i0 = e.tensor_scalar(out=mag[:, :], in0=mag[:, :], scalar1=1.0, scalar2=0.5,
                                     op0=ALU.add, op1=ALU.mult)
                C.settle(e, "dvx", i0)
                tb = tern[:, (ti % 2) * 4096:(ti % 2 + 1) * 4096]
                ins = e.tensor_tensor(out=tb, in0=sgn[:, :], in1=mag[:, :], op=ALU.mult)
                ins.then_inc(C.sems["dv"], 1)
                if m == "pj":
                    e.wait_ge(C.sems["pe"], S2.t("pe", ti, "lbmm"))
                    for n in range(4):
                        ins = e.tensor_scalar(out=tb[:, n * 1024:(n + 1) * 1024],
                                              in0=tb[:, n * 1024:(n + 1) * 1024],
                                              scalar1=lnw[:, 4 * r + n:4 * r + n + 1],
                                              scalar2=None, op0=ALU.mult)
                else:
                    ins = e.tensor_copy(out=acc[:, 3:4], in_=acc[:, 3:4])
                ins.then_inc(C.sems["dv"], 1)
    S2.done()
    Se = Serial(C)
    Se.add("dv", lambda e: e.tensor_copy(out=C.lnbrow[:, :], in_=ps_lb[0:1, 0:1024]))
    Se.emit(extra_waits=[("pe", C.base["pe"])])
    emit_fence(C)
    sc.close()



def emit_l0_ffn(C, hb, wfc_t_hbm, wproj_t_hbm, xnt_hbm, x_hbm, s_hbm, sg_hbm, x1_hbm):
    import os
    NTF = int(os.environ.get("NTF", str(NT)))
    nc = C.nc
    K = C.consts
    sc = C.scope()
    wfc = sc.sbuf("ff_wfc", [128, 8 * 4096], BF16)
    wpj = sc.sbuf("ff_wpj", [128, 32 * 1024], BF16)
    lnbrow = C.lnbrow
    xg = sc.sbuf("ff_xg", [128, 4096], BF16)
    hg = sc.sbuf("ff_hg", [128, 4096], F32)
    hn = sc.sbuf("ff_hn", [128, 4096], BF16)
    hnt = sc.sbuf("ff_hnt", [128, 4096], BF16)
    gs = sc.sbuf("ff_gs", [128, 1024], F32)
    xt = sc.sbuf("ff_x", [128, 1024], F32)
    st = sc.sbuf("ff_s", [128, 1024], F32)
    bn6 = sc.sbuf("ff_bn6", [128, 56], F32)
    selt = sc.sbuf("ff_sel", [128, 4], F32)
    ggsb = sc.sbuf("ff_gg", [4, 1024], F32)
    sgt = sc.sbuf("ff_sgt", [4, 1024], F32)
    xg4 = sc.sbuf("ff_xg4", [4, 1024], F32)
    tgi = sc.sbuf("ff_tgi", [4, 1], I32)
    tgb = sc.sbuf("ff_tgb", [128, 4], F32)
    ps_h = sc.psum("ff_psh", [128, 1024], F32)
    ps_pt = sc.psum("ff_pspt", [128, 4096], BF16)
    ps_g = sc.psum("ff_psg", [128, 1024], F32)

    Sp = Serial(C)
    for r in range(8):
        Sp.add("sy", lambda e, r=r: e.dma_start(
            out=wfc[:, r * 4096:(r + 1) * 4096],
            in_=wfc_t_hbm[r * 128:(r + 1) * 128, :]), dma=True)
    for r in range(32):
        Sp.add("sy", lambda e, r=r: e.dma_start(
            out=wpj[:, r * 1024:(r + 1) * 1024],
            in_=wproj_t_hbm[r * 128:(r + 1) * 128, :]), dma=True)
    Sp.add("sy", lambda e: e.dma_start(out=tgi[:, :], in_=hb["tg"][:, :]), dma=True)
    Sp.add("sy", lambda e: e.dma_start(out=sgt[:, :], in_=sg_hbm[:, :]), dma=True)
    Sp.add("gp", lambda e: e.indirect_dma_start(
        out=xg4[:, :], out_offset=None, in_=x_hbm[:, :],
        in_offset=IndirectOffsetOnAxis(ap=tgi[:, :], axis=0)), dma=True, sw=True)
    Sp.add("gp", lambda e: e.memset(ggsb[:, :], 0.0))
    Sp.add("dv", lambda e: e.tensor_copy(out=tgb[0:4, 0:1], in_=tgi[:, :]))
    Sp.add("pe", lambda e: e.transpose(ps_h[0:1, 0:4], tgb[0:4, 0:1], K["ident_f"][0:4, 0:4]))
    Sp.add("dv", lambda e: e.tensor_copy(out=tgb[0:1, 0:4], in_=ps_h[0:1, 0:4]))
    Sp.add("pe", lambda e: e.matmul(ps_h[:, 4:8], K["ones_rf"][0:1, :], tgb[0:1, 0:4],
                                    start=True, stop=True))
    Sp.add("dv", lambda e: e.tensor_copy(out=tgb[:, :], in_=ps_h[:, 4:8]))
    Sp.emit()
    emit_fence(C)

    cnt = {k: C.base[k] for k in ENGS}
    Tg = {}

    def res(eng, key):
        cnt[eng] += 16 if eng in ("qs", "qg", "qa") else 1
        Tg[key] = cnt[eng]

    for i in range(NTF):
        if i % 4 == 0:
            res("qa", ("xg", i // 4))
        res("qs", ("x", i))
        res("qs", ("s", i))
        for sl in range(8):
            res("pe", ("f", i, sl))
        for sl in range(8):
            res("ac", ("g", i, sl))
        res("dv", ("sel", i))
        res("dv", ("stats", i))
        res("ac", ("sqrt", i))
        res("dv", ("hn", i))
        res("pe", ("tr", i))
        res("dv", ("pte", i))
        res("pe", ("pj", i))
        res("ac", ("silu", i))
        res("pe", ("selmm", i))
        res("dv", ("comb", i))
        res("dv", ("gg", i))
        res("qs", ("x1", i))

    def tg2(key, default=0):
        return Tg.get(key, default)

    with nc.Block() as blk:
        @blk.sync
        def _(e):
            for i in range(NTF):
                e.wait_ge(C.sems["dv"], tg2(("comb", i - 1), 0))
                e.dma_start(out=xt[:, :], in_=x_hbm[i * 128:(i + 1) * 128, :]).then_inc(C.sems["qs"], 16)
                e.dma_start(out=st[:, :], in_=s_hbm[i * 128:(i + 1) * 128, :]).then_inc(C.sems["qs"], 16)
                e.wait_ge(C.sems["dv"], tg2(("comb", i)))
                e.dma_start(out=x1_hbm[i * 128:(i + 1) * 128, :], in_=gs[:, :]).then_inc(C.sems["qs"], 16)

        @blk.tensor
        def _(e):
            for i in range(NTF):
                gi = i // 4
                e.wait_ge(C.sems["qa"], tg2(("xg", gi)))
                xgb = xg[:, :]
                for sl in range(8):
                    if sl >= 2:
                        e.wait_ge(C.sems["ac"], tg2(("g", i, sl - 2)))
                    elif i >= 1:
                        e.wait_ge(C.sems["ac"], tg2(("g", i - 1, 6 + sl)))
                    for dch in range(DC):
                        ins = e.matmul(ps_h[:, (sl % 2) * 512:(sl % 2 + 1) * 512],
                                       xgb[:, dch * 512 + (i % 4) * 128:dch * 512 + (i % 4) * 128 + 128],
                                       wfc[:, dch * 4096 + sl * 512:dch * 4096 + (sl + 1) * 512],
                                       start=(dch == 0), stop=(dch == DC - 1))
                    ins.then_inc(C.sems["pe"], 1)
                e.wait_ge(C.sems["dv"], tg2(("hn", i)))
                if i >= 1:
                    e.wait_ge(C.sems["dv"], tg2(("pte", i - 1)))
                for c in range(32):
                    ins = e.transpose(ps_pt[:, c * 128:(c + 1) * 128],
                                      hn[:, c * 128:(c + 1) * 128], K["ident_b"][:, :])
                ins.then_inc(C.sems["pe"], 1)
                e.wait_ge(C.sems["dv"], tg2(("pte", i)))
                if i >= 1:
                    e.wait_ge(C.sems["ac"], tg2(("silu", i - 1)))
                    e.wait_ge(C.sems["dv"], tg2(("gg", i - 1)))
                for half in range(2):
                    for c in range(32):
                        e.matmul(ps_g[:, half * 512:(half + 1) * 512],
                                 hnt[:, c * 128:(c + 1) * 128],
                                 wpj[:, c * 1024 + half * 512:c * 1024 + (half + 1) * 512],
                                 start=(c == 0), stop=False)
                    ins = e.matmul(ps_g[:, half * 512:(half + 1) * 512], K["ones_r"][0:1, :],
                                   lnbrow[0:1, half * 512:(half + 1) * 512],
                                   start=False, stop=True)
                ins.then_inc(C.sems["pe"], 1)
                e.wait_ge(C.sems["ac"], tg2(("silu", i)))
                e.wait_ge(C.sems["dv"], tg2(("sel", i)))
                e.matmul(ps_g[0:4, 0:512], selt[:, :], gs[:, 0:512], start=True, stop=True)
                ins = e.matmul(ps_g[0:4, 512:1024], selt[:, :], gs[:, 512:1024], start=True, stop=True)
                ins.then_inc(C.sems["pe"], 1)

        @blk.scalar
        def _(e):
            for i in range(NTF):
                if i % 4 == 0:
                    gi = i // 4
                    if gi >= 1:
                        e.wait_ge(C.sems["pe"], tg2(("f", gi * 4 - 1, 7)))
                    e.dma_start(out=xg.ap().rearrange("p (c t) -> p c t", c=DC),
                                in_=xnt_hbm.rearrange("(c p) t -> p c t", p=128)[:, :, gi * 512:(gi + 1) * 512]
                                ).then_inc(C.sems["qa"], 16)
                for sl in range(8):
                    e.wait_ge(C.sems["pe"], tg2(("f", i, sl)))
                    if i >= 1:
                        e.wait_ge(C.sems["dv"], tg2(("hn", i - 1)))
                    e.activation(out=hg[:, sl * 512:(sl + 1) * 512],
                                 in_=ps_h[:, (sl % 2) * 512:(sl % 2 + 1) * 512],
                                 func=AF.Gelu).then_inc(C.sems["ac"], 1)
                e.wait_ge(C.sems["dv"], tg2(("stats", i)))
                e.activation(out=bn6[:, 51:52], in_=bn6[:, 49:50], func=AF.Sqrt,
                             bias=K["cvals"][:, 0:1]).then_inc(C.sems["ac"], 1)
                e.wait_ge(C.sems["pe"], tg2(("pj", i)))
                if i >= 1:
                    e.wait_ge(C.sems["qs"], tg2(("x1", i - 1)))
                e.activation(out=gs[:, :], in_=ps_g[:, 0:1024], func=AF.Silu).then_inc(C.sems["ac"], 1)

        @blk.vector
        def _(e):
            for i in range(NTF):
                i1 = e.tensor_scalar(out=selt[:, :], in0=tgb[:, :], scalar1=K["iotaq"][:, i:i + 1],
                                     scalar2=None, op0=ALU.subtract)
                C.settle(e, "dvx", i1)
                i2 = e.tensor_tensor(out=selt[:, :], in0=selt[:, :], in1=selt[:, :], op=ALU.mult)
                C.settle(e, "dvx", i2)
                e.tensor_single_scalar(out=selt[:, :], in_=selt[:, :], scalar=0.5,
                                       op=ALU.is_lt).then_inc(C.sems["dv"], 1)
                e.wait_ge(C.sems["ac"], tg2(("g", i, 7)))
                for sl in range(8):
                    i3 = e.bn_stats(out=bn6[:, sl * 6:(sl + 1) * 6], in_=hg[:, sl * 512:(sl + 1) * 512])
                C.settle(e, "dvx", i3)
                i4 = e.bn_aggr(out=bn6[:, 48:50], in_=bn6[:, 0:48])
                C.settle(e, "dvx", i4)
                e.tensor_scalar(out=bn6[:, 50:51], in0=bn6[:, 48:49], scalar1=-1.0,
                                scalar2=None, op0=ALU.mult).then_inc(C.sems["dv"], 1)
                e.wait_ge(C.sems["ac"], tg2(("sqrt", i)))
                i5 = e.reciprocal(out=bn6[:, 52:53], in_=bn6[:, 51:52])
                C.settle(e, "dvx", i5)
                if i >= 1:
                    e.wait_ge(C.sems["pe"], tg2(("tr", i - 1)))
                e.tensor_scalar(out=hn[:, :], in0=hg[:, :], scalar1=bn6[:, 50:51],
                                scalar2=bn6[:, 52:53], op0=ALU.add,
                                op1=ALU.mult).then_inc(C.sems["dv"], 1)
                e.wait_ge(C.sems["pe"], tg2(("tr", i)))
                if i >= 1:
                    e.wait_ge(C.sems["pe"], tg2(("pj", i - 1)))
                e.tensor_copy(out=hnt[:, :], in_=ps_pt[:, :]).then_inc(C.sems["dv"], 1)
                e.wait_ge(C.sems["ac"], tg2(("silu", i)))
                e.wait_ge(C.sems["qs"], tg2(("s", i)))
                e.wait_ge(C.sems["pe"], tg2(("selmm", i)))
                i9 = e.tensor_tensor(out=gs[:, :], in0=st[:, :], in1=gs[:, :], op=ALU.mult)
                C.settle(e, "dvx", i9)
                e.tensor_tensor(out=gs[:, :], in0=gs[:, :], in1=xt[:, :],
                                op=ALU.add).then_inc(C.sems["dv"], 1)
                e.wait_ge(C.sems["pe"], tg2(("selmm", i)))
                e.tensor_tensor(out=ggsb[:, :], in0=ggsb[:, :], in1=ps_g[0:4, 0:1024],
                                op=ALU.add).then_inc(C.sems["dv"], 1)

    C.base.update({k: cnt[k] for k in cnt})

    Sa = Serial(C)
    Sa.add("dv", lambda e: e.tensor_tensor(out=sgt[:, :], in0=sgt[:, :], in1=ggsb[:, :], op=ALU.mult))
    Sa.add("dv", lambda e: e.tensor_tensor(out=sgt[:, :], in0=sgt[:, :], in1=xg4[:, :], op=ALU.add))
    Sa.add("gp", lambda e: e.indirect_dma_start(
        out=x1_hbm[:, :], out_offset=IndirectOffsetOnAxis(ap=tgi[:, :], axis=0),
        in_=sgt[:, :], in_offset=None), dma=True, sw=True)
    Sa.emit(extra_waits=[("qs", C.base["qs"])])
    emit_fence(C)
    sc.close()



def emit_l1_kv(C, xnt_sb, hb):
    nc = C.nc
    K = C.consts
    if not hasattr(C, "stage"):
        C.stage = C.sbuf("stage", [128, 4096], F32)
    kvt = C.sbuf("l1_kvt", [65, T], BF16)
    ckvx = C.sbuf("l1_ckvx", [128, NT * 33], BF16)
    wdkr = C.sbuf("l1_wdkr", [128, 8 * 64], BF16)
    wdq = C.sbuf("l1_wdq", [128, 8 * 32], BF16)
    wuq = C.sbuf("l1_wuq", [32, 1024], BF16)
    wqr = C.sbuf("l1_wqr", [32, 256], BF16)
    wuk = C.sbuf("l1_wuk", [128, 8 * 32], BF16)
    bukt = C.sbuf("l1_bukt", [128, 8], BF16)
    wcomb = C.sbuf("l1_wcomb", [33, 8 * 1024], BF16)
    bdq = C.sbuf("l1_bdq", [32, 1], F32)
    bdkr = C.sbuf("l1_bdkr", [64, 1], F32)
    buqt = C.sbuf("l1_buqt", [128, 8], F32)
    bqrt = C.sbuf("l1_bqrt", [32, 8], F32)
    bdkvb = C.sbuf("l1_bdkvb", [128, 32], F32)
    C.l1 = dict(kvt=kvt, ckvx=ckvx, wdq=wdq, wuq=wuq, wqr=wqr, wuk=wuk, bukt=bukt,
                wcomb=wcomb, bdq=bdq, buqt=buqt, bqrt=bqrt)
    sc = C.scope()
    wuvT = sc.sbuf("l1_wuvT", [128, 8 * 32], BF16)
    buvc = sc.sbuf("l1_buvc", [128, 8], BF16)
    bo1 = sc.sbuf("l1_bo1", [1, 1024], F32)
    wo1c = sc.sbuf("l1_wo1c", [128, 1024], BF16)
    ps = sc.psum("l1_ps", [128, 1024], F32)
    psc = sc.psum("l1_psc", [2, 1024], F32)
    st = C.stage

    S = Serial(C)
    S.add("sy", lambda e: e.dma_start(
        out=st[:, 0:256].rearrange("p (c f) -> p c f", c=DC),
        in_=hb["wdkv"].rearrange("(c p) f -> p c f", p=128)), dma=True)
    S.add("sy", lambda e: e.dma_start(
        out=st[:, 256:512].rearrange("p (c f) -> p c f", c=DC),
        in_=hb["wkr"].rearrange("(c p) f -> p c f", p=128)), dma=True)
    def wdkr_cast(e):
        e.tensor_copy(out=wdkr.ap().rearrange("p (c f) -> p c f", c=DC)[:, :, 0:32],
                      in_=st[:, 0:256].rearrange("p (c f) -> p c f", c=DC))
        return e.tensor_copy(out=wdkr.ap().rearrange("p (c f) -> p c f", c=DC)[:, :, 32:64],
                             in_=st[:, 256:512].rearrange("p (c f) -> p c f", c=DC))
    S.add("gp", wdkr_cast)
    S.add("sy", lambda e: e.dma_start(
        out=st[:, 0:256].rearrange("p (c f) -> p c f", c=DC),
        in_=hb["wdq"].rearrange("(c p) f -> p c f", p=128)), dma=True)
    S.add("gp", lambda e: e.tensor_copy(out=wdq[:, :], in_=st[:, 0:256]))
    S.add("sy", lambda e: e.dma_start(out=st[0:32, 0:1024], in_=hb["wuq"][:, :]), dma=True)
    S.add("gp", lambda e: e.tensor_copy(out=wuq[:, :], in_=st[0:32, 0:1024]))
    S.add("sy", lambda e: e.dma_start(out=st[0:32, 0:256], in_=hb["wqr"][:, :]), dma=True)
    S.add("gp", lambda e: e.tensor_copy(out=wqr[:, :], in_=st[0:32, 0:256]))
    S.add("sy", lambda e: e.dma_start(
        out=st[:, 0:256].rearrange("p (h f) -> p h f", h=8),
        in_=hb["wukT"].rearrange("h p f -> p h f")), dma=True)
    S.add("gp", lambda e: e.tensor_copy(out=wuk[:, :], in_=st[:, 0:256]))
    S.add("sy", lambda e: e.dma_start(out=st[:, 256:264], in_=hb["bukt"][:, :]), dma=True)
    S.add("gp", lambda e: e.tensor_copy(out=bukt[:, :], in_=st[:, 256:264]))
    S.add("sy", lambda e: e.dma_start(out=bdq[:, :], in_=hb["bdq"][:, :]), dma=True)
    S.add("sy", lambda e: e.dma_start(out=bdkr[0:32, :], in_=hb["bdkv"][:, :]), dma=True)
    S.add("sy", lambda e: e.dma_start(out=bdkr[32:64, :], in_=hb["bkr"][:, :]), dma=True)
    S.add("sy", lambda e: e.dma_start(out=buqt[:, :], in_=hb["buqt"][:, :]), dma=True)
    S.add("sy", lambda e: e.dma_start(out=bqrt[:, :], in_=hb["bqrt"][:, :]), dma=True)
    S.add("sy", lambda e: e.dma_start(out=bdkvb[:, :], in_=hb["bdkvb"][:, :]), dma=True)
    S.add("sy", lambda e: e.dma_start(
        out=st[:, 0:256].rearrange("p (h f) -> p h f", h=8),
        in_=hb["wuvT"].rearrange("h p f -> p h f")), dma=True)
    S.add("gp", lambda e: e.tensor_copy(out=wuvT[:, :], in_=st[:, 0:256]))
    S.add("sy", lambda e: e.dma_start(out=st[:, 256:264], in_=hb["buvt"][:, :]), dma=True)
    S.add("gp", lambda e: e.tensor_copy(out=buvc[:, :], in_=st[:, 256:264]))
    S.add("sy", lambda e: e.dma_start(out=bo1[:, :], in_=hb["bo1"][:, :]), dma=True)
    S.add("gp", lambda e: e.memset(wcomb[32:33, :], 0.0))
    S.add("gp", lambda e: e.memset(kvt[64:65, :], 1.0))
    S.add("gp", lambda e: e.memset(ckvx[:, :], 1.0))
    for h in range(L1_NH):
        S.add("sy", lambda e, h=h: e.dma_start(out=st[:, 1024:2048],
                                               in_=hb["wo1"][h * 128:(h + 1) * 128, :]), dma=True)
        S.add("gp", lambda e: e.tensor_copy(out=wo1c[:, :], in_=st[:, 1024:2048]))
        def wcmm(e, h=h):
            e.matmul(ps[0:32, 0:512], wuvT[:, h * 32:(h + 1) * 32], wo1c[:, 0:512],
                     start=True, stop=True)
            return e.matmul(ps[0:32, 512:1024], wuvT[:, h * 32:(h + 1) * 32],
                            wo1c[:, 512:1024], start=True, stop=True)
        S.add("pe", wcmm)
        S.add("dv", lambda e, h=h: e.tensor_copy(out=wcomb[0:32, h * 1024:(h + 1) * 1024],
                                                 in_=ps[0:32, 0:1024]))
        def crmm(e, h=h):
            e.matmul(psc[0:1, 0:512], buvc[:, h:h + 1], wo1c[:, 0:512],
                     start=(h == 0), stop=(h == L1_NH - 1), skip_group_check=True)
            return e.matmul(psc[0:1, 512:1024], buvc[:, h:h + 1], wo1c[:, 512:1024],
                            start=(h == 0), stop=(h == L1_NH - 1), skip_group_check=True)
        S.add("pe", crmm)
    S.add("dv", lambda e: e.tensor_add(out=st[0:1, 0:1024], in0=psc[0:1, 0:1024], in1=bo1[0:1, :]))
    S.add("dv", lambda e: e.tensor_copy(out=wcomb[32:33, 0:1024], in_=st[0:1, 0:1024]))
    S.emit(extra_waits=[("qs", C.consts_ready[0]), ("gp", C.consts_ready[1])])
    emit_fence(C)

    ps_k = sc.psum("l1_psk", [64, 2 * 512], F32)
    ps_v = sc.psum("l1_psv", [128, 2 * 512], F32)
    Sk = Sched(C, 8, {"pe": ["kmm"], "dv": ["kb"]})
    with nc.Block() as blk:
        @blk.tensor
        def _(e):
            for sl in range(8):
                if sl >= 2:
                    e.wait_ge(C.sems["dv"], Sk.t("dv", sl - 2, "kb"))
                for dch in range(DC):
                    ins = e.matmul(ps_k[:, (sl % 2) * 512:(sl % 2 + 1) * 512],
                                   wdkr[:, dch * 64:(dch + 1) * 64],
                                   xnt_sb[:, dch * T + sl * 512:dch * T + (sl + 1) * 512],
                                   start=(dch == 0), stop=(dch == DC - 1))
                ins.then_inc(C.sems["pe"], 1)

        @blk.vector
        def _(e):
            for sl in range(8):
                e.wait_ge(C.sems["pe"], Sk.t("pe", sl, "kmm"))
                e.tensor_scalar(out=kvt[0:64, sl * 512:(sl + 1) * 512],
                                in0=ps_k[:, (sl % 2) * 512:(sl % 2 + 1) * 512],
                                scalar1=bdkr[:, 0:1], scalar2=None,
                                op0=ALU.add).then_inc(C.sems["dv"], 1)
    Sk.done()
    Sv = Sched(C, NT, {"pe": ["vmm"], "dv": ["vb"]})
    with nc.Block() as blk:
        @blk.tensor
        def _(e):
            for i in range(NT):
                if i >= 2:
                    e.wait_ge(C.sems["dv"], Sv.t("dv", i - 2, "vb"))
                for dch in range(DC):
                    ins = e.matmul(ps_v[:, (i % 2) * 512:(i % 2) * 512 + 32],
                                   xnt_sb[:, dch * T + i * 128:dch * T + (i + 1) * 128],
                                   wdkr[:, dch * 64:dch * 64 + 32],
                                   start=(dch == 0), stop=(dch == DC - 1))
                ins.then_inc(C.sems["pe"], 1)

        @blk.vector
        def _(e):
            for i in range(NT):
                e.wait_ge(C.sems["pe"], Sv.t("pe", i, "vmm"))
                e.tensor_tensor(out=ckvx[:, i * 33:i * 33 + 32],
                                in0=ps_v[:, (i % 2) * 512:(i % 2) * 512 + 32],
                                in1=bdkvb[:, :], op=ALU.add).then_inc(C.sems["dv"], 1)
    Sv.done()
    emit_fence(C)
    sc.close()
    return C.l1



def emit_l1_attn(C, xnt_sb, hb, s_hbm):
    nc = C.nc
    K = C.consts
    L = C.l1
    kvt, ckvx = L["kvt"], L["ckvx"]
    sc = C.scope()
    cqt = sc.sbuf("m_cqt", [32, 512], BF16)
    qct = sc.sbuf("m_qct", [128, 4096], BF16)
    qeff = sc.sbuf("m_qeff", [65, 4096], BF16)
    ptb = sc.sbuf("m_ptb", [128, 2 * 2048], BF16)
    norm = sc.sbuf("m_norm", [33, 512], F32)
    mst = sc.sbuf("m_mst", [128, 4], F32)
    rb = sc.sbuf("m_rb", [32, 512], F32)
    ot = sc.sbuf("m_ot", [33, 4096], BF16)
    onesf2 = sc.sbuf("m_onesf2", [64, 64], F32)
    s_tile = sc.sbuf("m_stile", [128, 1024], F32)
    psA = sc.psum("m_psA", [128, 2048], F32)
    psB = sc.psum("m_psB", [128, 2048], F32)

    NG = 8
    cnt = {k: C.base[k] for k in ENGS}
    Tg = {}

    def res(eng, key):
        cnt[eng] += 16 if eng in ("qs", "qg", "qa") else 1
        Tg[key] = cnt[eng]

    for g4 in range(NG):
        res("pe", ("cq", g4))
        for h in range(8):
            res("pe", ("qc", g4, h))
        for h in range(8):
            res("pe", ("qe", g4, h))
        for h in range(8):
            for j in range(4):
                res("pe", ("s1a", g4, h, j))
                res("pe", ("s1b", g4, h, j))
                res("pe", ("mtr", g4, h, j))
            for b in range(8):
                res("pe", ("st4", g4, h, b))
                res("pe", ("pv", g4, h, b))
            res("pe", ("rb", g4, h))
        for j in range(4):
            res("pe", ("sp", g4, j))
        res("dv", ("cqb", g4))
        for h in range(8):
            res("dv", ("qce", g4, h))
        for h in range(8):
            res("dv", ("qeb", g4, h))
        for h in range(8):
            for j in range(4):
                res("dv", ("m0", g4, h, j))
                res("dv", ("m1", g4, h, j))
                res("dv", ("r64", g4, h, j))
            res("dv", ("rcp", g4, h))
            res("dv", ("onrm", g4, h))
        for h in range(8):
            res("ac", ("qea", g4, h))
        for h in range(8):
            for b in range(8):
                res("ac", ("exp", g4, h, b))
            res("ac", ("rbe", g4, h))
        for j in range(4):
            res("ac", ("spe", g4, j))
        for j in range(4):
            res("qs", ("sst", g4, j))

    def tg(key, default=0):
        return Tg.get(key, default)

    with nc.Block() as blk:
        @blk.gpsimd
        def _(e):
            e.memset(onesf2[:, :], 1.0)
            e.memset(ot[32:33, :], 1.0).then_inc(C.sems["gp"], 1)
        gp_t = C.base["gp"] + 1

        @blk.tensor
        def _(e):
            e.wait_ge(C.sems["gp"], gp_t)
            for g4 in range(NG):
                xsl = slice(g4 * 512, (g4 + 1) * 512)
                if g4 >= 1:
                    e.wait_ge(C.sems["ac"], tg(("spe", g4 - 1, 3)))
                for dch in range(DC):
                    ins = e.matmul(psA[0:32, 0:512], L["wdq"][:, dch * 32:(dch + 1) * 32],
                                   xnt_sb[:, dch * T + g4 * 512:dch * T + (g4 + 1) * 512],
                                   start=(dch == 0), stop=(dch == DC - 1))
                ins.then_inc(C.sems["pe"], 1)
                for h in range(8):
                    e.wait_ge(C.sems["dv"], tg(("cqb", g4)))
                    if h >= 2:
                        e.wait_ge(C.sems["dv"], tg(("qce", g4, h - 2)))
                    e.matmul(psA[:, 512 + (h % 2) * 512:512 + (h % 2) * 512 + 512],
                             L["wuq"][0:32, h * 128:(h + 1) * 128], cqt[:, :],
                             start=True, stop=True).then_inc(C.sems["pe"], 1)
                for h in range(8):
                    e.wait_ge(C.sems["dv"], tg(("qce", g4, h)))
                    if h >= 1:
                        e.wait_ge(C.sems["ac"], tg(("qea", g4, h - 1)))
                        e.wait_ge(C.sems["dv"], tg(("qeb", g4, h - 1)))
                    e.matmul(psA[0:32, 1536:2048], L["wuk"][:, h * 32:(h + 1) * 32],
                             qct[:, h * 512:(h + 1) * 512], start=True, stop=True)
                    e.matmul(psA[32:64, 1536:2048], L["wqr"][0:32, h * 32:(h + 1) * 32],
                             cqt[:, :], start=True, stop=True)
                    e.matmul(psA[64:65, 1536:2048], L["bukt"][:, h:h + 1],
                             qct[:, h * 512:(h + 1) * 512],
                             start=True, stop=True).then_inc(C.sems["pe"], 1)
                for h in range(8):
                    qsl = lambda j: slice(h * 512 + j * 128, h * 512 + (j + 1) * 128)
                    for j in range(4):
                        e.wait_ge(C.sems["ac"], tg(("qea", g4, 7)))
                        e.wait_ge(C.sems["dv"], tg(("qeb", g4, 7)))
                        if j >= 1:
                            e.wait_ge(C.sems["dv"], tg(("r64", g4, h, j - 1)))
                        elif h >= 1:
                            e.wait_ge(C.sems["dv"], tg(("onrm", g4, h - 1)))
                        for sl in range(4):
                            ins = e.matmul(psA[:, sl * 512:(sl + 1) * 512],
                                           qeff[0:65, qsl(j)],
                                           kvt[0:65, sl * 512:(sl + 1) * 512],
                                           start=True, stop=True)
                        ins.then_inc(C.sems["pe"], 1)
                        e.wait_ge(C.sems["dv"], tg(("m0", g4, h, j)))
                        for sl in range(4):
                            ins = e.matmul(psA[:, sl * 512:(sl + 1) * 512],
                                           qeff[0:65, qsl(j)],
                                           kvt[0:65, 2048 + sl * 512:2048 + (sl + 1) * 512],
                                           start=True, stop=True)
                        ins.then_inc(C.sems["pe"], 1)
                        e.wait_ge(C.sems["dv"], tg(("m1", g4, h, j)))
                        e.matmul(psA[64:65, 1920:2048], mst[:, 2:3], K["ident_f"][:, :],
                                 start=True, stop=True).then_inc(C.sems["pe"], 1)
                    for b in range(8):
                        e.wait_ge(C.sems["dv"], tg(("r64", g4, h, 3)))
                        if b >= 1:
                            e.wait_ge(C.sems["ac"], tg(("exp", g4, h, b - 1)))
                        elif h >= 1:
                            e.wait_ge(C.sems["ac"], tg(("rbe", g4, h - 1)))
                        for c in range(4):
                            kc = b * 4 + c
                            ins = e.matmul(psB[:, c * 512:(c + 1) * 512],
                                           kvt[0:65, kc * 128:(kc + 1) * 128],
                                           qeff[0:65, h * 512:(h + 1) * 512],
                                           start=True, stop=True)
                        ins.then_inc(C.sems["pe"], 1)
                        e.wait_ge(C.sems["ac"], tg(("exp", g4, h, b)))
                        if b == 0:
                            e.wait_ge(C.sems["dv"], tg(("onrm", g4, h - 1))) if h >= 1 else None
                        for c in range(4):
                            kc = b * 4 + c
                            ins = e.matmul(psA[0:33, 0:512],
                                           ckvx[:, kc * 33:(kc + 1) * 33],
                                           ptb[:, (b % 2) * 2048 + c * 512:(b % 2) * 2048 + (c + 1) * 512],
                                           start=(b == 0 and c == 0), stop=(b == 7 and c == 3))
                        ins.then_inc(C.sems["pe"], 1)
                    e.wait_ge(C.sems["dv"], tg(("rcp", g4, h)))
                    e.matmul(psB[0:32, 0:512], onesf2[32:33, 0:32], norm[32:33, 0:512],
                             start=True, stop=True).then_inc(C.sems["pe"], 1)
                for j in range(4):
                    e.wait_ge(C.sems["dv"], tg(("onrm", g4, 7)))
                    if j >= 1:
                        e.wait_ge(C.sems["ac"], tg(("spe", g4, j - 1)))
                    for half in range(2):
                        for h in range(8):
                            ins = e.matmul(psB[:, 1024 + half * 512:1024 + (half + 1) * 512],
                                           ot[0:33, h * 512 + j * 128:h * 512 + (j + 1) * 128],
                                           L["wcomb"][0:33, h * 1024 + half * 512:h * 1024 + (half + 1) * 512],
                                           start=(h == 0), stop=(h == 7))
                    ins.then_inc(C.sems["pe"], 1)

        @blk.vector
        def _(e):
            for g4 in range(NG):
                e.wait_ge(C.sems["pe"], tg(("cq", g4)))
                e.tensor_scalar(out=cqt[:, :], in0=psA[0:32, 0:512], scalar1=L["bdq"][:, 0:1],
                                scalar2=None, op0=ALU.add).then_inc(C.sems["dv"], 1)
                for h in range(8):
                    e.wait_ge(C.sems["pe"], tg(("qc", g4, h)))
                    e.tensor_scalar(out=qct[:, h * 512:(h + 1) * 512],
                                    in0=psA[:, 512 + (h % 2) * 512:512 + (h % 2) * 512 + 512],
                                    scalar1=L["buqt"][:, h:h + 1], scalar2=None,
                                    op0=ALU.add).then_inc(C.sems["dv"], 1)
                for h in range(8):
                    e.wait_ge(C.sems["pe"], tg(("qe", g4, h)))
                    e.tensor_scalar(out=qeff[32:64, h * 512:(h + 1) * 512],
                                    in0=psA[32:64, 1536:2048],
                                    scalar1=L["bqrt"][:, h:h + 1], scalar2=None,
                                    op0=ALU.add).then_inc(C.sems["dv"], 1)
                for h in range(8):
                    for j in range(4):
                        e.wait_ge(C.sems["pe"], tg(("s1a", g4, h, j)))
                        e.reduce_max(out=mst[:, 0:1], in_=psA[:, 0:2048], axis=AX.X,
                                     negate=True).then_inc(C.sems["dv"], 1)
                        e.wait_ge(C.sems["pe"], tg(("s1b", g4, h, j)))
                        i1 = e.reduce_max(out=mst[:, 1:2], in_=psA[:, 0:2048], axis=AX.X,
                                          negate=True)
                        C.settle(e, "dvx", i1)
                        e.tensor_tensor(out=mst[:, 2:3], in0=mst[:, 0:1],
                                        in1=mst[:, 1:2], op=ALU.min).then_inc(C.sems["dv"], 1)
                        e.wait_ge(C.sems["pe"], tg(("mtr", g4, h, j)))
                        e.tensor_tensor(out=qeff[64:65, h * 512 + j * 128:h * 512 + (j + 1) * 128],
                                        in0=qeff[64:65, h * 512 + j * 128:h * 512 + (j + 1) * 128],
                                        in1=psA[64:65, 1920:2048],
                                        op=ALU.add).then_inc(C.sems["dv"], 1)
                    e.wait_ge(C.sems["pe"], tg(("pv", g4, h, 7)))
                    i2 = e.reciprocal(out=norm[32:33, 0:512], in_=psA[32:33, 0:512])
                    C.settle(e, "dvx", i2)
                    e.engine_nop().then_inc(C.sems["dv"], 1)
                    e.wait_ge(C.sems["ac"], tg(("rbe", g4, h)))
                    e.tensor_tensor(out=ot[0:32, h * 512:(h + 1) * 512],
                                    in0=psA[0:32, 0:512], in1=rb[:, :],
                                    op=ALU.mult).then_inc(C.sems["dv"], 1)

        @blk.scalar
        def _(e):
            for g4 in range(NG):
                for h in range(8):
                    e.wait_ge(C.sems["pe"], tg(("qe", g4, h)))
                    e.activation(out=qeff[0:32, h * 512:(h + 1) * 512],
                                 in_=psA[0:32, 1536:2048], func=AF.Copy)
                    e.activation(out=qeff[64:65, h * 512:(h + 1) * 512],
                                 in_=psA[64:65, 1536:2048], func=AF.Copy).then_inc(C.sems["ac"], 1)
                for h in range(8):
                    for b in range(8):
                        e.wait_ge(C.sems["pe"], tg(("st4", g4, h, b)))
                        if b >= 2:
                            e.wait_ge(C.sems["pe"], tg(("pv", g4, h, b - 2)))
                        e.activation(out=ptb[:, (b % 2) * 2048:(b % 2 + 1) * 2048],
                                     in_=psB[:, 0:2048], func=AF.Exp,
                                     scale=L1_SCALE).then_inc(C.sems["ac"], 1)
                    e.wait_ge(C.sems["pe"], tg(("rb", g4, h)))
                    e.activation(out=rb[:, :], in_=psB[0:32, 0:512],
                                 func=AF.Copy).then_inc(C.sems["ac"], 1)
                for j in range(4):
                    e.wait_ge(C.sems["pe"], tg(("sp", g4, j)))
                    if g4 >= 1 or j >= 1:
                        pj = (g4, j - 1) if j >= 1 else (g4 - 1, 3)
                        e.wait_ge(C.sems["qs"], tg(("sst",) + pj))
                    e.activation(out=s_tile[:, :], in_=psB[:, 1024:2048],
                                 func=AF.Copy).then_inc(C.sems["ac"], 1)

        @blk.sync
        def _(e):
            for g4 in range(NG):
                for j in range(4):
                    qt = g4 * 4 + j
                    e.wait_ge(C.sems["ac"], tg(("spe", g4, j)))
                    e.dma_start(out=s_hbm[qt * 128:(qt + 1) * 128, :],
                                in_=s_tile[:, :]).then_inc(C.sems["qs"], 16)

    C.base.update({k: cnt[k] for k in cnt})
    C.base["gp"] = gp_t
    emit_fence(C)
    sc.close()



def emit_l1_moe(C, hb, xnt_hbm, xtok_hbm, oh_hbm, s_hbm, x1_hbm, y_hbm):
    nc = C.nc
    K = C.consts
    if not hasattr(C, "stage"):
        C.stage = C.sbuf("stage", [128, 4096], F32)
    sc = C.scope()
    wex = sc.sbuf("e_wex", [128, NE * DC * 1024], BF16)
    xg = sc.sbuf("e_xg", [128, 4096], BF16)
    xtok = sc.sbuf("e_xtok", [128, 1024], BF16)
    ohb = sc.sbuf("e_oh", [128, 8], BF16)
    ohf = sc.sbuf("e_ohf", [128, 8], F32)
    st = sc.sbuf("e_s", [128, 1024], F32)
    x1t = sc.sbuf("e_x1", [128, 1024], F32)
    yac = sc.sbuf("e_yac", [128, 1024], F32)
    tmp = sc.sbuf("e_tmp", [128, 1024], F32)
    gs = sc.sbuf("e_gs", [128, 1024], F32)
    ps_e = sc.psum("e_pse", [128, 2 * 1024], F32)

    Sp = Serial(C)
    for ee in range(NE):
        for hh in range(2):
            Sp.add("sy", lambda e, ee=ee, hh=hh: e.dma_start(
                out=C.stage.ap().rearrange("p (c f) -> p c f", c=4),
                in_=hb["wexT"].rearrange("e (g c p) f -> e g p c f", p=128, c=4)[ee, hh, :, :, :]), dma=True)
            Sp.add("gp", lambda e, ee=ee, hh=hh: e.tensor_copy(
                out=wex.ap().rearrange("p (e c f) -> p e c f", e=NE, c=DC)[:, ee, hh * 4:(hh + 1) * 4, :],
                in_=C.stage.ap().rearrange("p (c f) -> p c f", c=4)))
    Sp.emit()
    emit_fence(C)

    cnt = {k: C.base[k] for k in ENGS}
    Tg = {}

    def res(eng, key):
        cnt[eng] += 16 if eng in ("qs", "qg", "qa") else 1
        Tg[key] = cnt[eng]

    for i in range(NT):
        if i % 4 == 0:
            res("qa", ("xg", i // 4))
        for nm in ["xt", "oh", "s", "x1"]:
            res("qs", (nm, i))
        for ee in range(NE):
            res("pe", ("mm", i, ee))
        res("dv", ("ohf", i))
        for ee in range(NE):
            res("ac", ("ev", i, ee))
            if ee >= 1:
                res("dv", ("ad", i, ee))
        res("dv", ("resid", i))
        res("ac", ("silu", i))
        res("dv", ("comb", i))
        res("qs", ("yo", i))

    def tg(key, default=0):
        return Tg.get(key, default)

    with nc.Block() as blk:
        @blk.sync
        def _(e):
            for i in range(NT):
                if i >= 1:
                    e.wait_ge(C.sems["dv"], tg(("resid", i - 1)))
                e.dma_start(out=xtok[:, :], in_=xtok_hbm[i * 128:(i + 1) * 128, :]).then_inc(C.sems["qs"], 16)
                e.dma_start(out=ohb[:, :], in_=oh_hbm[i * 128:(i + 1) * 128, :]).then_inc(C.sems["qs"], 16)
                if i >= 1:
                    e.wait_ge(C.sems["dv"], tg(("comb", i - 1)))
                e.dma_start(out=st[:, :], in_=s_hbm[i * 128:(i + 1) * 128, :]).then_inc(C.sems["qs"], 16)
                e.dma_start(out=x1t[:, :], in_=x1_hbm[i * 128:(i + 1) * 128, :]).then_inc(C.sems["qs"], 16)
                e.wait_ge(C.sems["dv"], tg(("comb", i)))
                e.dma_start(out=y_hbm[i * 128:(i + 1) * 128, :], in_=gs[:, :]).then_inc(C.sems["qs"], 16)

        @blk.scalar
        def _(e):
            for i in range(NT):
                if i % 4 == 0:
                    gi = i // 4
                    if gi >= 1:
                        e.wait_ge(C.sems["pe"], tg(("mm", gi * 4 - 1, NE - 1)))
                    e.dma_start(out=xg.ap().rearrange("p (c t) -> p c t", c=DC),
                                in_=xnt_hbm.rearrange("(c p) t -> p c t", p=128)[:, :, gi * 512:(gi + 1) * 512]
                                ).then_inc(C.sems["qa"], 16)
                for ee in range(NE):
                    e.wait_ge(C.sems["pe"], tg(("mm", i, ee)))
                    e.wait_ge(C.sems["dv"], tg(("ohf", i)))
                    if ee >= 2:
                        e.wait_ge(C.sems["dv"], tg(("ad", i, ee - 1)))
                    dst = yac if ee == 0 else tmp
                    e.activation(out=dst[:, :], in_=ps_e[:, (ee % 2) * 1024:(ee % 2 + 1) * 1024],
                                 func=AF.Copy, scale=ohf[:, ee:ee + 1]).then_inc(C.sems["ac"], 1)
                e.wait_ge(C.sems["dv"], tg(("resid", i)))
                e.activation(out=gs[:, :], in_=yac[:, :], func=AF.Silu).then_inc(C.sems["ac"], 1)

        @blk.tensor
        def _(e):
            for i in range(NT):
                gi = i // 4
                e.wait_ge(C.sems["qa"], tg(("xg", gi)))
                for ee in range(NE):
                    if ee >= 2:
                        e.wait_ge(C.sems["ac"], tg(("ev", i, ee - 2)))
                    elif i >= 1:
                        e.wait_ge(C.sems["ac"], tg(("ev", i - 1, NE - 2 + ee)))
                    for half in range(2):
                        for dch in range(DC):
                            ins = e.matmul(
                                ps_e[:, (ee % 2) * 1024 + half * 512:(ee % 2) * 1024 + (half + 1) * 512],
                                xg[:, dch * 512 + (i % 4) * 128:dch * 512 + (i % 4) * 128 + 128],
                                wex[:, (ee * 8 + dch) * 1024 + half * 512:(ee * 8 + dch) * 1024 + (half + 1) * 512],
                                start=(dch == 0), stop=(dch == DC - 1))
                    ins.then_inc(C.sems["pe"], 1)

        @blk.vector
        def _(e):
            for i in range(NT):
                e.wait_ge(C.sems["qs"], tg(("oh", i)))
                e.tensor_copy(out=ohf[:, :], in_=ohb[:, :]).then_inc(C.sems["dv"], 1)
                for ee in range(1, NE):
                    e.wait_ge(C.sems["ac"], tg(("ev", i, ee)))
                    e.tensor_add(out=yac[:, :], in0=yac[:, :], in1=tmp[:, :]).then_inc(C.sems["dv"], 1)
                e.wait_ge(C.sems["qs"], tg(("xt", i)))
                i1 = e.tensor_tensor(out=yac[:, :], in0=yac[:, :], in1=xtok[:, :], op=ALU.add)
                C.settle(e, "dvx", i1)
                e.engine_nop().then_inc(C.sems["dv"], 1)
                e.wait_ge(C.sems["ac"], tg(("silu", i)))
                e.wait_ge(C.sems["qs"], tg(("x1", i)))
                if i >= 1:
                    e.wait_ge(C.sems["qs"], tg(("yo", i - 1)))
                i2 = e.tensor_tensor(out=gs[:, :], in0=st[:, :], in1=gs[:, :], op=ALU.mult)
                C.settle(e, "dvx", i2)
                e.tensor_tensor(out=gs[:, :], in0=gs[:, :], in1=x1t[:, :],
                                op=ALU.add).then_inc(C.sems["dv"], 1)

    C.base.update({k: cnt[k] for k in cnt})
    emit_fence(C)
    sc.close()



L0A_CONSTS = None
LB_CONSTS = ["ident_f", "ident_b"]


def _declare(C, hc, ha, const_names, tensors):
    hb = {}
    for nm in (const_names if const_names is not None else
               ["ident_f", "ident_b", "rope_lhsT", "cosk", "sink", "winp2a", "winm2a",
                "cos_tok", "sin_tok", "ltm4"]):
        v = hc[nm]
        dt = F32 if v.dtype == np.float32 else BF16
        hb[nm] = C.nc.dram_tensor(nm, list(v.shape), dt, kind="ExternalInput").ap()
    for nm, kind in tensors:
        v = ha.get(nm)
        if v is not None:
            dt = I32 if v.dtype == np.int32 else (F32 if v.dtype == np.float32 else BF16)
            hb[nm] = C.nc.dram_tensor(nm, list(v.shape), dt, kind=kind).ap()
    return hb


def _finalize(C):
    with C.nc.Block() as blk:
        @blk.gpsimd
        def _(e):
            for s2 in C.base:
                if C.base[s2] > 0:
                    e.wait_ge(C.sems[s2], C.base[s2])
            for h, v in C.pending_sw:
                e.wait_ge(h, v)
    C.pending_sw = []
    C.close()


_BUILD_CACHE = {}


def build_l0a(hc, ha):
    C = Ctx("l0a")
    hb = _declare(C, hc, ha, None,
                  [("tg", "ExternalInput"), ("wq", "ExternalInput"), ("wk", "ExternalInput"),
                   ("wv", "ExternalInput"), ("bkt", "ExternalInput"), ("bqt", "ExternalInput"),
                   ("bvb", "ExternalInput"), ("wo_ext", "ExternalInput")])
    x = C.nc.dram_tensor("x", [T, D], F32, kind="ExternalInput").ap()
    xnt = C.nc.dram_tensor("xnt", [D, T], BF16, kind="ExternalOutput").ap()
    s_hbm = C.nc.dram_tensor("s", [T, D], F32, kind="ExternalOutput").ap()
    sg_hbm = C.nc.dram_tensor("sg", [4, D], F32, kind="ExternalOutput").ap()
    emit_consts(C, hb)
    xnt_sb = emit_phase_a(C, x, xnt, layer=0, gate=None)
    emit_l0_kv(C, xnt_sb, hb)
    emit_l0_globals_pre(C, hb, x)
    w = emit_l0_attn(C, xnt_sb, hb, s_hbm)
    emit_l0_fixup(C, hb, sg_hbm, w["wo"], w["wo_row"])
    _finalize(C)
    return C


def build_l0b(hc, ha):
    C = Ctx("l0b")
    hb = _declare(C, hc, ha, LB_CONSTS,
                  [("tg", "ExternalInput"), ("wfcT", "ExternalInput"),
                   ("wprojT", "ExternalInput"), ("lnwT", "ExternalInput"),
                   ("lnbT", "ExternalInput")])
    x = C.nc.dram_tensor("x", [T, D], F32, kind="ExternalInput").ap()
    xnt = C.nc.dram_tensor("xnt", [D, T], BF16, kind="ExternalInput").ap()
    s_hbm = C.nc.dram_tensor("s", [T, D], F32, kind="ExternalInput").ap()
    sg_hbm = C.nc.dram_tensor("sg", [4, D], F32, kind="ExternalInput").ap()
    wfc_t = C.nc.dram_tensor("wfct", [1024, 4096], BF16, kind="Internal").ap()
    wproj_t = C.nc.dram_tensor("wpjt", [4096, 1024], BF16, kind="Internal").ap()
    x1 = C.nc.dram_tensor("x1", [T, D], F32, kind="ExternalOutput").ap()
    emit_consts(C, hb, names=LB_CONSTS)
    emit_l0_ternarize(C, hb, wfc_t, wproj_t)
    emit_l0_ffn(C, hb, wfc_t, wproj_t, xnt, x, s_hbm, sg_hbm, x1)
    _finalize(C)
    return C


def build_l1a(hc, ha):
    C = Ctx("l1a")
    hb = _declare(C, hc, ha, LB_CONSTS,
                  [("gateT", "ExternalInput"), ("wdq", "ExternalInput"), ("bdq", "ExternalInput"),
                   ("wuq", "ExternalInput"), ("buqt", "ExternalInput"),
                   ("wdkv", "ExternalInput"), ("bdkv", "ExternalInput"),
                   ("wkr", "ExternalInput"), ("bkr", "ExternalInput"),
                   ("wukT", "ExternalInput"), ("bukt", "ExternalInput"),
                   ("wuvT", "ExternalInput"), ("buvt", "ExternalInput"),
                   ("wqr", "ExternalInput"), ("bqrt", "ExternalInput"),
                   ("wo1", "ExternalInput"), ("bo1", "ExternalInput"),
                   ("bdkvb", "ExternalInput")])
    x1 = C.nc.dram_tensor("x1", [T, D], F32, kind="ExternalInput").ap()
    xnt = C.nc.dram_tensor("xnt1", [D, T], BF16, kind="ExternalOutput").ap()
    oh = C.nc.dram_tensor("oh", [T, 8], BF16, kind="ExternalOutput").ap()
    xtok = C.nc.dram_tensor("xtok", [T, D], BF16, kind="ExternalOutput").ap()
    s_hbm = C.nc.dram_tensor("s1", [T, D], F32, kind="ExternalOutput").ap()
    emit_consts(C, hb, names=LB_CONSTS)
    gd = {"gateT_hbm": hb["gateT"], "oh_hbm": oh, "xtok_hbm": xtok}
    xnt_sb = emit_phase_a(C, x1, xnt, layer=1, gate=gd)
    emit_l1_kv(C, xnt_sb, hb)
    emit_l1_attn(C, xnt_sb, hb, s_hbm)
    _finalize(C)
    return C


def build_l1b(hc, ha):
    C = Ctx("l1b")
    hb = _declare(C, hc, ha, LB_CONSTS, [("wexT", "ExternalInput")])
    xnt = C.nc.dram_tensor("xnt1", [D, T], BF16, kind="ExternalInput").ap()
    oh = C.nc.dram_tensor("oh", [T, 8], BF16, kind="ExternalInput").ap()
    xtok = C.nc.dram_tensor("xtok", [T, D], BF16, kind="ExternalInput").ap()
    s_hbm = C.nc.dram_tensor("s1", [T, D], F32, kind="ExternalInput").ap()
    x1 = C.nc.dram_tensor("x1", [T, D], F32, kind="ExternalInput").ap()
    y = C.nc.dram_tensor("y", [T, D], F32, kind="ExternalOutput").ap()
    emit_consts(C, hb, names=LB_CONSTS)
    emit_l1_moe(C, hb, xnt, xtok, oh, s_hbm, x1, y)
    _finalize(C)
    return C


def build_fused(hc, ha):
    C = Ctx("mk")
    hb = _declare(C, hc, ha, None,
                  [("tg", "ExternalInput"), ("wq", "ExternalInput"), ("wk", "ExternalInput"),
                   ("wv", "ExternalInput"), ("bkt", "ExternalInput"), ("bqt", "ExternalInput"),
                   ("bvb", "ExternalInput"), ("wo_ext", "ExternalInput"),
                   ("wfcT", "ExternalInput"), ("wprojT", "ExternalInput"),
                   ("lnwT", "ExternalInput"), ("lnbT", "ExternalInput"),
                   ("gateT", "ExternalInput"), ("wdq", "ExternalInput"), ("bdq", "ExternalInput"),
                   ("wuq", "ExternalInput"), ("buqt", "ExternalInput"),
                   ("wdkv", "ExternalInput"), ("bdkv", "ExternalInput"),
                   ("wkr", "ExternalInput"), ("bkr", "ExternalInput"),
                   ("wukT", "ExternalInput"), ("bukt", "ExternalInput"),
                   ("wuvT", "ExternalInput"), ("buvt", "ExternalInput"),
                   ("wqr", "ExternalInput"), ("bqrt", "ExternalInput"),
                   ("wo1", "ExternalInput"), ("bo1", "ExternalInput"),
                   ("bdkvb", "ExternalInput"), ("wexT", "ExternalInput")])
    x = C.nc.dram_tensor("x", [T, D], F32, kind="ExternalInput").ap()
    xnt0 = C.nc.dram_tensor("xnt0", [D, T], BF16, kind="Internal").ap()
    s0 = C.nc.dram_tensor("s0", [T, D], F32, kind="Internal").ap()
    sg0 = C.nc.dram_tensor("sg0", [4, D], F32, kind="Internal").ap()
    wfc_t = C.nc.dram_tensor("wfct", [1024, 4096], BF16, kind="Internal").ap()
    wproj_t = C.nc.dram_tensor("wpjt", [4096, 1024], BF16, kind="Internal").ap()
    x1 = C.nc.dram_tensor("x1", [T, D], F32, kind="Internal").ap()
    xnt1 = C.nc.dram_tensor("xnt1", [D, T], BF16, kind="Internal").ap()
    oh = C.nc.dram_tensor("oh", [T, 8], BF16, kind="Internal").ap()
    xtok = C.nc.dram_tensor("xtok", [T, D], BF16, kind="Internal").ap()
    s1 = C.nc.dram_tensor("s1", [T, D], F32, kind="Internal").ap()
    y = C.nc.dram_tensor("y", [T, D], F32, kind="ExternalOutput").ap()

    emit_consts(C, hb, names=["ident_f", "ident_b"])

    sc0 = C.scope(); C.arena = sc0
    C.stage = C.sbuf("stage0", [128, 4096], F32)
    emit_consts(C, hb, names=["rope", "cosk", "sink", "winp2a", "winm2a"])
    xnt_sb = emit_phase_a(C, x, xnt0, layer=0, gate=None)
    emit_l0_kv(C, xnt_sb, hb)
    emit_l0_globals_pre(C, hb, x)
    w = emit_l0_attn(C, xnt_sb, hb, s0)
    emit_l0_fixup(C, hb, sg0, w["wo"], w["wo_row"])
    emit_fence(C); C.arena = None; sc0.close(); del C.stage

    sc1 = C.scope(); C.arena = sc1
    emit_l0_ternarize(C, hb, wfc_t, wproj_t)
    emit_l0_ffn(C, hb, wfc_t, wproj_t, xnt0, x, s0, sg0, x1)
    emit_fence(C); C.arena = None; sc1.close()

    sc2 = C.scope(); C.arena = sc2
    C.stage = C.sbuf("stage1", [128, 4096], F32)
    gd = {"gateT_hbm": hb["gateT"], "oh_hbm": oh, "xtok_hbm": xtok}
    xnt_sb1 = emit_phase_a(C, x1, xnt1, layer=1, gate=gd)
    emit_l1_kv(C, xnt_sb1, hb)
    emit_l1_attn(C, xnt_sb1, hb, s1)
    emit_fence(C); C.arena = None; sc2.close(); del C.stage

    emit_l1_moe(C, hb, xnt1, xtok, oh, s1, x1, y)
    _finalize(C)
    return C


def kernel(**inputs):
    hc = host_consts()
    ha = host_layout(inputs)
    x_full = np.ascontiguousarray(np.asarray(inputs["L0_attn_x"], dtype=np.float32))
    n = 8

    weight_names = ["tg", "wq", "wk", "wv", "bkt", "bqt", "bvb", "wo_ext",
                    "wfcT", "wprojT", "lnwT", "lnbT",
                    "gateT", "wdq", "bdq", "wuq", "buqt", "wdkv", "bdkv",
                    "wkr", "bkr", "wukT", "bukt", "wuvT", "buvt",
                    "wqr", "bqrt", "wo1", "bo1", "bdkvb", "wexT"]
    consts_all = ["ident_f", "ident_b", "rope_lhsT", "cosk", "sink", "winp2a",
                  "winm2a", "cos_tok", "sin_tok", "ltm4"]
    base = {nm: hc[nm] for nm in consts_all}
    base.update({nm: ha[nm] for nm in weight_names})
    Cf = build_fused(hc, ha)
    maps = [dict(base, x=x_full[b]) for b in range(n)]
    rf = run_bass_kernel_spmd(Cf.nc, maps, core_ids=list(range(n))).results
    return np.stack([rf[b]["y"] for b in range(n)], axis=0).astype(np.float32)


def kernel_unfused(**inputs):
    hc = host_consts()
    ha = host_layout(inputs)
    x_full = np.ascontiguousarray(np.asarray(inputs["L0_attn_x"], dtype=np.float32))
    n = 8

    def shared(names):
        return {nm: (hc[nm] if nm in hc else ha[nm]) for nm in names}

    consts_all = ["ident_f", "ident_b", "rope_lhsT", "cosk", "sink", "winp2a",
                  "winm2a", "cos_tok", "sin_tok", "ltm4"]
    consts_lb = ["ident_f", "ident_b"]

    Ca = build_l0a(hc, ha)
    base_a = shared(consts_all + ["tg", "wq", "wk", "wv", "bkt", "bqt", "bvb", "wo_ext"])
    maps = [dict(base_a, x=x_full[b]) for b in range(n)]
    ra = run_bass_kernel_spmd(Ca.nc, maps, core_ids=list(range(n))).results

    Cb = build_l0b(hc, ha)
    base_b = shared(consts_lb + ["tg", "wfcT", "wprojT", "lnwT", "lnbT"])
    maps = [dict(base_b, x=x_full[b], xnt=ra[b]["xnt"], s=ra[b]["s"], sg=ra[b]["sg"])
            for b in range(n)]
    rb = run_bass_kernel_spmd(Cb.nc, maps, core_ids=list(range(n))).results

    Cc = build_l1a(hc, ha)
    base_c = shared(consts_lb + ["gateT", "wdq", "bdq", "wuq", "buqt", "wdkv", "bdkv",
                                 "wkr", "bkr", "wukT", "bukt", "wuvT", "buvt",
                                 "wqr", "bqrt", "wo1", "bo1", "bdkvb"])
    maps = [dict(base_c, x1=rb[b]["x1"]) for b in range(n)]
    rc = run_bass_kernel_spmd(Cc.nc, maps, core_ids=list(range(n))).results

    Cd = build_l1b(hc, ha)
    base_d = shared(consts_lb + ["wexT"])
    maps = [dict(base_d, xnt1=rc[b]["xnt1"], oh=rc[b]["oh"], xtok=rc[b]["xtok"],
                 s1=rc[b]["s1"], x1=rb[b]["x1"]) for b in range(n)]
    rd = run_bass_kernel_spmd(Cd.nc, maps, core_ids=list(range(n))).results

    out = np.stack([rd[b]["y"] for b in range(n)], axis=0).astype(np.float32)
    return out


# version 04 (final)
# speedup vs baseline: 1.0073x; 1.0073x over previous
import math
import numpy as np
import ml_dtypes
import concourse.bass as bass
import concourse.mybir as mybir
from concourse.bass_utils import run_bass_kernel_spmd
from concourse.bass_core_rust import IndirectOffsetOnAxis

F32 = mybir.dt.float32
BF16 = mybir.dt.bfloat16
I32 = mybir.dt.int32
AF = mybir.ActivationFunctionType
ALU = mybir.AluOpType
AX = mybir.AxisListType

D = 1024
T = 4096
NT = T // 128
DC = D // 128
H0, G0, HD0 = 16, 4, 64
WIN = 256
L1_NH, L1_DK, L1_DC, L1_DHR = 8, 128, 32, 32
L1_SCALE = 1.0 / math.sqrt(L1_DK + L1_DHR)
L0_SCALE = 1.0 / math.sqrt(HD0)
FFH = 4096
NE = 8

BF = ml_dtypes.bfloat16

HEADS_A = [0, 1, 2, 3, 8, 9, 10, 11]
HEADS_B = [4, 5, 6, 7, 12, 13, 14, 15]
Q_PERM = np.concatenate([
    np.concatenate([np.arange(HEADS_A[c] * 64, HEADS_A[c] * 64 + 64),
                    np.arange(HEADS_B[c] * 64, HEADS_B[c] * 64 + 64)])
    for c in range(8)])


def head_slot(h):
    half = (h // 4) % 2
    c = (HEADS_A if half == 0 else HEADS_B).index(h)
    return c, half



def host_consts():
    c = {}
    c["ident_b"] = np.eye(128, dtype=BF)
    c["ident_f"] = np.eye(128, dtype=np.float32)
    p64 = np.zeros((64, 64), np.float32)
    for i in range(32):
        p64[2 * i, 2 * i + 1] = -1.0
        p64[2 * i + 1, 2 * i] = 1.0
    p128 = np.zeros((128, 128), np.float32)
    p128[0:64, 0:64] = p64
    p128[64:128, 64:128] = p64
    c["rope_lhsT"] = np.ascontiguousarray(p128.T).astype(BF)
    inv_freq = (1.0 / (10000.0 ** (np.arange(0, HD0, 2, dtype=np.float32) / HD0))).astype(np.float32)
    t = np.arange(T, dtype=np.float32)
    freqs = t[:, None] * inv_freq[None, :]
    emb = np.concatenate([freqs, freqs], axis=-1)
    sin64 = np.sin(emb).astype(np.float32)
    cos64 = np.cos(emb).astype(np.float32)
    c["cosk"] = np.ascontiguousarray(np.vstack([cos64.T, cos64.T])).astype(BF)
    c["sink"] = np.ascontiguousarray(np.vstack([sin64.T, sin64.T])).astype(BF)
    c["cos_tok"] = np.ascontiguousarray(np.hstack([cos64, cos64])).astype(BF)
    c["sin_tok"] = np.ascontiguousarray(np.hstack([sin64, sin64])).astype(BF)
    p = np.arange(128)
    c["winp2a"] = np.where(p[:, None] - p[None, :] >= 0, 0.0, -1e30).astype(BF)
    c["winm2a"] = np.where(p[:, None] - p[None, :] <= 0, 0.0, -1e30).astype(BF)
    c["ltm4"] = np.ascontiguousarray(np.tril(np.ones((4, 4), np.float32), -1)).astype(BF)
    return c


def host_layout(inp):
    a = {}
    cc = np.ascontiguousarray
    a["tg"] = inp["L0_attn_global_positions"].astype(np.int32).reshape(4, 1)
    a["wq"] = cc(np.asarray(inp["L0_attn_Wq"])[:, Q_PERM])
    a["bqt"] = cc(np.asarray(inp["L0_attn_bq"])[Q_PERM].reshape(8, 128).T)
    a["wk"] = cc(inp["L0_attn_Wk"])
    a["bkt"] = cc(np.asarray(inp["L0_attn_bk"]).reshape(2, 128).T)
    a["wv"] = cc(inp["L0_attn_Wv"])
    a["bvb"] = cc(np.broadcast_to(np.asarray(inp["L0_attn_bv"]), (128, 256)))
    a["wo_ext"] = cc(np.concatenate([np.asarray(inp["L0_attn_Wo"])[Q_PERM, :],
                                     np.asarray(inp["L0_attn_bo"])[None, :]], axis=0))
    a["wfcT"] = cc(np.asarray(inp["L0_ffn_W_fc"]).T)
    a["wprojT"] = cc(np.asarray(inp["L0_ffn_W_proj"]).T)
    a["lnwT"] = cc(np.asarray(inp["L0_ffn_ln_w"]).reshape(32, 128).T)
    a["lnbT"] = cc(np.asarray(inp["L0_ffn_ln_b"]).reshape(32, 128).T)
    a["wdq"] = cc(inp["L1_attn_W_DQ"])
    a["bdq"] = cc(np.asarray(inp["L1_attn_b_DQ"]).reshape(32, 1))
    a["wuq"] = cc(inp["L1_attn_W_UQ"])
    a["buqt"] = cc(np.asarray(inp["L1_attn_b_UQ"]).reshape(8, 128).T)
    a["wdkv"] = cc(inp["L1_attn_W_DKV"])
    a["bdkv"] = cc(np.asarray(inp["L1_attn_b_DKV"]).reshape(32, 1))
    a["wkr"] = cc(inp["L1_attn_W_KR"])
    a["bkr"] = cc(np.asarray(inp["L1_attn_b_KR"]).reshape(32, 1))
    wuk = np.asarray(inp["L1_attn_W_UK"])
    a["wukT"] = cc(np.stack([wuk[:, h * 128:(h + 1) * 128].T for h in range(8)]))
    a["bukt"] = cc(np.asarray(inp["L1_attn_b_UK"]).reshape(8, 128).T)
    wuv = np.asarray(inp["L1_attn_W_UV"])
    a["wuvT"] = cc(np.stack([wuv[:, h * 128:(h + 1) * 128].T for h in range(8)]))
    a["buvt"] = cc(np.asarray(inp["L1_attn_b_UV"]).reshape(8, 128).T)
    a["wqr"] = cc(inp["L1_attn_W_QR"])
    a["bqrt"] = cc(np.asarray(inp["L1_attn_b_QR"]).reshape(8, 32).T)
    a["bdkvb"] = cc(np.broadcast_to(np.asarray(inp["L1_attn_b_DKV"]), (128, 32)))
    a["wo1"] = cc(inp["L1_attn_W_O"])
    a["bo1"] = cc(np.asarray(inp["L1_attn_b_O"]).reshape(1, 1024))
    a["gateT"] = cc(np.asarray(inp["L1_ffn_gate_w"]).T)
    a["wexT"] = cc(np.stack([np.asarray(inp["L1_ffn_expert_w"][e]).T for e in range(NE)]))
    out = {}
    for k, v in a.items():
        v = np.asarray(v)
        if v.dtype == np.float64:
            v = v.astype(np.float32)
        out[k] = v
    return out



ENGS = ["qs", "qg", "qa", "pe", "dv", "ac", "gp"]
XENGS = ["dvx", "acx", "gpx", "pex"]


class Ctx:
    def __init__(self, name="prog"):
        import os
        drc = os.environ.get("BASS_DRC", "1") == "1"
        self.nc = bass.Bass("TRN2", target_bir_lowering=False, detect_race_conditions=drc)
        self.name = name
        self._stack = []
        self.sems = {}
        self.base = {}
        for s in ENGS:
            self.sems[s] = self._enter(self.nc.semaphore(f"sem_{s}"))
            self.base[s] = 0
        self.xcnt = {}
        for s in XENGS:
            self.sems[s] = self._enter(self.nc.semaphore(f"sem_{s}"))
            self.xcnt[s] = 0
        self.consts = {}
        self.pending_sw = []

    def swsem(self):
        if not hasattr(self, "_swsems"):
            self._swsems = []
        n = len(self._swsems)
        h = self._enter(self.nc.semaphore(f"sem_sw{n}"))
        self._swsems.append(h)
        self.pending_sw.append((h, 16))
        return h

    def settle(self, e, key, instr):
        self.xcnt[key] += 1
        instr.then_inc(self.sems[key], 1)
        e.wait_ge(self.sems[key], self.xcnt[key])

    def _enter(self, cm):
        v = cm.__enter__()
        self._stack.append(cm)
        return v

    def sbuf(self, name, shape, dtype):
        ar = getattr(self, "arena", None)
        if ar is not None:
            return ar.sbuf(name, shape, dtype)
        return self._enter(self.nc.sbuf_tensor(name, shape, dtype))

    def psum(self, name, shape, dtype=F32):
        ar = getattr(self, "arena", None)
        if ar is not None:
            return ar.psum(name, shape, dtype)
        return self._enter(self.nc.psum_tensor(name, shape, dtype))

    def scope(self):
        return Scope(self)

    def close(self):
        while self._stack:
            self._stack.pop().__exit__(None, None, None)


class Scope:

    def __init__(self, C):
        self.C = C
        self.cms = []

    def _enter(self, cm):
        v = cm.__enter__()
        self.cms.append(cm)
        return v

    def sbuf(self, name, shape, dtype):
        return self._enter(self.C.nc.sbuf_tensor(name, shape, dtype))

    def psum(self, name, shape, dtype=F32):
        return self._enter(self.C.nc.psum_tensor(name, shape, dtype))

    def close(self):
        while self.cms:
            self.cms.pop().__exit__(None, None, None)


def emit_fence(C, engines=None, queues=None):
    targets = [(C.sems[s], v) for s, v in C.base.items() if v > 0]
    targets += list(C.pending_sw)
    C.pending_sw = []
    with C.nc.Block() as blk:
        def mk(e):
            for h, v in targets:
                e.wait_ge(h, v)
        blk.tensor(lambda e: mk(e))
        blk.vector(lambda e: mk(e))
        blk.scalar(lambda e: mk(e))
        blk.gpsimd(lambda e: mk(e))
        blk.sync(lambda e: mk(e))


class Sched:

    def __init__(self, C, tiles, table, pre=None):
        self.C = C
        self.tiles = tiles
        self.table = {k: list(v) for k, v in table.items()}
        self.pre = dict(pre or {})
        self.base0 = dict(C.base)
        self.b = {}
        for e in table:
            step = 16 if e in ("qs", "qg", "qa") else 1
            self.b[e] = C.base[e] + self.pre.get(e, 0) * step

    def step(self, eng):
        return 16 if eng in ("qs", "qg", "qa") else 1

    def per(self, eng):
        return len(self.table[eng]) * self.step(eng)

    def t(self, eng, i, name):
        idx = self.table[eng].index(name) + 1
        return self.b[eng] + i * self.per(eng) + idx * self.step(eng)

    def pre_t(self, eng, k=None):
        n = self.pre.get(eng, 0) if k is None else k
        return self.base0[eng] + n * self.step(eng)

    def done(self):
        for e in self.table:
            self.C.base[e] = self.b[e] + self.tiles * self.per(e)
        return self



ALL_CONSTS = [("ident_f", "c_identf", [128, 128], F32, "ident_f"),
              ("ident_b", "c_identb", [128, 128], BF16, "ident_b"),
              ("rope", "c_rope", [128, 128], BF16, "rope_lhsT"),
              ("cosk", "c_cosk", [128, T], BF16, "cosk"),
              ("sink", "c_sink", [128, T], BF16, "sink"),
              ("winp2a", "c_winp2a", [128, 128], BF16, "winp2a"),
              ("winm2a", "c_winm2a", [128, 128], BF16, "winm2a")]


def emit_consts(C, hb, names=None):
    nc = C.nc
    K = C.consts
    sel = [t for t in ALL_CONSTS if names is None or t[0] in names]
    for key, sbn, shape, dt, _src in sel:
        K[key] = C.sbuf(sbn, shape, dt)
    derived = "iotaq" not in K
    if derived:
        K["iotaq"] = C.sbuf("c_iotaq", [128, NT], F32)
        K["ones_r"] = C.sbuf("c_onesr", [1, 128], BF16)
        K["ones_rf"] = C.sbuf("c_onesrf", [1, 128], F32)
        K["cvals"] = C.sbuf("c_cvals", [128, 4], F32)
    with nc.Block() as blk:
        @blk.sync
        def _(e):
            for key, _sbn, _shape, _dt, srcnm in sel:
                e.dma_start(out=K[key][:, :], in_=hb[srcnm][:, :]).then_inc(C.sems["qs"], 16)

        if derived:
            @blk.gpsimd
            def _(e):
                e.iota(K["iotaq"][:, :], [[128, NT]], base=0, channel_multiplier=1,
                       allow_small_or_imprecise_dtypes=True)
                e.memset(K["ones_r"][:, :], 1.0)
                e.memset(K["ones_rf"][:, :], 1.0)
                e.memset(K["cvals"][:, 0:1], 1e-5).then_inc(C.sems["gp"], 1)
    C.base["qs"] += len(sel) * 16
    if derived:
        C.base["gp"] += 1
    C.consts_ready = (C.base["qs"], C.base["gp"])


def wait_consts(C, e):
    e.wait_ge(C.sems["qs"], C.consts_ready[0])
    e.wait_ge(C.sems["gp"], C.consts_ready[1])



def emit_phase_a(C, x_hbm, xnt_hbm, layer, gate=None, write_xnt_hbm=True, dbg=None):
    nc = C.nc
    g = gate is not None
    xnt_sb = C.sbuf(f"xnt_sb_{layer}", [128, DC * T], BF16)
    sc = C.scope()
    xb = sc.sbuf(f"pa_x_{layer}", [128, 2 * D], F32)
    xnf = sc.sbuf(f"pa_xn_{layer}", [128, 2 * D], F32)
    xsq = sc.sbuf(f"pa_sq_{layer}", [128, D], F32)
    st = sc.sbuf(f"pa_st_{layer}", [128, 8], F32)
    ps_t = sc.psum(f"pa_pt_{layer}", [128, 2 * D], F32)
    if g:
        gf32 = sc.sbuf(f"pa_gf32_{layer}", [128, 2 * D], F32)
        ohsb = sc.sbuf(f"pa_oh_{layer}", [128, 2 * 8], BF16)
        xtokb = sc.sbuf(f"pa_xtok_{layer}", [128, 2 * D], BF16)
        gatew = sc.sbuf(f"pa_gatew_{layer}", [128, 8 * DC], F32)
        ps_g = sc.psum(f"pa_pg_{layer}", [128, 2 * 512], F32)

    tbl = {
        "qs": ["x"],
        "dv": (["var", "rstd", "xn"] + (["gmax", "oh"] if g else [])),
        "ac": (["sq", "sqrt", "gf32", "xnt", "xtok"] if g
               else ["sq", "sqrt", "xnt"]),
        "pe": (["tr", "gmm"] if g else ["tr"]),
        "qa": (["oh", "xtok"] if g else []),
    }
    S = Sched(C, NT, tbl, pre={"qs": 1 if g else 0})

    with nc.Block() as blk:
        @blk.sync
        def _(e):
            if g:
                e.dma_start(out=gatew.ap().rearrange("p (c f) -> p c f", c=DC),
                            in_=gate["gateT_hbm"].rearrange("(c p) f -> p c f", p=128)
                            ).then_inc(C.sems["qs"], 16)
            for i in range(NT):
                if i >= 2:
                    e.wait_ge(C.sems["dv"], S.t("dv", i - 2, "xn"))
                    e.wait_ge(C.sems["ac"], S.t("ac", i - 2, "sq"))
                e.dma_start(out=xb[:, (i % 2) * D:(i % 2 + 1) * D],
                            in_=x_hbm.rearrange("(n p) m -> n p m", p=128)[i, :, :]
                            ).then_inc(C.sems["qs"], 16)
            if write_xnt_hbm:
                e.wait_ge(C.sems["ac"], S.t("ac", NT - 1, "xnt"))
                for c in range(DC):
                    e.dma_start(out=xnt_hbm.rearrange("(c p) t -> c p t", p=128)[c, :, :],
                                in_=xnt_sb[:, c * T:(c + 1) * T]).then_inc(C.sems["qs"], 16)

        @blk.vector
        def _(e):
            for i in range(NT):
                buf = xb[:, (i % 2) * D:(i % 2 + 1) * D]
                nb = xnf[:, (i % 2) * D:(i % 2 + 1) * D]
                e.wait_ge(C.sems["qs"], S.t("qs", min(i + 1, NT - 1), "x"))
                negmu = st[:, 0:1]; ex2 = st[:, 1:2]; var = st[:, 2:3]
                rstd = st[:, 3:4]; sq = st[:, 5:6]
                e.reduce_sum(out=negmu, in_=buf, axis=AX.X, negate=True)
                e.wait_ge(C.sems["ac"], S.t("ac", i, "sq"))
                ins = e.reduce_sum(out=ex2, in_=xsq[:, :], axis=AX.X)
                C.settle(e, "dvx", ins)
                e.tensor_scalar(out=negmu, in0=negmu, scalar1=1.0 / D, scalar2=None, op0=ALU.mult)
                ins = e.tensor_scalar(out=ex2, in0=ex2, scalar1=1.0 / D, scalar2=None, op0=ALU.mult)
                C.settle(e, "dvx", ins)
                ins = e.tensor_tensor(out=var, in0=negmu, in1=negmu, op=ALU.mult)
                C.settle(e, "dvx", ins)
                e.tensor_sub(out=var, in0=ex2, in1=var).then_inc(C.sems["dv"], 1)
                e.wait_ge(C.sems["ac"], S.t("ac", i, "sqrt"))
                e.reciprocal(out=rstd, in_=sq).then_inc(C.sems["dv"], 1)
                e.wait_ge(C.sems["dv"], S.t("dv", i, "rstd"))
                if i >= 2:
                    e.wait_ge(C.sems["pe"], S.t("pe", i - 2, "tr"))
                    if g:
                        e.wait_ge(C.sems["ac"], S.t("ac", i - 2, "xtok"))
                e.tensor_scalar(out=nb, in0=buf, scalar1=negmu, scalar2=rstd,
                                op0=ALU.add, op1=ALU.mult).then_inc(C.sems["dv"], 1)
                if g:
                    e.wait_ge(C.sems["pe"], S.t("pe", i, "gmm"))
                    gm = st[:, 6:7]
                    psg = ps_g[:, (i % 2) * 512:(i % 2) * 512 + 8]
                    e.reduce_max(out=gm, in_=psg, axis=AX.X).then_inc(C.sems["dv"], 1)
                    e.wait_ge(C.sems["dv"], S.t("dv", i, "gmax"))
                    if i >= 2:
                        e.wait_ge(C.sems["qa"], S.t("qa", i - 2, "oh"))
                    e.tensor_scalar(out=ohsb[:, (i % 2) * 8:(i % 2 + 1) * 8], in0=psg,
                                    scalar1=gm, scalar2=None,
                                    op0=ALU.is_ge).then_inc(C.sems["dv"], 1)

        @blk.scalar
        def _(e):
            wait_consts(C, e)
            for i in range(NT):
                buf = xb[:, (i % 2) * D:(i % 2 + 1) * D]
                nb = xnf[:, (i % 2) * D:(i % 2 + 1) * D]
                e.wait_ge(C.sems["qs"], S.t("qs", min(i + 1, NT - 1), "x"))
                if i >= 1:
                    e.wait_ge(C.sems["dv"], S.t("dv", i - 1, "var"))
                e.activation(out=xsq[:, :], in_=buf, func=AF.Square).then_inc(C.sems["ac"], 1)
                e.wait_ge(C.sems["dv"], S.t("dv", i, "var"))
                e.activation(out=st[:, 5:6], in_=st[:, 2:3], func=AF.Sqrt,
                             bias=C.consts["cvals"][:, 0:1]).then_inc(C.sems["ac"], 1)
                e.wait_ge(C.sems["pe"], S.t("pe", i, "tr"))
                if g:
                    if i >= 2:
                        e.wait_ge(C.sems["pe"], S.t("pe", i - 2, "gmm"))
                    e.activation(out=gf32[:, (i % 2) * D:(i % 2 + 1) * D],
                                 in_=ps_t[:, (i % 2) * D:(i % 2 + 1) * D],
                                 func=AF.Copy).then_inc(C.sems["ac"], 1)
                e.activation(
                    out=xnt_sb.ap().rearrange("p (c t) -> p c t", c=DC)[:, :, i * 128:(i + 1) * 128],
                    in_=ps_t.ap().rearrange("p (c t) -> p c t", c=2 * DC)[:, (i % 2) * DC:(i % 2 + 1) * DC, :],
                    func=AF.Copy).then_inc(C.sems["ac"], 1)
                if g:
                    if i >= 2:
                        e.wait_ge(C.sems["qa"], S.t("qa", i - 2, "xtok"))
                    e.wait_ge(C.sems["dv"], S.t("dv", i, "xn"))
                    e.activation(out=xtokb[:, (i % 2) * D:(i % 2 + 1) * D], in_=nb,
                                 func=AF.Copy).then_inc(C.sems["ac"], 1)
                    e.wait_ge(C.sems["ac"], S.t("ac", i, "xtok"))
                    e.wait_ge(C.sems["dv"], S.t("dv", i, "oh"))
                    e.dma_start(out=gate["oh_hbm"][i * 128:(i + 1) * 128, :],
                                in_=ohsb[:, (i % 2) * 8:(i % 2 + 1) * 8]).then_inc(C.sems["qa"], 16)
                    e.dma_start(out=gate["xtok_hbm"][i * 128:(i + 1) * 128, :],
                                in_=xtokb[:, (i % 2) * D:(i % 2 + 1) * D]).then_inc(C.sems["qa"], 16)

        @blk.tensor
        def _(e):
            wait_consts(C, e)
            for i in range(NT):
                nb = xnf[:, (i % 2) * D:(i % 2 + 1) * D]
                e.wait_ge(C.sems["dv"], S.t("dv", i, "xn"))
                if i >= 2:
                    e.wait_ge(C.sems["ac"], S.t("ac", i - 2, "xnt"))
                for c in range(DC):
                    ins = e.transpose(ps_t[:, (i % 2) * D + c * 128:(i % 2) * D + (c + 1) * 128],
                                      nb[:, c * 128:(c + 1) * 128], C.consts["ident_f"][:, :])
                ins.then_inc(C.sems["pe"], 1)
                if g:
                    e.wait_ge(C.sems["ac"], S.t("ac", i, "gf32"))
                    e.wait_ge(C.sems["qs"], S.pre_t("qs"))
                    if i >= 2:
                        e.wait_ge(C.sems["dv"], S.t("dv", i - 2, "oh"))
                    for c in range(DC):
                        ins = e.matmul(ps_g[:, (i % 2) * 512:(i % 2) * 512 + 8],
                                       gf32[:, (i % 2) * D + c * 128:(i % 2) * D + (c + 1) * 128],
                                       gatew[:, c * 8:(c + 1) * 8],
                                       start=(c == 0), stop=(c == DC - 1))
                    ins.then_inc(C.sems["pe"], 1)

        if dbg is not None:
            @blk.gpsimd
            def _(e):
                for i in range(NT):
                    e.wait_ge(C.sems["dv"], S.t("dv", i, "xn"))
                    e.dma_start(out=dbg[i * 128:(i + 1) * 128, :],
                                in_=xnf[:, (i % 2) * D:(i % 2 + 1) * D]).then_inc(C.sems["qg"], 16)
                C.base["qg"] += NT * 16


    S.done()
    if write_xnt_hbm:
        C.base["qs"] += DC * 16
    emit_fence(C)
    sc.close()
    return xnt_sb



def emit_l0_kv(C, xnt_sb, hb, parts=3):
    nc = C.nc
    if not hasattr(C, "stage"):
        C.stage = C.sbuf("stage", [128, 4096], F32)
    kt_sb = C.sbuf("l0_kt", [128, 2 * T], BF16)
    vext = [C.sbuf(f"l0_vext{g}", [128, NT * 65], BF16) for g in range(G0)]
    wq_sb = C.sbuf("l0_wq", [128, 8192], BF16)
    bkt_sb = C.sbuf("l0_bkt", [128, 2], F32)
    bqt_sb = C.sbuf("l0_bqt", [128, 8], F32)
    bvb_sb = C.sbuf("l0_bvb", [128, 256], F32)
    wk_sb = C.sbuf("l0_wk", [128, 2048], BF16)
    wv_sb = C.sbuf("l0_wv", [128, 2048], BF16)
    sc = C.scope()
    kb_sb = sc.sbuf("l0_kb", [128, 2 * 512], BF16)
    rot_sb = sc.sbuf("l0_rot", [128, 2 * 512], BF16)
    t1_sb = sc.sbuf("l0_t1", [128, 512], BF16)
    pk = sc.psum("l0_pk", [128, 2 * 512], F32)
    pr = sc.psum("l0_pr", [128, 2 * 512], F32)
    pv = sc.psum("l0_pv", [128, 2 * 512], F32)
    C.l0 = dict(kt=kt_sb, vext=vext, wq=wq_sb, wk=wk_sb, wv=wv_sb,
                bkt=bkt_sb, bqt=bqt_sb, bvb=bvb_sb)

    Sp = Serial(C)
    Sp.add("sy", lambda e: e.dma_start(out=bkt_sb[:, :], in_=hb["bkt"][:, :]), dma=True)
    Sp.add("sy", lambda e: e.dma_start(out=bqt_sb[:, :], in_=hb["bqt"][:, :]), dma=True)
    Sp.add("sy", lambda e: e.dma_start(out=bvb_sb[:, :], in_=hb["bvb"][:, :]), dma=True)
    for hh in range(2):
        Sp.add("sy", lambda e, hh=hh: e.dma_start(
            out=C.stage.ap().rearrange("p (c f) -> p c f", c=4),
            in_=hb["wq"].rearrange("(g c p) f -> g p c f", p=128, c=4)[hh, :, :, :]), dma=True)
        Sp.add("gp", lambda e, hh=hh: e.tensor_copy(out=wq_sb[:, hh * 4096:(hh + 1) * 4096],
                                                    in_=C.stage[:, :]))
    Sp.add("sy", lambda e: e.dma_start(out=C.stage[:, 0:2048].rearrange("p (c f) -> p c f", c=DC),
                                       in_=hb["wk"].rearrange("(c p) f -> p c f", p=128)), dma=True)
    Sp.add("gp", lambda e: e.tensor_copy(out=wk_sb[:, :], in_=C.stage[:, 0:2048]))
    Sp.add("sy", lambda e: e.dma_start(out=C.stage[:, 0:2048].rearrange("p (c f) -> p c f", c=DC),
                                       in_=hb["wv"].rearrange("(c p) f -> p c f", p=128)), dma=True)
    Sp.add("gp", lambda e: e.tensor_copy(out=wv_sb[:, :], in_=C.stage[:, 0:2048]))
    def _memsets(e):
        for g in range(G0):
            e.memset(vext[g][:, :], 1.0)
        return e.memset(kt_sb[:, :], 0.0)
    Sp.add("gp", _memsets)
    Sp.emit()
    emit_fence(C)

    NS = 16 if parts in (2, 3) else 0
    Sk = Sched(C, NS, {
        "pe": ["kmm", "rot"],
        "dv": ["kb", "kcomb"],
        "ac": ["rote"],
    })

    with nc.Block() as blk:
        @blk.tensor
        def _(e):
            for s in range(NS):
                c2, sl = s // 8, s % 8
                pkb = pk[:, (s % 2) * 512:(s % 2 + 1) * 512]
                if s >= 2:
                    e.wait_ge(C.sems["dv"], Sk.t("dv", s - 2, "kb"))
                for dch in range(DC):
                    ins = e.matmul(pkb,
                                   wk_sb[:, dch * 256 + c2 * 128:dch * 256 + (c2 + 1) * 128],
                                   xnt_sb[:, dch * T + sl * 512:dch * T + (sl + 1) * 512],
                                   start=(dch == 0), stop=(dch == DC - 1))
                ins.then_inc(C.sems["pe"], 1)
                e.wait_ge(C.sems["dv"], Sk.t("dv", s, "kb"))
                if s >= 2:
                    e.wait_ge(C.sems["ac"], Sk.t("ac", s - 2, "rote"))
                ins = e.matmul(pr[:, (s % 2) * 512:(s % 2 + 1) * 512],
                               C.consts["rope"][:, :],
                               kb_sb[:, (s % 2) * 512:(s % 2 + 1) * 512],
                               start=True, stop=True)
                ins.then_inc(C.sems["pe"], 1)

        @blk.scalar
        def _(e):
            for s in range(NS):
                e.wait_ge(C.sems["pe"], Sk.t("pe", s, "rot"))
                if s >= 2:
                    e.wait_ge(C.sems["dv"], Sk.t("dv", s - 2, "kcomb"))
                e.activation(out=rot_sb[:, (s % 2) * 512:(s % 2 + 1) * 512],
                             in_=pr[:, (s % 2) * 512:(s % 2 + 1) * 512],
                             func=AF.Copy).then_inc(C.sems["ac"], 1)

        @blk.vector
        def _(e):
            for s in range(NS):
                c2, sl = s // 8, s % 8
                cs = slice(sl * 512, (sl + 1) * 512)
                e.wait_ge(C.sems["pe"], Sk.t("pe", s, "kmm"))
                if s >= 2:
                    e.wait_ge(C.sems["pe"], Sk.t("pe", s - 2, "rot"))
                e.tensor_scalar(out=kb_sb[:, (s % 2) * 512:(s % 2 + 1) * 512],
                                in0=pk[:, (s % 2) * 512:(s % 2 + 1) * 512],
                                scalar1=bkt_sb[:, c2:c2 + 1], scalar2=None,
                                op0=ALU.add).then_inc(C.sems["dv"], 1)
                e.wait_ge(C.sems["ac"], Sk.t("ac", s, "rote"))
                e.tensor_tensor(out=t1_sb[:, :], in0=kb_sb[:, (s % 2) * 512:(s % 2 + 1) * 512],
                                in1=C.consts["cosk"][:, cs], op=ALU.mult)
                e.tensor_tensor(out=kt_sb[:, c2 * T:][:, cs], in0=rot_sb[:, (s % 2) * 512:(s % 2 + 1) * 512],
                                in1=C.consts["sink"][:, cs], op=ALU.mult)
                e.tensor_tensor(out=kt_sb[:, c2 * T:][:, cs], in0=kt_sb[:, c2 * T:][:, cs],
                                in1=t1_sb[:, :], op=ALU.add).then_inc(C.sems["dv"], 1)
    Sk.done()

    NV = NT if parts in (3, 4) else 0
    Sv = Sched(C, NV, {"pe": ["vmm"], "dv": ["vb"]})
    with nc.Block() as blk:
        @blk.tensor
        def _(e):
            for i in range(NV):
                pvb = pv[:, (i % 2) * 512:(i % 2) * 512 + 256]
                if i >= 2:
                    e.wait_ge(C.sems["dv"], Sv.t("dv", i - 2, "vb"))
                for dch in range(DC):
                    ins = e.matmul(pvb,
                                   xnt_sb[:, dch * T + i * 128:dch * T + (i + 1) * 128],
                                   wv_sb[:, dch * 256:(dch + 1) * 256],
                                   start=(dch == 0), stop=(dch == DC - 1))
                ins.then_inc(C.sems["pe"], 1)

        @blk.vector
        def _(e):
            for i in range(NV):
                e.wait_ge(C.sems["pe"], Sv.t("pe", i, "vmm"))
                for g in range(G0):
                    ins = e.tensor_tensor(out=vext[g][:, i * 65:i * 65 + 64],
                                          in0=pv[:, (i % 2) * 512 + g * 64:(i % 2) * 512 + (g + 1) * 64],
                                          in1=bvb_sb[:, g * 64:(g + 1) * 64], op=ALU.add)
                ins.then_inc(C.sems["dv"], 1)
    Sv.done()
    emit_fence(C)
    sc.close()
    return C.l0



class Serial:

    def __init__(self, C):
        self.C = C
        self.steps = []

    def add(self, eng, fn, dma=False, sw=False):
        self.steps.append((eng, fn, dma, sw))

    def emit(self, extra_waits=None):
        C = self.C
        if "sr" not in C.sems:
            C.sems["sr"] = C._enter(C.nc.semaphore("sem_sr"))
            C.base["sr"] = 0
        targets = []
        prev = None
        for eng, fn, dma, sw in self.steps:
            start = prev
            if sw:
                h = C.swsem()
                prev = (h, 16)
            else:
                C.base["sr"] += 16 if dma else 1
                prev = (C.sems["sr"], C.base["sr"])
            targets.append((eng, fn, dma, sw, start, prev))
        engines = {"pe": "tensor", "dv": "vector", "ac": "scalar", "gp": "gpsimd", "sy": "sync"}
        with C.nc.Block() as blk:
            for ename, bname in engines.items():
                mine = [t for t in targets if t[0] == ename]
                if not mine:
                    continue

                def body(e, mine=mine):
                    if extra_waits:
                        for s, val in extra_waits:
                            e.wait_ge(C.sems[s], val)
                    for _, fn, dma, sw, start, end in mine:
                        if start is not None:
                            e.wait_ge(start[0], start[1])
                        ins = fn(e)
                        ins.then_inc(end[0], 16 if dma else 1)
                getattr(blk, bname)(body)
        self.final = prev



def emit_l0_globals_pre(C, hb, x_hbm):
    nc = C.nc
    g = {}
    g["tgi"] = C.sbuf("g_tgi", [4, 1], I32)
    g["tgb"] = C.sbuf("g_tgb", [128, 4], F32)
    g["nodupb"] = C.sbuf("g_nodupb", [128, 4], F32)
    g["xngt"] = C.sbuf("g_xngt", [128, 32], BF16)
    g["kgt"] = C.sbuf("g_kgt", [128, 8], BF16)
    g["vg"] = C.sbuf("g_vg", [4, 260], BF16)
    g["onesall"] = C.sbuf("g_ones", [128, 128], BF16)
    sc = C.scope()
    g["misc"] = sc.sbuf("g_misc", [128, 16], F32)
    g["eq"] = sc.sbuf("g_eq", [4, 8], F32)
    g["ltm4"] = sc.sbuf("g_ltm4", [4, 4], BF16)
    g["xg"] = sc.sbuf("g_xg", [4, 1024], F32)
    g["xgsq"] = sc.sbuf("g_xgsq", [4, 1024], F32)
    g["xng"] = sc.sbuf("g_xng", [4, 1024], F32)
    g["ctok"] = sc.sbuf("g_ctok", [4, 256], BF16)
    g["cosg"] = sc.sbuf("g_cosg", [128, 8], BF16)
    g["kgb"] = sc.sbuf("g_kgb", [128, 8], BF16)
    ps = sc.psum("g_ps", [128, 512], F32)
    psb = sc.psum("g_psb", [128, 128], BF16)
    S = Serial(C)
    K = C.consts

    S.add("sy", lambda e: e.dma_start(out=g["tgi"][:, :], in_=hb["tg"][:, :]), dma=True)
    S.add("gp", lambda e: e.memset(g["vg"][:, :], 1.0))
    S.add("gp", lambda e: e.memset(g["onesall"][:, :], 1.0))
    S.add("sy", lambda e: e.dma_start(out=g["ltm4"][:, :], in_=hb["ltm4"][:, :]), dma=True)
    S.add("dv", lambda e: e.tensor_copy(out=g["misc"][0:4, 0:1], in_=g["tgi"][:, :]))
    S.add("pe", lambda e: e.transpose(ps[0:1, 0:4], g["misc"][0:4, 0:1], K["ident_f"][0:4, 0:4]))
    S.add("dv", lambda e: e.tensor_copy(out=g["misc"][0:1, 4:8], in_=ps[0:1, 0:4]))
    S.add("pe", lambda e: e.matmul(ps[:, 0:4], K["ones_rf"][0:1, :], g["misc"][0:1, 4:8],
                                   start=True, stop=True))
    S.add("dv", lambda e: e.tensor_copy(out=g["tgb"][:, :], in_=ps[:, 0:4]))
    S.add("pe", lambda e: e.matmul(ps[0:4, 4:8], K["ones_rf"][0:1, 0:4], g["misc"][0:1, 4:8],
                                   start=True, stop=True))
    S.add("dv", lambda e: e.tensor_scalar(out=g["eq"][:, 0:4], in0=ps[0:4, 4:8],
                                          scalar1=g["misc"][0:4, 0:1], scalar2=None,
                                          op0=ALU.subtract))
    S.add("dv", lambda e: e.tensor_single_scalar(out=g["eq"][:, 4:8], in_=g["eq"][0:4, 0:4],
                                                 scalar=0.0, op=ALU.is_equal))
    S.add("dv", lambda e: e.tensor_tensor(out=g["ltm4"][:, :], in0=g["eq"][0:4, 4:8],
                                          in1=g["ltm4"][:, :], op=ALU.mult))
    S.add("pe", lambda e: e.matmul(ps[0:4, 12:16], g["ltm4"][:, :], g["onesall"][0:4, 0:4],
                                   start=True, stop=True))
    S.add("dv", lambda e: e.tensor_single_scalar(out=g["misc"][0:4, 1:2], in_=ps[0:4, 12:13],
                                                 scalar=0.5, op=ALU.is_lt))
    S.add("pe", lambda e: e.transpose(ps[0:1, 0:4], g["misc"][0:4, 1:2], K["ident_f"][0:4, 0:4]))
    S.add("dv", lambda e: e.tensor_copy(out=g["misc"][0:1, 8:12], in_=ps[0:1, 0:4]))
    S.add("pe", lambda e: e.matmul(ps[:, 0:4], K["ones_rf"][0:1, :], g["misc"][0:1, 8:12],
                                   start=True, stop=True))
    S.add("dv", lambda e: e.tensor_copy(out=g["nodupb"][:, :], in_=ps[:, 0:4]))

    S.add("gp", lambda e: e.indirect_dma_start(
        out=g["xg"][:, :], out_offset=None, in_=x_hbm[:, :],
        in_offset=IndirectOffsetOnAxis(ap=g["tgi"][:, :], axis=0)), dma=True, sw=True)
    S.add("ac", lambda e: e.activation(out=g["xgsq"][:, :], in_=g["xg"][:, :], func=AF.Square))
    S.add("dv", lambda e: e.reduce_sum(out=g["misc"][0:4, 2:3], in_=g["xg"][:, :], axis=AX.X, negate=True))
    S.add("dv", lambda e: e.reduce_sum(out=g["misc"][0:4, 3:4], in_=g["xgsq"][:, :], axis=AX.X))
    S.add("dv", lambda e: e.tensor_scalar(out=g["misc"][0:4, 2:3], in0=g["misc"][0:4, 2:3],
                                          scalar1=1.0 / D, scalar2=None, op0=ALU.mult))
    S.add("dv", lambda e: e.tensor_scalar(out=g["misc"][0:4, 3:4], in0=g["misc"][0:4, 3:4],
                                          scalar1=1.0 / D, scalar2=None, op0=ALU.mult))
    S.add("dv", lambda e: e.tensor_tensor(out=g["misc"][0:4, 4:5], in0=g["misc"][0:4, 2:3],
                                          in1=g["misc"][0:4, 2:3], op=ALU.mult))
    S.add("dv", lambda e: e.tensor_sub(out=g["misc"][0:4, 4:5], in0=g["misc"][0:4, 3:4],
                                       in1=g["misc"][0:4, 4:5]))
    S.add("ac", lambda e: e.activation(out=g["misc"][0:4, 5:6], in_=g["misc"][0:4, 4:5],
                                       func=AF.Sqrt, bias=K["cvals"][0:4, 0:1]))
    S.add("dv", lambda e: e.reciprocal(out=g["misc"][0:4, 6:7], in_=g["misc"][0:4, 5:6]))
    S.add("dv", lambda e: e.tensor_scalar(out=g["xng"][:, :], in0=g["xg"][:, :],
                                          scalar1=g["misc"][0:4, 2:3], scalar2=g["misc"][0:4, 6:7],
                                          op0=ALU.add, op1=ALU.mult))
    def xtr(e):
        for c in range(DC):
            ins = e.transpose(ps[:, c * 4:(c + 1) * 4], g["xng"][0:4, c * 128:(c + 1) * 128],
                              K["ident_f"][0:4, 0:4])
        return ins
    S.add("pe", xtr)
    S.add("ac", lambda e: e.activation(out=g["xngt"][:, :], in_=ps[:, 0:32], func=AF.Copy))

    S.add("gp", lambda e: e.indirect_dma_start(
        out=g["ctok"][:, 0:128], out_offset=None, in_=hb["cos_tok"][:, :],
        in_offset=IndirectOffsetOnAxis(ap=g["tgi"][:, :], axis=0)), dma=True, sw=True)
    S.add("gp", lambda e: e.indirect_dma_start(
        out=g["ctok"][:, 128:256], out_offset=None, in_=hb["sin_tok"][:, :],
        in_offset=IndirectOffsetOnAxis(ap=g["tgi"][:, :], axis=0)), dma=True, sw=True)
    S.add("pe", lambda e: e.transpose(psb[:, 0:4], g["ctok"][0:4, 0:128], K["ident_b"][0:4, 0:4]))
    S.add("pe", lambda e: e.transpose(psb[:, 4:8], g["ctok"][0:4, 128:256], K["ident_b"][0:4, 0:4]))
    S.add("ac", lambda e: e.activation(out=g["cosg"][:, :], in_=psb[:, 0:8], func=AF.Copy))

    wk = C.l0["wk"]; wv = C.l0["wv"]; bkt = C.l0["bkt"]; bvb = C.l0["bvb"]
    for c2 in range(2):
        def kmm(e, c2=c2):
            for dch in range(DC):
                ins = e.matmul(ps[:, 64 + c2 * 4:64 + (c2 + 1) * 4],
                               wk[:, dch * 256 + c2 * 128:dch * 256 + (c2 + 1) * 128],
                               g["xngt"][:, dch * 4:(dch + 1) * 4],
                               start=(dch == 0), stop=(dch == DC - 1))
            return ins
        S.add("pe", kmm)
    S.add("dv", lambda e: e.tensor_scalar(out=g["kgb"][:, 0:4], in0=ps[:, 64:68],
                                          scalar1=bkt[:, 0:1], scalar2=None, op0=ALU.add))
    S.add("dv", lambda e: e.tensor_scalar(out=g["kgb"][:, 4:8], in0=ps[:, 68:72],
                                          scalar1=bkt[:, 1:2], scalar2=None, op0=ALU.add))
    S.add("pe", lambda e: e.matmul(ps[:, 72:80], C.consts["rope"][:, :], g["kgb"][:, :],
                                   start=True, stop=True))
    S.add("dv", lambda e: e.tensor_copy(out=g["kgt"][:, :], in_=ps[:, 72:80]))
    def kcomb1(e):
        e.tensor_tensor(out=g["kgt"][:, 0:4], in0=g["kgt"][:, 0:4], in1=g["cosg"][:, 4:8], op=ALU.mult)
        e.tensor_tensor(out=g["kgt"][:, 4:8], in0=g["kgt"][:, 4:8], in1=g["cosg"][:, 4:8], op=ALU.mult)
        e.tensor_tensor(out=g["kgb"][:, 0:4], in0=g["kgb"][:, 0:4], in1=g["cosg"][:, 0:4], op=ALU.mult)
        return e.tensor_tensor(out=g["kgb"][:, 4:8], in0=g["kgb"][:, 4:8], in1=g["cosg"][:, 0:4], op=ALU.mult)
    S.add("dv", kcomb1)
    S.add("dv", lambda e: e.tensor_tensor(out=g["kgt"][:, :], in0=g["kgt"][:, :],
                                          in1=g["kgb"][:, :], op=ALU.add))
    def vgmm(e):
        for dch in range(DC):
            ins = e.matmul(ps[0:4, 256:512], g["xngt"][:, dch * 4:(dch + 1) * 4],
                           wv[:, dch * 256:(dch + 1) * 256],
                           start=(dch == 0), stop=(dch == DC - 1))
        return ins
    S.add("pe", vgmm)
    def vgbias(e):
        for gg in range(G0):
            ins = e.tensor_tensor(out=g["vg"][:, gg * 65:gg * 65 + 64],
                                  in0=ps[0:4, 256 + gg * 64:256 + (gg + 1) * 64],
                                  in1=bvb[0:4, gg * 64:(gg + 1) * 64], op=ALU.add)
        return ins
    S.add("dv", vgbias)
    S.emit(extra_waits=[("qs", C.consts_ready[0]), ("gp", C.consts_ready[1])])
    emit_fence(C)
    sc.close()
    C.l0g = g
    return g



def _win_range(qt):
    t0 = qt * 128
    w0 = max(0, t0 - WIN)
    w1 = min(T, t0 + 128 + WIN)
    return w0, w1


def emit_l0_attn(C, xnt_sb, hb, s_hbm, dbg=None, dbg_it=(0, 5)):
    nc = C.nc
    K = C.consts
    g = C.l0g
    kt = C.l0["kt"]; vext = C.l0["vext"]; wq = C.l0["wq"]; bqt = C.l0["bqt"]
    if not hasattr(C, "stage"):
        C.stage = C.sbuf("stage", [128, 4096], F32)
    wo_sb = C.sbuf("l0_wo", [128, 8192], BF16)
    wo_row = C.sbuf("l0_worow", [1, 1024], BF16)
    sc = C.scope()
    qt_sb = sc.sbuf("a_qt", [128, 4096], BF16)
    qb_sb = sc.sbuf("a_qb", [128, 2 * 512], BF16)
    qrotb = sc.sbuf("a_qrot", [128, 2 * 512], BF16)
    qt1 = sc.sbuf("a_qt1", [128, 512], BF16)
    ext_sb = sc.sbuf("a_ext", [128, 16], BF16)
    ext_f = sc.sbuf("a_extf", [128, 4], F32)
    p_sb = sc.sbuf("a_p", [128, 2 * 656], BF16)
    pt_sb = sc.sbuf("a_pt", [128, 768], BF16)
    st2 = sc.sbuf("a_st2", [128, 4], F32)
    ot_sb = sc.sbuf("a_ot", [128, 4096], BF16)
    s_tile = sc.sbuf("a_stile", [128, 1024], F32)
    ps_s = sc.psum("a_pss", [128, 2048], F32)
    ps_pt = sc.psum("a_pspt", [128, 768], BF16)
    ps_o = sc.psum("a_pso", [128, 256], F32)
    ps_q = sc.psum("a_psq", [128, 1024], F32)

    NG = 8
    NI = 64

    cnt = {k: C.base[k] for k in ENGS}
    Tg = {}

    def res(eng, key):
        cnt[eng] += 16 if eng in ("qs", "qg", "qa") else 1
        Tg[key] = cnt[eng]

    for g4 in range(NG):
        res("pe", ("qmm", g4, 0))
        for oc in range(1, DC):
            res("pe", ("qmm", g4, oc))
            res("pe", ("qrot", g4, oc - 1))
        res("pe", ("qrot", g4, DC - 1))
        res("pe", ("smm", g4, 0))
        for i in range(1, NI):
            res("pe", ("smm", g4, i))
            res("pe", ("tr", g4, i - 1))
            res("pe", ("pv", g4, i - 1))
        res("pe", ("tr", g4, NI - 1))
        res("pe", ("pv", g4, NI - 1))
        for j in range(4):
            res("pe", ("sproj", g4, j))
        for j in range(4):
            res("dv", ("ext01", g4, j))
        for oc in range(DC):
            res("dv", ("qb", g4, oc))
            if oc >= 1:
                res("dv", ("qcomb", g4, oc - 1))
        res("dv", ("qcomb", g4, DC - 1))
        for i in range(NI):
            res("dv", ("max", g4, i))
            res("dv", ("nrm", g4, i))
        for oc in range(DC):
            res("ac", ("qre", g4, oc))
        for i in range(NI):
            res("ac", ("exp", g4, i))
            res("ac", ("pte", g4, i))
            res("ac", ("oev", g4, i))
        for j in range(4):
            res("ac", ("spe", g4, j))
        for j in range(4):
            res("qs", ("sst", g4, j))
    if dbg is not None:
        for nm in ["qt", "p", "st2"]:
            res("qs", ("dbg", nm))

    def tg(key, default=0):
        return Tg.get(key, default)

    def it_of(i):
        h, j = i // 4, i % 4
        return h, j

    Sw = Serial(C)
    for hh in range(2):
        Sw.add("sy", lambda e, hh=hh: e.dma_start(
            out=C.stage.ap().rearrange("p (c f) -> p c f", c=4),
            in_=hb["wo_ext"][0:1024, :].rearrange("(g c p) f -> g p c f", p=128, c=4)[hh, :, :, :]), dma=True)
        Sw.add("gp", lambda e, hh=hh: e.tensor_copy(out=wo_sb[:, hh * 4096:(hh + 1) * 4096],
                                                    in_=C.stage[:, :]))
    Sw.add("sy", lambda e: e.dma_start(out=C.stage[0:1, 0:1024], in_=hb["wo_ext"][1024:1025, :]), dma=True)
    Sw.add("gp", lambda e: e.tensor_copy(out=wo_row[:, :], in_=C.stage[0:1, 0:1024]))
    Sw.emit()
    emit_fence(C)

    with nc.Block() as blk:
        @blk.tensor
        def _(e):
            for g4 in range(NG):
                def emit_qmm(oc):
                    if oc >= 1:
                        e.wait_ge(C.sems["dv"], tg(("qb", g4, oc - 1)))
                    elif g4 >= 1:
                        e.wait_ge(C.sems["ac"], tg(("spe", g4 - 1, 3)))
                    for dch in range(DC):
                        ins = e.matmul(ps_q[:, 0:512],
                                       wq[:, dch * 1024 + oc * 128:dch * 1024 + (oc + 1) * 128],
                                       xnt_sb[:, dch * T + g4 * 512:dch * T + (g4 + 1) * 512],
                                       start=(dch == 0), stop=(dch == DC - 1))
                    ins.then_inc(C.sems["pe"], 1)

                def emit_qrot(oc):
                    e.wait_ge(C.sems["dv"], tg(("qb", g4, oc)))
                    if oc >= 1:
                        e.wait_ge(C.sems["ac"], tg(("qre", g4, oc - 1)))
                    e.matmul(ps_q[:, 512:1024], K["rope"][:, :],
                             qb_sb[:, (oc % 2) * 512:(oc % 2 + 1) * 512],
                             start=True, stop=True).then_inc(C.sems["pe"], 1)

                emit_qmm(0)
                for oc in range(1, DC):
                    emit_qmm(oc)
                    emit_qrot(oc - 1)
                emit_qrot(DC - 1)

                def emit_smm(i):
                    h, j = it_of(i)
                    qt = g4 * 4 + j
                    git = g4 * NI + i
                    oc, hh = head_slot(h)
                    g0 = h // 4
                    c2, gg = g0 // 2, g0 % 2
                    w0, w1 = _win_range(qt)
                    w = w1 - w0
                    nblk = w // 128
                    if git >= 2:
                        pg4, pi = (git - 2) // NI, (git - 2) % NI
                        e.wait_ge(C.sems["ac"], tg(("exp", pg4, pi)))
                    e.wait_ge(C.sems["dv"], tg(("qcomb", g4, oc)))
                    e.wait_ge(C.sems["dv"], tg(("ext01", g4, j)))
                    b = (git % 2) * 1024
                    lhs = qt_sb[hh * 64:(hh + 1) * 64, oc * 512 + j * 128:oc * 512 + (j + 1) * 128]
                    rhs0 = kt[gg * 64:(gg + 1) * 64, c2 * T + w0:c2 * T + w1]
                    n1 = min(512, w)
                    e.matmul(ps_s[:, b:b + n1], lhs, rhs0[:, 0:n1], start=True, stop=(w <= 512))
                    if w > 512:
                        e.matmul(ps_s[:, b + 512:b + w], lhs, rhs0[:, 512:w], start=True, stop=True)
                    qb_idx = qt
                    if qb_idx - 2 >= 0:
                        e.matmul(ps_s[:, b:b + 128], K["ident_b"][:, :], K["winm2a"][:, :],
                                 start=False, stop=True, skip_group_check=True)
                    if qb_idx + 2 <= NT - 1:
                        e.matmul(ps_s[:, b + w - 128:b + w], K["ident_b"][:, :], K["winp2a"][:, :],
                                 start=False, stop=True, skip_group_check=True)
                    e.matmul(ps_s[:, b + w:b + w + 4], lhs, g["kgt"][gg * 64:(gg + 1) * 64, c2 * 4:(c2 + 1) * 4],
                             start=True, stop=False)
                    e.matmul(ps_s[:, b + w:b + w + 4], K["ident_b"][:, :], ext_sb[:, j * 4:(j + 1) * 4],
                             start=False, stop=True,
                             skip_group_check=True).then_inc(C.sems["pe"], 1)

                def emit_tr(i):
                    h, j = it_of(i)
                    qt = g4 * 4 + j
                    git = g4 * NI + i
                    w0, w1 = _win_range(qt)
                    w = w1 - w0
                    nblk = w // 128
                    b2 = (git % 2) * 656
                    e.wait_ge(C.sems["dv"], tg(("nrm", g4, i)))
                    if git >= 1:
                        pg4, pi = (git - 1) // NI, (git - 1) % NI
                        e.wait_ge(C.sems["ac"], tg(("pte", pg4, pi)))
                    for c in range(nblk):
                        e.transpose(ps_pt[:, c * 128:(c + 1) * 128],
                                    p_sb[:, b2 + c * 128:b2 + (c + 1) * 128], K["ident_b"][:, :])
                    e.transpose(ps_pt[0:4, nblk * 128:(nblk + 1) * 128],
                                p_sb[:, b2 + w:b2 + w + 4],
                                K["ident_b"][:, :]).then_inc(C.sems["pe"], 1)

                def emit_pv(i):
                    h, j = it_of(i)
                    qt = g4 * 4 + j
                    git = g4 * NI + i
                    g0 = h // 4
                    _, hh = head_slot(h)
                    w0, w1 = _win_range(qt)
                    w = w1 - w0
                    nblk = w // 128
                    kb0 = w0 // 128
                    e.wait_ge(C.sems["ac"], tg(("pte", g4, i)))
                    if git >= 2:
                        pg4, pi = (git - 2) // NI, (git - 2) % NI
                        e.wait_ge(C.sems["ac"], tg(("oev", pg4, pi)))
                    ob = (git % 2) * 128
                    for c in range(nblk):
                        e.matmul(ps_o[hh * 64:(hh + 1) * 64, ob:ob + 128],
                                 vext[g0][:, (kb0 + c) * 65:(kb0 + c) * 65 + 64],
                                 pt_sb[:, c * 128:(c + 1) * 128],
                                 start=(c == 0), stop=False)
                    e.matmul(ps_o[hh * 64:(hh + 1) * 64, ob:ob + 128],
                             g["vg"][0:4, g0 * 65:g0 * 65 + 64],
                             pt_sb[0:4, nblk * 128:(nblk + 1) * 128],
                             start=False, stop=True).then_inc(C.sems["pe"], 1)

                emit_smm(0)
                for i in range(1, NI):
                    emit_smm(i)
                    emit_tr(i - 1)
                    emit_pv(i - 1)
                emit_tr(NI - 1)
                emit_pv(NI - 1)

                for j in range(4):
                    e.wait_ge(C.sems["ac"], tg(("oev", g4, 60 + j)))
                    if j == 0:
                        e.wait_ge(C.sems["dv"], tg(("qb", g4, DC - 1)))
                        e.wait_ge(C.sems["ac"], tg(("qre", g4, DC - 1)))
                    else:
                        e.wait_ge(C.sems["ac"], tg(("spe", g4, j - 1)))
                    for half in range(2):
                        for oc in range(DC):
                            e.matmul(ps_q[:, half * 512:(half + 1) * 512],
                                     ot_sb[:, (j * 8 + oc) * 128:(j * 8 + oc + 1) * 128],
                                     wo_sb[:, oc * 1024 + half * 512:oc * 1024 + (half + 1) * 512],
                                     start=(oc == 0), stop=False)
                        ins = e.matmul(ps_q[:, half * 512:(half + 1) * 512], K["ones_r"][0:1, :],
                                       wo_row[0:1, half * 512:(half + 1) * 512],
                                       start=False, stop=True)
                    ins.then_inc(C.sems["pe"], 1)

        @blk.vector
        def _(e):
            def qcomb(g4, oc):
                cs = slice(g4 * 512, (g4 + 1) * 512)
                e.wait_ge(C.sems["ac"], tg(("qre", g4, oc)))
                if g4 >= 1:
                    e.wait_ge(C.sems["pe"], tg(("smm", g4 - 1, NI - 1)))
                e.tensor_tensor(out=qt1[:, :], in0=qb_sb[:, (oc % 2) * 512:(oc % 2 + 1) * 512],
                                in1=K["cosk"][:, cs], op=ALU.mult)
                e.tensor_tensor(out=qt_sb[:, oc * 512:(oc + 1) * 512],
                                in0=qrotb[:, (oc % 2) * 512:(oc % 2 + 1) * 512],
                                in1=K["sink"][:, cs], op=ALU.mult)
                e.tensor_tensor(out=qt_sb[:, oc * 512:(oc + 1) * 512],
                                in0=qt_sb[:, oc * 512:(oc + 1) * 512],
                                in1=qt1[:, :], op=ALU.add).then_inc(C.sems["dv"], 1)

            for g4 in range(NG):
                for j in range(4):
                    qt = g4 * 4 + j
                    if g4 >= 1:
                        e.wait_ge(C.sems["pe"], tg(("smm", g4 - 1, 60 + j)))
                    i1 = e.tensor_scalar(out=ext_f[:, :], in0=g["tgb"][:, :],
                                         scalar1=K["iotaq"][:, qt:qt + 1], scalar2=None,
                                         op0=ALU.subtract)
                    C.settle(e, "dvx", i1)
                    i2 = e.tensor_tensor(out=ext_f[:, :], in0=ext_f[:, :],
                                         in1=ext_f[:, :], op=ALU.mult)
                    C.settle(e, "dvx", i2)
                    i3 = e.tensor_single_scalar(out=ext_f[:, :], in_=ext_f[:, :],
                                                scalar=(float(WIN) + 0.5) ** 2, op=ALU.is_gt)
                    C.settle(e, "dvx", i3)
                    i4 = e.tensor_tensor(out=ext_f[:, :], in0=ext_f[:, :],
                                         in1=g["nodupb"][:, :], op=ALU.mult)
                    C.settle(e, "dvx", i4)
                    e.tensor_scalar(out=ext_sb[:, j * 4:(j + 1) * 4], in0=ext_f[:, :],
                                    scalar1=1.0, scalar2=1e30, op0=ALU.subtract,
                                    op1=ALU.mult).then_inc(C.sems["dv"], 1)
                for oc in range(DC):
                    e.wait_ge(C.sems["pe"], tg(("qmm", g4, oc)))
                    if oc >= 2:
                        e.wait_ge(C.sems["pe"], tg(("qrot", g4, oc - 2)))
                    e.tensor_scalar(out=qb_sb[:, (oc % 2) * 512:(oc % 2 + 1) * 512],
                                    in0=ps_q[:, 0:512], scalar1=bqt[:, oc:oc + 1],
                                    scalar2=None, op0=ALU.add).then_inc(C.sems["dv"], 1)
                    if oc >= 1:
                        qcomb(g4, oc - 1)
                qcomb(g4, DC - 1)
                for i in range(NI):
                    h, j = it_of(i)
                    qt = g4 * 4 + j
                    git = g4 * NI + i
                    w0, w1 = _win_range(qt)
                    w = w1 - w0
                    b = (git % 2) * 1024
                    b2 = (git % 2) * 656
                    e.wait_ge(C.sems["pe"], tg(("smm", g4, i)))
                    i1 = e.reduce_max(out=st2[:, 0:1], in_=ps_s[:, b:b + w + 4],
                                      axis=AX.X, negate=True)
                    C.settle(e, "dvx", i1)
                    e.tensor_scalar(out=st2[:, 1:2], in0=st2[:, 0:1], scalar1=L0_SCALE,
                                    scalar2=None, op0=ALU.mult).then_inc(C.sems["dv"], 1)
                    e.wait_ge(C.sems["ac"], tg(("exp", g4, i)))
                    i2 = e.reciprocal(out=st2[:, 3:4], in_=st2[:, 2:3])
                    C.settle(e, "dvx", i2)
                    e.tensor_scalar(out=p_sb[:, b2:b2 + w + 4], in0=p_sb[:, b2:b2 + w + 4],
                                    scalar1=st2[:, 3:4], scalar2=None,
                                    op0=ALU.mult).then_inc(C.sems["dv"], 1)

        @blk.scalar
        def _(e):
            for g4 in range(NG):
                for oc in range(DC):
                    e.wait_ge(C.sems["pe"], tg(("qrot", g4, oc)))
                    if oc >= 2:
                        e.wait_ge(C.sems["dv"], tg(("qcomb", g4, oc - 2)))
                    e.activation(out=qrotb[:, (oc % 2) * 512:(oc % 2 + 1) * 512],
                                 in_=ps_q[:, 512:1024], func=AF.Copy).then_inc(C.sems["ac"], 1)
                for i in range(NI):
                    h, j = it_of(i)
                    qt = g4 * 4 + j
                    git = g4 * NI + i
                    w0, w1 = _win_range(qt)
                    w = w1 - w0
                    nblk = w // 128
                    b = (git % 2) * 1024
                    b2 = (git % 2) * 656
                    oc, hh = head_slot(h)
                    e.wait_ge(C.sems["dv"], tg(("max", g4, i)))
                    if git >= 2:
                        pg4, pi = (git - 2) // NI, (git - 2) % NI
                        e.wait_ge(C.sems["pe"], tg(("tr", pg4, pi)))
                    e.activation(out=p_sb[:, b2:b2 + w + 4], in_=ps_s[:, b:b + w + 4],
                                 func=AF.Exp, scale=L0_SCALE, bias=st2[:, 1:2],
                                 accum_out=st2[:, 2:3]).then_inc(C.sems["ac"], 1)
                    e.wait_ge(C.sems["pe"], tg(("tr", g4, i)))
                    if git >= 1:
                        pg4, pi = (git - 1) // NI, (git - 1) % NI
                        e.wait_ge(C.sems["pe"], tg(("pv", pg4, pi)))
                    e.activation(out=pt_sb[:, 0:(nblk + 1) * 128], in_=ps_pt[:, 0:(nblk + 1) * 128],
                                 func=AF.Copy).then_inc(C.sems["ac"], 1)
                    e.wait_ge(C.sems["pe"], tg(("pv", g4, i)))
                    if g4 >= 1:
                        e.wait_ge(C.sems["pe"], tg(("sproj", g4 - 1, 3)))
                    e.activation(out=ot_sb[hh * 64:(hh + 1) * 64, (j * 8 + oc) * 128:(j * 8 + oc + 1) * 128],
                                 in_=ps_o[hh * 64:(hh + 1) * 64, (git % 2) * 128:(git % 2) * 128 + 128],
                                 func=AF.Copy).then_inc(C.sems["ac"], 1)
                for j in range(4):
                    e.wait_ge(C.sems["pe"], tg(("sproj", g4, j)))
                    if g4 >= 1 or j >= 1:
                        pj = (g4, j - 1) if j >= 1 else (g4 - 1, 3)
                        e.wait_ge(C.sems["qs"], tg(("sst",) + pj))

                    e.activation(out=s_tile[:, :],
                                 in_=ps_q[:, 0:1024], func=AF.Copy).then_inc(C.sems["ac"], 1)

        @blk.sync
        def _(e):
            if dbg is not None:
                dg4, di = dbg_it
                git = dg4 * NI + di
                e.wait_ge(C.sems["dv"], tg(("qcomb", 0, DC - 1)))
                e.dma_start(out=dbg["qt"], in_=qt_sb[:, :]).then_inc(C.sems["qs"], 16)
                e.wait_ge(C.sems["dv"], tg(("nrm", dg4, di)))
                e.dma_start(out=dbg["p"], in_=p_sb[:, (git % 2) * 656:(git % 2 + 1) * 656]).then_inc(C.sems["qs"], 16)
                e.dma_start(out=dbg["st2"], in_=st2[:, :]).then_inc(C.sems["qs"], 16)
            for g4 in range(NG):
                for j in range(4):
                    qt = g4 * 4 + j
                    e.wait_ge(C.sems["ac"], tg(("spe", g4, j)))
                    e.dma_start(out=s_hbm[qt * 128:(qt + 1) * 128, :],
                                in_=s_tile[:, :]).then_inc(C.sems["qs"], 16)

    C.base.update({k: cnt[k] for k in cnt})
    emit_fence(C)
    sc.close()
    return dict(wo=wo_sb, wo_row=wo_row)




def emit_l0_fixup(C, hb, sg_hbm, wo, wo_row):
    nc = C.nc
    K = C.consts
    g = C.l0g
    kt = C.l0["kt"]; vext = C.l0["vext"]; wq = C.l0["wq"]; bqt = C.l0["bqt"]
    sc = C.scope()
    qgt = sc.sbuf("f_qgt", [128, 32], BF16)
    qgb = sc.sbuf("f_qgb", [128, 32], BF16)
    cosgt = sc.sbuf("f_cosgt", [128, 64], BF16)
    sgf = sc.sbuf("f_sg", [4, T], F32)
    pgb = sc.sbuf("f_pg", [4, T], BF16)
    st4 = sc.sbuf("f_st4", [4, 4], F32)
    ptg = sc.sbuf("f_ptg", [128, 128], BF16)
    otg = sc.sbuf("f_otg", [128, 32], BF16)
    sgo = sc.sbuf("f_sgo", [4, 1024], F32)
    ps1 = sc.psum("f_ps1", [4, 2048], F32)
    ps2 = sc.psum("f_ps2", [128, 512], F32)
    psb = sc.psum("f_psb", [128, 128], BF16)
    S = Serial(C)

    g2 = {}
    g2["ctok"] = sc.sbuf("f_ctok", [4, 256], BF16)
    S.add("gp", lambda e: e.indirect_dma_start(
        out=g2["ctok"][:, 0:128], out_offset=None, in_=hb["cos_tok"][:, :],
        in_offset=IndirectOffsetOnAxis(ap=g["tgi"][:, :], axis=0)), dma=True, sw=True)
    S.add("gp", lambda e: e.indirect_dma_start(
        out=g2["ctok"][:, 128:256], out_offset=None, in_=hb["sin_tok"][:, :],
        in_offset=IndirectOffsetOnAxis(ap=g["tgi"][:, :], axis=0)), dma=True, sw=True)
    S.add("pe", lambda e: e.transpose(psb[:, 0:4], g2["ctok"][0:4, 0:128], K["ident_b"][0:4, 0:4]))
    S.add("pe", lambda e: e.transpose(psb[:, 4:8], g2["ctok"][0:4, 128:256], K["ident_b"][0:4, 0:4]))
    def ctile2(e):
        for oc in range(DC):
            e.activation(out=cosgt[:, oc * 4:(oc + 1) * 4], in_=psb[:, 0:4], func=AF.Copy)
            ins = e.activation(out=cosgt[:, 32 + oc * 4:32 + (oc + 1) * 4], in_=psb[:, 4:8], func=AF.Copy)
        return ins
    S.add("ac", ctile2)

    def qmm(e):
        for oc in range(DC):
            for dch in range(DC):
                ins = e.matmul(ps2[:, oc * 4:(oc + 1) * 4],
                               wq[:, dch * 1024 + oc * 128:dch * 1024 + (oc + 1) * 128],
                               g["xngt"][:, dch * 4:(dch + 1) * 4],
                               start=(dch == 0), stop=(dch == DC - 1))
        return ins
    S.add("pe", qmm)
    def qbias(e):
        for oc in range(DC):
            ins = e.tensor_scalar(out=qgb[:, oc * 4:(oc + 1) * 4], in0=ps2[:, oc * 4:(oc + 1) * 4],
                                  scalar1=bqt[:, oc:oc + 1], scalar2=None, op0=ALU.add)
        return ins
    S.add("dv", qbias)
    S.add("pe", lambda e: e.matmul(ps2[:, 64:96], K["rope"][:, :], qgb[:, :], start=True, stop=True))
    S.add("dv", lambda e: e.tensor_copy(out=qgt[:, :], in_=ps2[:, 64:96]))
    S.add("dv", lambda e: e.tensor_tensor(out=qgt[:, :], in0=qgt[:, :], in1=cosgt[:, 32:64], op=ALU.mult))
    S.add("dv", lambda e: e.tensor_tensor(out=qgb[:, :], in0=qgb[:, :], in1=cosgt[:, 0:32], op=ALU.mult))
    S.add("dv", lambda e: e.tensor_tensor(out=qgt[:, :], in0=qgt[:, :], in1=qgb[:, :], op=ALU.add))

    for h in range(H0):
        oc, hh = head_slot(h)
        g0 = h // 4
        c2, gg = g0 // 2, g0 % 2
        for half in range(2):
            def smm(e, oc=oc, hh=hh, c2=c2, gg=gg, half=half):
                for sl in range(4):
                    ins = e.matmul(ps1[0:4, sl * 512:(sl + 1) * 512],
                                   qgt[hh * 64:(hh + 1) * 64, oc * 4:(oc + 1) * 4],
                                   kt[gg * 64:(gg + 1) * 64,
                                      c2 * T + half * 2048 + sl * 512:c2 * T + half * 2048 + (sl + 1) * 512],
                                   start=True, stop=True)
                return ins
            S.add("pe", smm)
            S.add("dv", lambda e, half=half: e.tensor_copy(out=sgf[:, half * 2048:(half + 1) * 2048],
                                                           in_=ps1[0:4, 0:2048]))
        S.add("dv", lambda e: e.reduce_max(out=st4[:, 0:1], in_=sgf[:, :], axis=AX.X, negate=True))
        S.add("dv", lambda e: e.tensor_scalar(out=st4[:, 1:2], in0=st4[:, 0:1], scalar1=L0_SCALE,
                                              scalar2=None, op0=ALU.mult))
        S.add("ac", lambda e: e.activation(out=pgb[:, :], in_=sgf[:, :], func=AF.Exp,
                                           scale=L0_SCALE, bias=st4[0:4, 1:2],
                                           accum_out=st4[0:4, 2:3]))
        S.add("dv", lambda e: e.reciprocal(out=st4[:, 3:4], in_=st4[0:4, 2:3]))
        S.add("dv", lambda e: e.tensor_scalar(out=pgb[:, :], in0=pgb[:, :],
                                              scalar1=st4[0:4, 3:4], scalar2=None, op0=ALU.mult))
        def ptr(e):
            for c in range(NT):
                ins = e.transpose(psb[:, c * 4:(c + 1) * 4], pgb[0:4, c * 128:(c + 1) * 128],
                                  K["ident_b"][0:4, 0:4])
            return ins
        S.add("pe", ptr)
        S.add("ac", lambda e: e.activation(out=ptg[:, :], in_=psb[:, :], func=AF.Copy))
        def pv(e, g0=g0, hh=hh):
            for c in range(NT):
                ins = e.matmul(ps2[hh * 64:(hh + 1) * 64, 0:4],
                               vext[g0][:, c * 65:c * 65 + 64],
                               ptg[:, c * 4:(c + 1) * 4],
                               start=(c == 0), stop=(c == NT - 1))
            return ins
        S.add("pe", pv)
        S.add("ac", lambda e, oc=oc, hh=hh: e.activation(
            out=otg[hh * 64:(hh + 1) * 64, oc * 4:(oc + 1) * 4],
            in_=ps2[hh * 64:(hh + 1) * 64, 0:4], func=AF.Copy))

    def sproj(e):
        for half in range(2):
            for oc in range(DC):
                e.matmul(ps1[0:4, half * 512:(half + 1) * 512], otg[:, oc * 4:(oc + 1) * 4],
                         wo[:, oc * 1024 + half * 512:oc * 1024 + (half + 1) * 512],
                         start=(oc == 0), stop=False)
            ins = e.matmul(ps1[0:4, half * 512:(half + 1) * 512], K["ones_r"][0:1, 0:4],
                           wo_row[0:1, half * 512:(half + 1) * 512], start=False, stop=True)
        return ins
    S.add("pe", sproj)
    S.add("ac", lambda e: e.activation(out=sgo[:, :], in_=ps1[0:4, 0:1024], func=AF.Copy))
    S.add("sy", lambda e: e.dma_start(out=sg_hbm[:, :], in_=sgo[:, :]), dma=True)
    S.emit()
    emit_fence(C)
    sc.close()



def emit_l0_ternarize(C, hb, wfc_t_hbm, wproj_t_hbm):
    nc = C.nc
    K = C.consts
    C.lnbrow = C.sbuf("lnbrow", [1, 1024], BF16)
    sc = C.scope()
    wbuf = sc.sbuf("t_w", [128, 2 * 4096], F32)
    habs = sc.sbuf("t_abs", [128, 4096], F32)
    sgn = sc.sbuf("t_sgn", [128, 4096], BF16)
    mag = sc.sbuf("t_mag", [128, 4096], F32)
    tern = sc.sbuf("t_tern", [128, 2 * 4096], BF16)
    acc = sc.sbuf("t_acc", [128, 4], F32)
    thr = sc.sbuf("t_thr", [128, 4], F32)
    onesf = sc.sbuf("t_onesf", [128, 1], F32)
    lnw = sc.sbuf("t_lnw", [128, 32], F32)
    lnbf = sc.sbuf("t_lnbf", [128, 32], F32)
    lnbb = sc.sbuf("t_lnbb", [128, 32], BF16)
    ps = sc.psum("t_ps", [128, 512], F32)
    ps_lb = sc.psum("t_pslb", [1, 1024], F32)
    tiles = [("fc", r) for r in range(8)] + [("pj", r) for r in range(8)]
    NTL = len(tiles)

    def src(m, r):
        if m == "fc":
            return hb["wfcT"].rearrange("(n p) f -> n p f", p=128)[r, :, :]
        return hb["wprojT"].rearrange("(a n p) f -> a p n f", p=128, n=4)[r, :, :, :]

    def dst(m, r):
        if m == "fc":
            return wfc_t_hbm.rearrange("(n p) f -> n p f", p=128)[r, :, :]
        return wproj_t_hbm.rearrange("(a n p) f -> a p n f", p=128, n=4)[r, :, :, :]

    def buf3(t, ti):
        return t.ap().rearrange("p (b n f) -> p b n f", b=2, n=4)[:, ti % 2, :, :]

    S1 = Sched(C, NTL, {"qs": ["w"], "dv": ["acc"]}, pre={"qs": 2})
    with nc.Block() as blk:
        @blk.sync
        def _(e):
            e.dma_start(out=lnw[:, :], in_=hb["lnwT"][:, :]).then_inc(C.sems["qs"], 16)
            e.dma_start(out=lnbf[:, :], in_=hb["lnbT"][:, :]).then_inc(C.sems["qs"], 16)
            for ti, (m, r) in enumerate(tiles):
                if ti >= 2:
                    e.wait_ge(C.sems["dv"], S1.t("dv", ti - 2, "acc"))
                e.dma_start(out=wbuf[:, (ti % 2) * 4096:(ti % 2 + 1) * 4096],
                            in_=src(m, r)).then_inc(C.sems["qs"], 16)

        @blk.gpsimd
        def _(e):
            e.memset(acc[:, :], 0.0)
            e.memset(onesf[:, :], 1.0)
            e.wait_ge(C.sems["qs"], S1.pre_t("qs"))
            e.tensor_copy(out=lnbb[:, :], in_=lnbf[:, :]).then_inc(C.sems["gp"], 1)

        @blk.vector
        def _(e):
            e.wait_ge(C.sems["gp"], C.base["gp"] + 1)
            for ti, (m, r) in enumerate(tiles):
                col = 0 if m == "fc" else 1
                e.wait_ge(C.sems["qs"], S1.t("qs", min(ti + 1, NTL - 1), "w"))
                if ti >= 1:
                    e.wait_ge(C.sems["dv"], S1.t("dv", ti - 1, "acc"))
                i1 = e.reduce_sum(out=acc[:, 2:3], in_=wbuf[:, (ti % 2) * 4096:(ti % 2 + 1) * 4096],
                                  axis=AX.X, apply_absolute_value=True)
                C.settle(e, "dvx", i1)
                e.tensor_add(out=acc[:, col:col + 1], in0=acc[:, col:col + 1],
                             in1=acc[:, 2:3]).then_inc(C.sems["dv"], 1)
    S1.done()
    C.base["gp"] += 1

    Sf = Serial(C)
    Sf.add("pe", lambda e: e.matmul(ps[0:1, 0:2], onesf[:, 0:1], acc[:, 0:2], start=True, stop=True))
    Sf.add("dv", lambda e: e.tensor_scalar(out=acc[0:1, 0:2], in0=ps[0:1, 0:2],
                                           scalar1=1.0 / (4096 * 1024), scalar2=None, op0=ALU.mult))
    Sf.add("dv", lambda e: e.tensor_scalar(out=acc[0:1, 0:2], in0=acc[0:1, 0:2], scalar1=1e-6,
                                           scalar2=-0.5, op0=ALU.add, op1=ALU.mult))
    Sf.add("pe", lambda e: e.matmul(ps[:, 4:6], K["ones_rf"][0:1, :], acc[0:1, 0:2],
                                    start=True, stop=True))
    Sf.add("dv", lambda e: e.tensor_copy(out=thr[:, 0:2], in_=ps[:, 4:6]))
    Sf.add("dv", lambda e: e.tensor_tensor(out=thr[:, 2:4], in0=thr[:, 0:2], in1=thr[:, 0:2],
                                           op=ALU.mult))
    Sf.add("dv", lambda e: e.tensor_scalar(out=thr[:, 2:4], in0=thr[:, 2:4], scalar1=-1.0,
                                           scalar2=None, op0=ALU.mult))
    Sf.emit()
    sr_done = C.base["sr"]
    qs_after_s1 = C.base["qs"]

    S2 = Sched(C, NTL, {"qs": ["w", "out"], "dv": ["tern", "scaled"], "ac": ["sgn", "cmp"], "pe": ["lbmm"]})
    with nc.Block() as blk:
        @blk.sync
        def _(e):
            e.wait_ge(C.sems["sr"], sr_done)
            for ti, (m, r) in enumerate(tiles):
                if ti >= 2:
                    e.wait_ge(C.sems["dv"], S2.t("dv", ti - 2, "scaled"))
                e.dma_start(out=(buf3(wbuf, ti) if m == "pj" else wbuf[:, (ti % 2) * 4096:(ti % 2 + 1) * 4096]),
                            in_=src(m, r)).then_inc(C.sems["qs"], 16)
                e.wait_ge(C.sems["dv"], S2.t("dv", ti, "scaled"))
                e.dma_start(out=dst(m, r),
                            in_=(buf3(tern, ti) if m == "pj" else tern[:, (ti % 2) * 4096:(ti % 2 + 1) * 4096])
                            ).then_inc(C.sems["qs"], 16)

        @blk.scalar
        def _(e):
            e.wait_ge(C.sems["sr"], sr_done)
            for ti, (m, r) in enumerate(tiles):
                col = 0 if m == "fc" else 1
                wb = wbuf[:, (ti % 2) * 4096:(ti % 2 + 1) * 4096]
                e.wait_ge(C.sems["qs"], S2.t("qs", ti, "w"))
                if ti >= 1:
                    e.wait_ge(C.sems["dv"], S2.t("dv", ti - 1, "scaled"))
                i1 = e.activation(out=habs[:, :], in_=wb, func=AF.Square)
                e.activation(out=sgn[:, :], in_=wb, func=AF.Sign).then_inc(C.sems["ac"], 1)
                C.settle(e, "acx", i1)
                e.activation(out=mag[:, :], in_=habs[:, :], func=AF.Sign,
                             bias=thr[:, 2 + col:3 + col]).then_inc(C.sems["ac"], 1)

        @blk.tensor
        def _(e):
            for ti, (m, r) in enumerate(tiles):
                if m != "pj":
                    e.matmul(ps[0:1, 100:101], onesf[0:1, 0:1], onesf[0:1, 0:1],
                             start=True, stop=True).then_inc(C.sems["pe"], 1)
                    continue
                e.wait_ge(C.sems["dv"], S2.t("dv", ti, "tern"))
                tb = tern[:, (ti % 2) * 4096:(ti % 2 + 1) * 4096]
                first = (ti == 8)
                for n in range(4):
                    for half in range(2):
                        ins = e.matmul(ps_lb[0:1, half * 512:(half + 1) * 512],
                                       lnbb[:, (4 * r + n):(4 * r + n) + 1],
                                       tb[:, n * 1024 + half * 512:n * 1024 + (half + 1) * 512],
                                       start=(first and n == 0), stop=(ti == NTL - 1 and n == 3),
                                       skip_group_check=True)
                ins.then_inc(C.sems["pe"], 1)

        @blk.vector
        def _(e):
            e.wait_ge(C.sems["sr"], sr_done)
            e.wait_ge(C.sems["qs"], qs_after_s1)
            for ti, (m, r) in enumerate(tiles):
                e.wait_ge(C.sems["ac"], S2.t("ac", ti, "cmp"))
                if ti >= 2:
                    e.wait_ge(C.sems["qs"], S2.t("qs", ti - 1, "w"))
                i0 = e.tensor_scalar(out=mag[:, :], in0=mag[:, :], scalar1=1.0, scalar2=0.5,
                                     op0=ALU.add, op1=ALU.mult)
                C.settle(e, "dvx", i0)
                tb = tern[:, (ti % 2) * 4096:(ti % 2 + 1) * 4096]
                ins = e.tensor_tensor(out=tb, in0=sgn[:, :], in1=mag[:, :], op=ALU.mult)
                ins.then_inc(C.sems["dv"], 1)
                if m == "pj":
                    e.wait_ge(C.sems["pe"], S2.t("pe", ti, "lbmm"))
                    for n in range(4):
                        ins = e.tensor_scalar(out=tb[:, n * 1024:(n + 1) * 1024],
                                              in0=tb[:, n * 1024:(n + 1) * 1024],
                                              scalar1=lnw[:, 4 * r + n:4 * r + n + 1],
                                              scalar2=None, op0=ALU.mult)
                else:
                    ins = e.tensor_copy(out=acc[:, 3:4], in_=acc[:, 3:4])
                ins.then_inc(C.sems["dv"], 1)
    S2.done()
    Se = Serial(C)
    Se.add("dv", lambda e: e.tensor_copy(out=C.lnbrow[:, :], in_=ps_lb[0:1, 0:1024]))
    Se.emit(extra_waits=[("pe", C.base["pe"])])
    emit_fence(C)
    sc.close()



def emit_l0_ffn(C, hb, wfc_t_hbm, wproj_t_hbm, xnt_hbm, x_hbm, s_hbm, sg_hbm, x1_hbm):
    import os
    NTF = int(os.environ.get("NTF", str(NT)))
    nc = C.nc
    K = C.consts
    sc = C.scope()
    wfc = sc.sbuf("ff_wfc", [128, 8 * 4096], BF16)
    wpj = sc.sbuf("ff_wpj", [128, 32 * 1024], BF16)
    lnbrow = C.lnbrow
    xg = sc.sbuf("ff_xg", [128, 4096], BF16)
    hg = sc.sbuf("ff_hg", [128, 4096], F32)
    hn = sc.sbuf("ff_hn", [128, 4096], BF16)
    hnt = sc.sbuf("ff_hnt", [128, 4096], BF16)
    gs = sc.sbuf("ff_gs", [128, 1024], F32)
    xt = sc.sbuf("ff_x", [128, 1024], F32)
    st = sc.sbuf("ff_s", [128, 1024], F32)
    bn6 = sc.sbuf("ff_bn6", [128, 56], F32)
    selt = sc.sbuf("ff_sel", [128, 4], F32)
    ggsb = sc.sbuf("ff_gg", [4, 1024], F32)
    sgt = sc.sbuf("ff_sgt", [4, 1024], F32)
    xg4 = sc.sbuf("ff_xg4", [4, 1024], F32)
    tgi = sc.sbuf("ff_tgi", [4, 1], I32)
    tgb = sc.sbuf("ff_tgb", [128, 4], F32)
    ps_h = sc.psum("ff_psh", [128, 1024], F32)
    ps_pt = sc.psum("ff_pspt", [128, 4096], BF16)
    ps_g = sc.psum("ff_psg", [128, 1024], F32)

    Sp = Serial(C)
    for r in range(8):
        Sp.add("sy", lambda e, r=r: e.dma_start(
            out=wfc[:, r * 4096:(r + 1) * 4096],
            in_=wfc_t_hbm[r * 128:(r + 1) * 128, :]), dma=True)
    for r in range(32):
        Sp.add("sy", lambda e, r=r: e.dma_start(
            out=wpj[:, r * 1024:(r + 1) * 1024],
            in_=wproj_t_hbm[r * 128:(r + 1) * 128, :]), dma=True)
    Sp.add("sy", lambda e: e.dma_start(out=tgi[:, :], in_=hb["tg"][:, :]), dma=True)
    Sp.add("sy", lambda e: e.dma_start(out=sgt[:, :], in_=sg_hbm[:, :]), dma=True)
    Sp.add("gp", lambda e: e.indirect_dma_start(
        out=xg4[:, :], out_offset=None, in_=x_hbm[:, :],
        in_offset=IndirectOffsetOnAxis(ap=tgi[:, :], axis=0)), dma=True, sw=True)
    Sp.add("gp", lambda e: e.memset(ggsb[:, :], 0.0))
    Sp.add("dv", lambda e: e.tensor_copy(out=tgb[0:4, 0:1], in_=tgi[:, :]))
    Sp.add("pe", lambda e: e.transpose(ps_h[0:1, 0:4], tgb[0:4, 0:1], K["ident_f"][0:4, 0:4]))
    Sp.add("dv", lambda e: e.tensor_copy(out=tgb[0:1, 0:4], in_=ps_h[0:1, 0:4]))
    Sp.add("pe", lambda e: e.matmul(ps_h[:, 4:8], K["ones_rf"][0:1, :], tgb[0:1, 0:4],
                                    start=True, stop=True))
    Sp.add("dv", lambda e: e.tensor_copy(out=tgb[:, :], in_=ps_h[:, 4:8]))
    Sp.emit()
    emit_fence(C)

    cnt = {k: C.base[k] for k in ENGS}
    Tg = {}

    def res(eng, key):
        cnt[eng] += 16 if eng in ("qs", "qg", "qa") else 1
        Tg[key] = cnt[eng]

    for i in range(NTF):
        if i % 4 == 0:
            res("qa", ("xg", i // 4))
        res("qs", ("x", i))
        res("qs", ("s", i))
        for sl in range(8):
            res("pe", ("f", i, sl))
        for sl in range(8):
            res("ac", ("g", i, sl))
        res("dv", ("sel", i))
        res("dv", ("stats", i))
        res("ac", ("sqrt", i))
        res("dv", ("hn", i))
        res("pe", ("tr", i))
        res("dv", ("pte", i))
        res("pe", ("pj", i))
        res("ac", ("silu", i))
        res("pe", ("selmm", i))
        res("dv", ("comb", i))
        res("dv", ("gg", i))
        res("qs", ("x1", i))

    def tg2(key, default=0):
        return Tg.get(key, default)

    with nc.Block() as blk:
        @blk.sync
        def _(e):
            for i in range(NTF):
                e.wait_ge(C.sems["dv"], tg2(("comb", i - 1), 0))
                e.dma_start(out=xt[:, :], in_=x_hbm[i * 128:(i + 1) * 128, :]).then_inc(C.sems["qs"], 16)
                e.dma_start(out=st[:, :], in_=s_hbm[i * 128:(i + 1) * 128, :]).then_inc(C.sems["qs"], 16)
                e.wait_ge(C.sems["dv"], tg2(("comb", i)))
                e.dma_start(out=x1_hbm[i * 128:(i + 1) * 128, :], in_=gs[:, :]).then_inc(C.sems["qs"], 16)

        @blk.tensor
        def _(e):
            for i in range(NTF):
                gi = i // 4
                e.wait_ge(C.sems["qa"], tg2(("xg", gi)))
                xgb = xg[:, :]
                for sl in range(8):
                    if sl >= 2:
                        e.wait_ge(C.sems["ac"], tg2(("g", i, sl - 2)))
                    elif i >= 1:
                        e.wait_ge(C.sems["ac"], tg2(("g", i - 1, 6 + sl)))
                    for dch in range(DC):
                        ins = e.matmul(ps_h[:, (sl % 2) * 512:(sl % 2 + 1) * 512],
                                       xgb[:, dch * 512 + (i % 4) * 128:dch * 512 + (i % 4) * 128 + 128],
                                       wfc[:, dch * 4096 + sl * 512:dch * 4096 + (sl + 1) * 512],
                                       start=(dch == 0), stop=(dch == DC - 1))
                    ins.then_inc(C.sems["pe"], 1)
                e.wait_ge(C.sems["dv"], tg2(("hn", i)))
                if i >= 1:
                    e.wait_ge(C.sems["dv"], tg2(("pte", i - 1)))
                for c in range(32):
                    ins = e.transpose(ps_pt[:, c * 128:(c + 1) * 128],
                                      hn[:, c * 128:(c + 1) * 128], K["ident_b"][:, :])
                ins.then_inc(C.sems["pe"], 1)
                e.wait_ge(C.sems["dv"], tg2(("pte", i)))
                if i >= 1:
                    e.wait_ge(C.sems["ac"], tg2(("silu", i - 1)))
                    e.wait_ge(C.sems["dv"], tg2(("gg", i - 1)))
                for half in range(2):
                    for c in range(32):
                        e.matmul(ps_g[:, half * 512:(half + 1) * 512],
                                 hnt[:, c * 128:(c + 1) * 128],
                                 wpj[:, c * 1024 + half * 512:c * 1024 + (half + 1) * 512],
                                 start=(c == 0), stop=False)
                    ins = e.matmul(ps_g[:, half * 512:(half + 1) * 512], K["ones_r"][0:1, :],
                                   lnbrow[0:1, half * 512:(half + 1) * 512],
                                   start=False, stop=True)
                ins.then_inc(C.sems["pe"], 1)
                e.wait_ge(C.sems["ac"], tg2(("silu", i)))
                e.wait_ge(C.sems["dv"], tg2(("sel", i)))
                e.matmul(ps_g[0:4, 0:512], selt[:, :], gs[:, 0:512], start=True, stop=True)
                ins = e.matmul(ps_g[0:4, 512:1024], selt[:, :], gs[:, 512:1024], start=True, stop=True)
                ins.then_inc(C.sems["pe"], 1)

        @blk.scalar
        def _(e):
            for i in range(NTF):
                if i % 4 == 0:
                    gi = i // 4
                    if gi >= 1:
                        e.wait_ge(C.sems["pe"], tg2(("f", gi * 4 - 1, 7)))
                    e.dma_start(out=xg.ap().rearrange("p (c t) -> p c t", c=DC),
                                in_=xnt_hbm.rearrange("(c p) t -> p c t", p=128)[:, :, gi * 512:(gi + 1) * 512]
                                ).then_inc(C.sems["qa"], 16)
                for sl in range(8):
                    e.wait_ge(C.sems["pe"], tg2(("f", i, sl)))
                    if i >= 1:
                        e.wait_ge(C.sems["dv"], tg2(("hn", i - 1)))
                    e.activation(out=hg[:, sl * 512:(sl + 1) * 512],
                                 in_=ps_h[:, (sl % 2) * 512:(sl % 2 + 1) * 512],
                                 func=AF.Gelu).then_inc(C.sems["ac"], 1)
                e.wait_ge(C.sems["dv"], tg2(("stats", i)))
                e.activation(out=bn6[:, 51:52], in_=bn6[:, 49:50], func=AF.Sqrt,
                             bias=K["cvals"][:, 0:1]).then_inc(C.sems["ac"], 1)
                e.wait_ge(C.sems["pe"], tg2(("pj", i)))
                if i >= 1:
                    e.wait_ge(C.sems["qs"], tg2(("x1", i - 1)))
                e.activation(out=gs[:, :], in_=ps_g[:, 0:1024], func=AF.Silu).then_inc(C.sems["ac"], 1)

        @blk.vector
        def _(e):
            for i in range(NTF):
                i1 = e.tensor_scalar(out=selt[:, :], in0=tgb[:, :], scalar1=K["iotaq"][:, i:i + 1],
                                     scalar2=None, op0=ALU.subtract)
                C.settle(e, "dvx", i1)
                i2 = e.tensor_tensor(out=selt[:, :], in0=selt[:, :], in1=selt[:, :], op=ALU.mult)
                C.settle(e, "dvx", i2)
                e.tensor_single_scalar(out=selt[:, :], in_=selt[:, :], scalar=0.5,
                                       op=ALU.is_lt).then_inc(C.sems["dv"], 1)
                e.wait_ge(C.sems["ac"], tg2(("g", i, 7)))
                for sl in range(8):
                    i3 = e.bn_stats(out=bn6[:, sl * 6:(sl + 1) * 6], in_=hg[:, sl * 512:(sl + 1) * 512])
                C.settle(e, "dvx", i3)
                i4 = e.bn_aggr(out=bn6[:, 48:50], in_=bn6[:, 0:48])
                C.settle(e, "dvx", i4)
                e.tensor_scalar(out=bn6[:, 50:51], in0=bn6[:, 48:49], scalar1=-1.0,
                                scalar2=None, op0=ALU.mult).then_inc(C.sems["dv"], 1)
                e.wait_ge(C.sems["ac"], tg2(("sqrt", i)))
                i5 = e.reciprocal(out=bn6[:, 52:53], in_=bn6[:, 51:52])
                C.settle(e, "dvx", i5)
                if i >= 1:
                    e.wait_ge(C.sems["pe"], tg2(("tr", i - 1)))
                e.tensor_scalar(out=hn[:, :], in0=hg[:, :], scalar1=bn6[:, 50:51],
                                scalar2=bn6[:, 52:53], op0=ALU.add,
                                op1=ALU.mult).then_inc(C.sems["dv"], 1)
                e.wait_ge(C.sems["pe"], tg2(("tr", i)))
                if i >= 1:
                    e.wait_ge(C.sems["pe"], tg2(("pj", i - 1)))
                e.tensor_copy(out=hnt[:, :], in_=ps_pt[:, :]).then_inc(C.sems["dv"], 1)
                e.wait_ge(C.sems["ac"], tg2(("silu", i)))
                e.wait_ge(C.sems["qs"], tg2(("s", i)))
                e.wait_ge(C.sems["pe"], tg2(("selmm", i)))
                i9 = e.tensor_tensor(out=gs[:, :], in0=st[:, :], in1=gs[:, :], op=ALU.mult)
                C.settle(e, "dvx", i9)
                e.tensor_tensor(out=gs[:, :], in0=gs[:, :], in1=xt[:, :],
                                op=ALU.add).then_inc(C.sems["dv"], 1)
                e.wait_ge(C.sems["pe"], tg2(("selmm", i)))
                e.tensor_tensor(out=ggsb[:, :], in0=ggsb[:, :], in1=ps_g[0:4, 0:1024],
                                op=ALU.add).then_inc(C.sems["dv"], 1)

    C.base.update({k: cnt[k] for k in cnt})

    Sa = Serial(C)
    Sa.add("dv", lambda e: e.tensor_tensor(out=sgt[:, :], in0=sgt[:, :], in1=ggsb[:, :], op=ALU.mult))
    Sa.add("dv", lambda e: e.tensor_tensor(out=sgt[:, :], in0=sgt[:, :], in1=xg4[:, :], op=ALU.add))
    Sa.add("gp", lambda e: e.indirect_dma_start(
        out=x1_hbm[:, :], out_offset=IndirectOffsetOnAxis(ap=tgi[:, :], axis=0),
        in_=sgt[:, :], in_offset=None), dma=True, sw=True)
    Sa.emit(extra_waits=[("qs", C.base["qs"])])
    emit_fence(C)
    sc.close()



def emit_l1_kv(C, xnt_sb, hb):
    nc = C.nc
    K = C.consts
    if not hasattr(C, "stage"):
        C.stage = C.sbuf("stage", [128, 4096], F32)
    kvt = C.sbuf("l1_kvt", [128, T], BF16)
    ckvx = C.sbuf("l1_ckvx", [128, NT * 33], BF16)
    wdkr = C.sbuf("l1_wdkr", [128, 8 * 64], BF16)
    wdq = C.sbuf("l1_wdq", [128, 8 * 32], BF16)
    wuq = C.sbuf("l1_wuq", [32, 1024], BF16)
    wqr = C.sbuf("l1_wqr", [32, 256], BF16)
    wuk = C.sbuf("l1_wuk", [128, 8 * 32], BF16)
    bukt = C.sbuf("l1_bukt", [128, 8], BF16)
    wcomb = C.sbuf("l1_wcomb", [33, 8 * 1024], BF16)
    bdq = C.sbuf("l1_bdq", [32, 1], F32)
    bdkr = C.sbuf("l1_bdkr", [64, 1], F32)
    buqt = C.sbuf("l1_buqt", [128, 8], F32)
    bqrt = C.sbuf("l1_bqrt", [32, 8], F32)
    bdkvb = C.sbuf("l1_bdkvb", [128, 32], F32)
    C.l1 = dict(kvt=kvt, ckvx=ckvx, wdq=wdq, wuq=wuq, wqr=wqr, wuk=wuk, bukt=bukt,
                wcomb=wcomb, bdq=bdq, buqt=buqt, bqrt=bqrt)
    sc = C.scope()
    wuvT = sc.sbuf("l1_wuvT", [128, 8 * 32], BF16)
    buvc = sc.sbuf("l1_buvc", [128, 8], BF16)
    bo1 = sc.sbuf("l1_bo1", [1, 1024], F32)
    wo1c = sc.sbuf("l1_wo1c", [128, 1024], BF16)
    ps = sc.psum("l1_ps", [128, 1024], F32)
    psc = sc.psum("l1_psc", [2, 1024], F32)
    st = C.stage

    S = Serial(C)
    S.add("sy", lambda e: e.dma_start(
        out=st[:, 0:256].rearrange("p (c f) -> p c f", c=DC),
        in_=hb["wdkv"].rearrange("(c p) f -> p c f", p=128)), dma=True)
    S.add("sy", lambda e: e.dma_start(
        out=st[:, 256:512].rearrange("p (c f) -> p c f", c=DC),
        in_=hb["wkr"].rearrange("(c p) f -> p c f", p=128)), dma=True)
    def wdkr_cast(e):
        e.tensor_copy(out=wdkr.ap().rearrange("p (c f) -> p c f", c=DC)[:, :, 0:32],
                      in_=st[:, 0:256].rearrange("p (c f) -> p c f", c=DC))
        return e.tensor_copy(out=wdkr.ap().rearrange("p (c f) -> p c f", c=DC)[:, :, 32:64],
                             in_=st[:, 256:512].rearrange("p (c f) -> p c f", c=DC))
    S.add("gp", wdkr_cast)
    S.add("sy", lambda e: e.dma_start(
        out=st[:, 0:256].rearrange("p (c f) -> p c f", c=DC),
        in_=hb["wdq"].rearrange("(c p) f -> p c f", p=128)), dma=True)
    S.add("gp", lambda e: e.tensor_copy(out=wdq[:, :], in_=st[:, 0:256]))
    S.add("sy", lambda e: e.dma_start(out=st[0:32, 0:1024], in_=hb["wuq"][:, :]), dma=True)
    S.add("gp", lambda e: e.tensor_copy(out=wuq[:, :], in_=st[0:32, 0:1024]))
    S.add("sy", lambda e: e.dma_start(out=st[0:32, 0:256], in_=hb["wqr"][:, :]), dma=True)
    S.add("gp", lambda e: e.tensor_copy(out=wqr[:, :], in_=st[0:32, 0:256]))
    S.add("sy", lambda e: e.dma_start(
        out=st[:, 0:256].rearrange("p (h f) -> p h f", h=8),
        in_=hb["wukT"].rearrange("h p f -> p h f")), dma=True)
    S.add("gp", lambda e: e.tensor_copy(out=wuk[:, :], in_=st[:, 0:256]))
    S.add("sy", lambda e: e.dma_start(out=st[:, 256:264], in_=hb["bukt"][:, :]), dma=True)
    S.add("gp", lambda e: e.tensor_copy(out=bukt[:, :], in_=st[:, 256:264]))
    S.add("sy", lambda e: e.dma_start(out=bdq[:, :], in_=hb["bdq"][:, :]), dma=True)
    S.add("sy", lambda e: e.dma_start(out=bdkr[0:32, :], in_=hb["bdkv"][:, :]), dma=True)
    S.add("sy", lambda e: e.dma_start(out=bdkr[32:64, :], in_=hb["bkr"][:, :]), dma=True)
    S.add("sy", lambda e: e.dma_start(out=buqt[:, :], in_=hb["buqt"][:, :]), dma=True)
    S.add("sy", lambda e: e.dma_start(out=bqrt[:, :], in_=hb["bqrt"][:, :]), dma=True)
    S.add("sy", lambda e: e.dma_start(out=bdkvb[:, :], in_=hb["bdkvb"][:, :]), dma=True)
    S.add("sy", lambda e: e.dma_start(
        out=st[:, 0:256].rearrange("p (h f) -> p h f", h=8),
        in_=hb["wuvT"].rearrange("h p f -> p h f")), dma=True)
    S.add("gp", lambda e: e.tensor_copy(out=wuvT[:, :], in_=st[:, 0:256]))
    S.add("sy", lambda e: e.dma_start(out=st[:, 256:264], in_=hb["buvt"][:, :]), dma=True)
    S.add("gp", lambda e: e.tensor_copy(out=buvc[:, :], in_=st[:, 256:264]))
    S.add("sy", lambda e: e.dma_start(out=bo1[:, :], in_=hb["bo1"][:, :]), dma=True)
    S.add("gp", lambda e: e.memset(wcomb[32:33, :], 0.0))
    S.add("gp", lambda e: e.memset(kvt[64:128, :], 0.0))
    S.add("gp", lambda e: e.memset(kvt[64:65, :], 1.0))
    S.add("gp", lambda e: e.memset(ckvx[:, :], 1.0))
    for h in range(L1_NH):
        S.add("sy", lambda e, h=h: e.dma_start(out=st[:, 1024:2048],
                                               in_=hb["wo1"][h * 128:(h + 1) * 128, :]), dma=True)
        S.add("gp", lambda e: e.tensor_copy(out=wo1c[:, :], in_=st[:, 1024:2048]))
        def wcmm(e, h=h):
            e.matmul(ps[0:32, 0:512], wuvT[:, h * 32:(h + 1) * 32], wo1c[:, 0:512],
                     start=True, stop=True)
            return e.matmul(ps[0:32, 512:1024], wuvT[:, h * 32:(h + 1) * 32],
                            wo1c[:, 512:1024], start=True, stop=True)
        S.add("pe", wcmm)
        S.add("dv", lambda e, h=h: e.tensor_copy(out=wcomb[0:32, h * 1024:(h + 1) * 1024],
                                                 in_=ps[0:32, 0:1024]))
        def crmm(e, h=h):
            e.matmul(psc[0:1, 0:512], buvc[:, h:h + 1], wo1c[:, 0:512],
                     start=(h == 0), stop=(h == L1_NH - 1), skip_group_check=True)
            return e.matmul(psc[0:1, 512:1024], buvc[:, h:h + 1], wo1c[:, 512:1024],
                            start=(h == 0), stop=(h == L1_NH - 1), skip_group_check=True)
        S.add("pe", crmm)
    S.add("dv", lambda e: e.tensor_add(out=st[0:1, 0:1024], in0=psc[0:1, 0:1024], in1=bo1[0:1, :]))
    S.add("dv", lambda e: e.tensor_copy(out=wcomb[32:33, 0:1024], in_=st[0:1, 0:1024]))
    S.emit(extra_waits=[("qs", C.consts_ready[0]), ("gp", C.consts_ready[1])])
    emit_fence(C)

    ps_k = sc.psum("l1_psk", [64, 2 * 512], F32)
    ps_v = sc.psum("l1_psv", [128, 2 * 512], F32)
    Sk = Sched(C, 8, {"pe": ["kmm"], "dv": ["kb"]})
    with nc.Block() as blk:
        @blk.tensor
        def _(e):
            for sl in range(8):
                if sl >= 2:
                    e.wait_ge(C.sems["dv"], Sk.t("dv", sl - 2, "kb"))
                for dch in range(DC):
                    ins = e.matmul(ps_k[:, (sl % 2) * 512:(sl % 2 + 1) * 512],
                                   wdkr[:, dch * 64:(dch + 1) * 64],
                                   xnt_sb[:, dch * T + sl * 512:dch * T + (sl + 1) * 512],
                                   start=(dch == 0), stop=(dch == DC - 1))
                ins.then_inc(C.sems["pe"], 1)

        @blk.vector
        def _(e):
            for sl in range(8):
                e.wait_ge(C.sems["pe"], Sk.t("pe", sl, "kmm"))
                e.tensor_scalar(out=kvt[0:64, sl * 512:(sl + 1) * 512],
                                in0=ps_k[:, (sl % 2) * 512:(sl % 2 + 1) * 512],
                                scalar1=bdkr[:, 0:1], scalar2=None,
                                op0=ALU.add).then_inc(C.sems["dv"], 1)
    Sk.done()
    Sv = Sched(C, NT, {"pe": ["vmm"], "dv": ["vb"]})
    with nc.Block() as blk:
        @blk.tensor
        def _(e):
            for i in range(NT):
                if i >= 2:
                    e.wait_ge(C.sems["dv"], Sv.t("dv", i - 2, "vb"))
                for dch in range(DC):
                    ins = e.matmul(ps_v[:, (i % 2) * 512:(i % 2) * 512 + 32],
                                   xnt_sb[:, dch * T + i * 128:dch * T + (i + 1) * 128],
                                   wdkr[:, dch * 64:dch * 64 + 32],
                                   start=(dch == 0), stop=(dch == DC - 1))
                ins.then_inc(C.sems["pe"], 1)

        @blk.vector
        def _(e):
            for i in range(NT):
                e.wait_ge(C.sems["pe"], Sv.t("pe", i, "vmm"))
                e.tensor_tensor(out=ckvx[:, i * 33:i * 33 + 32],
                                in0=ps_v[:, (i % 2) * 512:(i % 2) * 512 + 32],
                                in1=bdkvb[:, :], op=ALU.add).then_inc(C.sems["dv"], 1)
    Sv.done()
    emit_fence(C)
    sc.close()
    return C.l1



def emit_l1_attn(C, xnt_sb, hb, s_hbm):
    nc = C.nc
    K = C.consts
    L = C.l1
    kvt, ckvx = L["kvt"], L["ckvx"]
    sc = C.scope()
    cqt = sc.sbuf("m_cqt", [32, 512], BF16)
    qct = sc.sbuf("m_qct", [128, 4096], BF16)
    qeff = sc.sbuf("m_qeff", [128, 4096], BF16)
    ptb = sc.sbuf("m_ptb", [128, 2 * 2048], BF16)
    norm = sc.sbuf("m_norm", [33, 512], F32)
    mst = sc.sbuf("m_mst", [128, 4], F32)
    rb = sc.sbuf("m_rb", [32, 512], F32)
    ot = sc.sbuf("m_ot", [33, 4096], BF16)
    onesf2 = sc.sbuf("m_onesf2", [64, 64], F32)
    s_tile = sc.sbuf("m_stile", [128, 1024], F32)
    psA = sc.psum("m_psA", [128, 2048], F32)
    psB = sc.psum("m_psB", [128, 2048], F32)

    NG = 8
    cnt = {k: C.base[k] for k in ENGS}
    Tg = {}

    def res(eng, key):
        cnt[eng] += 16 if eng in ("qs", "qg", "qa") else 1
        Tg[key] = cnt[eng]

    for g4 in range(NG):
        res("pe", ("cq", g4))
        for h in range(8):
            res("pe", ("qc", g4, h))
        for h in range(8):
            res("pe", ("qe", g4, h))
        for h in range(8):
            for j in range(4):
                res("pe", ("s1a", g4, h, j))
                res("pe", ("s1b", g4, h, j))
                res("pe", ("mtr", g4, h, j))
            for b in range(8):
                res("pe", ("st4", g4, h, b))
                res("pe", ("pv", g4, h, b))
            res("pe", ("rb", g4, h))
        for j in range(4):
            res("pe", ("sp", g4, j))
        res("dv", ("cqb", g4))
        for h in range(8):
            res("dv", ("qce", g4, h))
        for h in range(8):
            res("dv", ("qeb", g4, h))
        for h in range(8):
            for j in range(4):
                res("dv", ("m0", g4, h, j))
                res("dv", ("m1", g4, h, j))
                res("dv", ("r64", g4, h, j))
            res("dv", ("rcp", g4, h))
            res("dv", ("onrm", g4, h))
        for h in range(8):
            res("ac", ("qea", g4, h))
        for h in range(8):
            for b in range(8):
                res("ac", ("exp", g4, h, b))
            res("ac", ("rbe", g4, h))
        for j in range(4):
            res("ac", ("spe", g4, j))
        for j in range(4):
            res("qs", ("sst", g4, j))

    def tg(key, default=0):
        return Tg.get(key, default)

    with nc.Block() as blk:
        @blk.gpsimd
        def _(e):
            e.memset(onesf2[:, :], 1.0)
            e.memset(qeff[64:128, :], 0.0)
            e.memset(ot[32:33, :], 1.0).then_inc(C.sems["gp"], 1)
        gp_t = C.base["gp"] + 1

        @blk.tensor
        def _(e):
            e.wait_ge(C.sems["gp"], gp_t)
            for g4 in range(NG):
                xsl = slice(g4 * 512, (g4 + 1) * 512)
                if g4 >= 1:
                    e.wait_ge(C.sems["ac"], tg(("spe", g4 - 1, 3)))
                for dch in range(DC):
                    ins = e.matmul(psA[0:32, 0:512], L["wdq"][:, dch * 32:(dch + 1) * 32],
                                   xnt_sb[:, dch * T + g4 * 512:dch * T + (g4 + 1) * 512],
                                   start=(dch == 0), stop=(dch == DC - 1))
                ins.then_inc(C.sems["pe"], 1)
                for h in range(8):
                    e.wait_ge(C.sems["dv"], tg(("cqb", g4)))
                    if h >= 2:
                        e.wait_ge(C.sems["dv"], tg(("qce", g4, h - 2)))
                    e.matmul(psA[:, 512 + (h % 2) * 512:512 + (h % 2) * 512 + 512],
                             L["wuq"][0:32, h * 128:(h + 1) * 128], cqt[:, :],
                             start=True, stop=True).then_inc(C.sems["pe"], 1)
                for h in range(8):
                    e.wait_ge(C.sems["dv"], tg(("qce", g4, h)))
                    if h >= 1:
                        e.wait_ge(C.sems["ac"], tg(("qea", g4, h - 1)))
                        e.wait_ge(C.sems["dv"], tg(("qeb", g4, h - 1)))
                    e.matmul(psA[0:32, 1536:2048], L["wuk"][:, h * 32:(h + 1) * 32],
                             qct[:, h * 512:(h + 1) * 512], start=True, stop=True)
                    e.matmul(psA[32:64, 1536:2048], L["wqr"][0:32, h * 32:(h + 1) * 32],
                             cqt[:, :], start=True, stop=True)
                    e.matmul(psA[64:65, 1536:2048], L["bukt"][:, h:h + 1],
                             qct[:, h * 512:(h + 1) * 512],
                             start=True, stop=True).then_inc(C.sems["pe"], 1)
                for h in range(8):
                    qsl = lambda j: slice(h * 512 + j * 128, h * 512 + (j + 1) * 128)
                    for j in range(4):
                        e.wait_ge(C.sems["ac"], tg(("qea", g4, 7)))
                        e.wait_ge(C.sems["dv"], tg(("qeb", g4, 7)))
                        if j >= 1:
                            e.wait_ge(C.sems["dv"], tg(("r64", g4, h, j - 1)))
                        elif h >= 1:
                            e.wait_ge(C.sems["dv"], tg(("onrm", g4, h - 1)))
                        for sl in range(4):
                            ins = e.matmul(psA[:, sl * 512:(sl + 1) * 512],
                                           qeff[:, qsl(j)],
                                           kvt[:, sl * 512:(sl + 1) * 512],
                                           start=True, stop=True)
                        ins.then_inc(C.sems["pe"], 1)
                        e.wait_ge(C.sems["dv"], tg(("m0", g4, h, j)))
                        for sl in range(4):
                            ins = e.matmul(psA[:, sl * 512:(sl + 1) * 512],
                                           qeff[:, qsl(j)],
                                           kvt[:, 2048 + sl * 512:2048 + (sl + 1) * 512],
                                           start=True, stop=True)
                        ins.then_inc(C.sems["pe"], 1)
                        e.wait_ge(C.sems["dv"], tg(("m1", g4, h, j)))
                        e.matmul(psA[64:65, 1920:2048], mst[:, 2:3], K["ident_f"][:, :],
                                 start=True, stop=True).then_inc(C.sems["pe"], 1)
                    for b in range(8):
                        e.wait_ge(C.sems["dv"], tg(("r64", g4, h, 3)))
                        if b >= 1:
                            e.wait_ge(C.sems["ac"], tg(("exp", g4, h, b - 1)))
                        elif h >= 1:
                            e.wait_ge(C.sems["ac"], tg(("rbe", g4, h - 1)))
                        for c in range(4):
                            kc = b * 4 + c
                            ins = e.matmul(psB[:, c * 512:(c + 1) * 512],
                                           kvt[:, kc * 128:(kc + 1) * 128],
                                           qeff[:, h * 512:(h + 1) * 512],
                                           start=True, stop=True)
                        ins.then_inc(C.sems["pe"], 1)
                        e.wait_ge(C.sems["ac"], tg(("exp", g4, h, b)))
                        if b == 0:
                            e.wait_ge(C.sems["dv"], tg(("onrm", g4, h - 1))) if h >= 1 else None
                        for c in range(4):
                            kc = b * 4 + c
                            ins = e.matmul(psA[0:33, 0:512],
                                           ckvx[:, kc * 33:(kc + 1) * 33],
                                           ptb[:, (b % 2) * 2048 + c * 512:(b % 2) * 2048 + (c + 1) * 512],
                                           start=(b == 0 and c == 0), stop=(b == 7 and c == 3))
                        ins.then_inc(C.sems["pe"], 1)
                    e.wait_ge(C.sems["dv"], tg(("rcp", g4, h)))
                    e.matmul(psB[0:32, 0:512], onesf2[32:33, 0:32], norm[32:33, 0:512],
                             start=True, stop=True).then_inc(C.sems["pe"], 1)
                for j in range(4):
                    e.wait_ge(C.sems["dv"], tg(("onrm", g4, 7)))
                    if j >= 1:
                        e.wait_ge(C.sems["ac"], tg(("spe", g4, j - 1)))
                    for half in range(2):
                        for h in range(8):
                            ins = e.matmul(psB[:, 1024 + half * 512:1024 + (half + 1) * 512],
                                           ot[0:33, h * 512 + j * 128:h * 512 + (j + 1) * 128],
                                           L["wcomb"][0:33, h * 1024 + half * 512:h * 1024 + (half + 1) * 512],
                                           start=(h == 0), stop=(h == 7))
                    ins.then_inc(C.sems["pe"], 1)

        @blk.vector
        def _(e):
            for g4 in range(NG):
                e.wait_ge(C.sems["pe"], tg(("cq", g4)))
                e.tensor_scalar(out=cqt[:, :], in0=psA[0:32, 0:512], scalar1=L["bdq"][:, 0:1],
                                scalar2=None, op0=ALU.add).then_inc(C.sems["dv"], 1)
                for h in range(8):
                    e.wait_ge(C.sems["pe"], tg(("qc", g4, h)))
                    e.tensor_scalar(out=qct[:, h * 512:(h + 1) * 512],
                                    in0=psA[:, 512 + (h % 2) * 512:512 + (h % 2) * 512 + 512],
                                    scalar1=L["buqt"][:, h:h + 1], scalar2=None,
                                    op0=ALU.add).then_inc(C.sems["dv"], 1)
                for h in range(8):
                    e.wait_ge(C.sems["pe"], tg(("qe", g4, h)))
                    e.tensor_scalar(out=qeff[32:64, h * 512:(h + 1) * 512],
                                    in0=psA[32:64, 1536:2048],
                                    scalar1=L["bqrt"][:, h:h + 1], scalar2=None,
                                    op0=ALU.add).then_inc(C.sems["dv"], 1)
                for h in range(8):
                    for j in range(4):
                        e.wait_ge(C.sems["pe"], tg(("s1a", g4, h, j)))
                        e.reduce_max(out=mst[:, 0:1], in_=psA[:, 0:2048], axis=AX.X,
                                     negate=True).then_inc(C.sems["dv"], 1)
                        e.wait_ge(C.sems["pe"], tg(("s1b", g4, h, j)))
                        i1 = e.reduce_max(out=mst[:, 1:2], in_=psA[:, 0:2048], axis=AX.X,
                                          negate=True)
                        C.settle(e, "dvx", i1)
                        e.tensor_tensor(out=mst[:, 2:3], in0=mst[:, 0:1],
                                        in1=mst[:, 1:2], op=ALU.min).then_inc(C.sems["dv"], 1)
                        e.wait_ge(C.sems["pe"], tg(("mtr", g4, h, j)))
                        e.tensor_tensor(out=qeff[64:65, h * 512 + j * 128:h * 512 + (j + 1) * 128],
                                        in0=qeff[64:65, h * 512 + j * 128:h * 512 + (j + 1) * 128],
                                        in1=psA[64:65, 1920:2048],
                                        op=ALU.add).then_inc(C.sems["dv"], 1)
                    e.wait_ge(C.sems["pe"], tg(("pv", g4, h, 7)))
                    i2 = e.reciprocal(out=norm[32:33, 0:512], in_=psA[32:33, 0:512])
                    C.settle(e, "dvx", i2)
                    e.engine_nop().then_inc(C.sems["dv"], 1)
                    e.wait_ge(C.sems["ac"], tg(("rbe", g4, h)))
                    e.tensor_tensor(out=ot[0:32, h * 512:(h + 1) * 512],
                                    in0=psA[0:32, 0:512], in1=rb[:, :],
                                    op=ALU.mult).then_inc(C.sems["dv"], 1)

        @blk.scalar
        def _(e):
            for g4 in range(NG):
                for h in range(8):
                    e.wait_ge(C.sems["pe"], tg(("qe", g4, h)))
                    e.activation(out=qeff[0:32, h * 512:(h + 1) * 512],
                                 in_=psA[0:32, 1536:2048], func=AF.Copy)
                    e.activation(out=qeff[64:65, h * 512:(h + 1) * 512],
                                 in_=psA[64:65, 1536:2048], func=AF.Copy).then_inc(C.sems["ac"], 1)
                for h in range(8):
                    for b in range(8):
                        e.wait_ge(C.sems["pe"], tg(("st4", g4, h, b)))
                        if b >= 2:
                            e.wait_ge(C.sems["pe"], tg(("pv", g4, h, b - 2)))
                        e.activation(out=ptb[:, (b % 2) * 2048:(b % 2 + 1) * 2048],
                                     in_=psB[:, 0:2048], func=AF.Exp,
                                     scale=L1_SCALE).then_inc(C.sems["ac"], 1)
                    e.wait_ge(C.sems["pe"], tg(("rb", g4, h)))
                    e.activation(out=rb[:, :], in_=psB[0:32, 0:512],
                                 func=AF.Copy).then_inc(C.sems["ac"], 1)
                for j in range(4):
                    e.wait_ge(C.sems["pe"], tg(("sp", g4, j)))
                    if g4 >= 1 or j >= 1:
                        pj = (g4, j - 1) if j >= 1 else (g4 - 1, 3)
                        e.wait_ge(C.sems["qs"], tg(("sst",) + pj))
                    e.activation(out=s_tile[:, :], in_=psB[:, 1024:2048],
                                 func=AF.Copy).then_inc(C.sems["ac"], 1)

        @blk.sync
        def _(e):
            for g4 in range(NG):
                for j in range(4):
                    qt = g4 * 4 + j
                    e.wait_ge(C.sems["ac"], tg(("spe", g4, j)))
                    e.dma_start(out=s_hbm[qt * 128:(qt + 1) * 128, :],
                                in_=s_tile[:, :]).then_inc(C.sems["qs"], 16)

    C.base.update({k: cnt[k] for k in cnt})
    C.base["gp"] = gp_t
    emit_fence(C)
    sc.close()



def emit_l1_moe(C, hb, xnt_hbm, xtok_hbm, oh_hbm, s_hbm, x1_hbm, y_hbm):
    nc = C.nc
    K = C.consts
    if not hasattr(C, "stage"):
        C.stage = C.sbuf("stage", [128, 4096], F32)
    sc = C.scope()
    wex = sc.sbuf("e_wex", [128, NE * DC * 1024], BF16)
    xg = sc.sbuf("e_xg", [128, 4096], BF16)
    xtok = sc.sbuf("e_xtok", [128, 1024], BF16)
    ohb = sc.sbuf("e_oh", [128, 8], BF16)
    ohf = sc.sbuf("e_ohf", [128, 8], F32)
    st = sc.sbuf("e_s", [128, 1024], F32)
    x1t = sc.sbuf("e_x1", [128, 1024], F32)
    yac = sc.sbuf("e_yac", [128, 1024], F32)
    tmp = sc.sbuf("e_tmp", [128, 1024], F32)
    gs = sc.sbuf("e_gs", [128, 1024], F32)
    ps_e = sc.psum("e_pse", [128, 2 * 1024], F32)

    Sp = Serial(C)
    for ee in range(NE):
        for hh in range(2):
            Sp.add("sy", lambda e, ee=ee, hh=hh: e.dma_start(
                out=C.stage.ap().rearrange("p (c f) -> p c f", c=4),
                in_=hb["wexT"].rearrange("e (g c p) f -> e g p c f", p=128, c=4)[ee, hh, :, :, :]), dma=True)
            Sp.add("gp", lambda e, ee=ee, hh=hh: e.tensor_copy(
                out=wex.ap().rearrange("p (e c f) -> p e c f", e=NE, c=DC)[:, ee, hh * 4:(hh + 1) * 4, :],
                in_=C.stage.ap().rearrange("p (c f) -> p c f", c=4)))
    Sp.emit()
    emit_fence(C)

    cnt = {k: C.base[k] for k in ENGS}
    Tg = {}

    def res(eng, key):
        cnt[eng] += 16 if eng in ("qs", "qg", "qa") else 1
        Tg[key] = cnt[eng]

    for i in range(NT):
        if i % 4 == 0:
            res("qa", ("xg", i // 4))
        for nm in ["xt", "oh", "s", "x1"]:
            res("qs", (nm, i))
        for ee in range(NE):
            res("pe", ("mm", i, ee))
        res("dv", ("ohf", i))
        for ee in range(NE):
            res("ac", ("ev", i, ee))
            if ee >= 1:
                res("dv", ("ad", i, ee))
        res("dv", ("resid", i))
        res("ac", ("silu", i))
        res("dv", ("comb", i))
        res("qs", ("yo", i))

    def tg(key, default=0):
        return Tg.get(key, default)

    with nc.Block() as blk:
        @blk.sync
        def _(e):
            for i in range(NT):
                if i >= 1:
                    e.wait_ge(C.sems["dv"], tg(("resid", i - 1)))
                e.dma_start(out=xtok[:, :], in_=xtok_hbm[i * 128:(i + 1) * 128, :]).then_inc(C.sems["qs"], 16)
                e.dma_start(out=ohb[:, :], in_=oh_hbm[i * 128:(i + 1) * 128, :]).then_inc(C.sems["qs"], 16)
                if i >= 1:
                    e.wait_ge(C.sems["dv"], tg(("comb", i - 1)))
                e.dma_start(out=st[:, :], in_=s_hbm[i * 128:(i + 1) * 128, :]).then_inc(C.sems["qs"], 16)
                e.dma_start(out=x1t[:, :], in_=x1_hbm[i * 128:(i + 1) * 128, :]).then_inc(C.sems["qs"], 16)
                e.wait_ge(C.sems["dv"], tg(("comb", i)))
                e.dma_start(out=y_hbm[i * 128:(i + 1) * 128, :], in_=gs[:, :]).then_inc(C.sems["qs"], 16)

        @blk.scalar
        def _(e):
            for i in range(NT):
                if i % 4 == 0:
                    gi = i // 4
                    if gi >= 1:
                        e.wait_ge(C.sems["pe"], tg(("mm", gi * 4 - 1, NE - 1)))
                    e.dma_start(out=xg.ap().rearrange("p (c t) -> p c t", c=DC),
                                in_=xnt_hbm.rearrange("(c p) t -> p c t", p=128)[:, :, gi * 512:(gi + 1) * 512]
                                ).then_inc(C.sems["qa"], 16)
                for ee in range(NE):
                    e.wait_ge(C.sems["pe"], tg(("mm", i, ee)))
                    e.wait_ge(C.sems["dv"], tg(("ohf", i)))
                    if ee >= 2:
                        e.wait_ge(C.sems["dv"], tg(("ad", i, ee - 1)))
                    dst = yac if ee == 0 else tmp
                    e.activation(out=dst[:, :], in_=ps_e[:, (ee % 2) * 1024:(ee % 2 + 1) * 1024],
                                 func=AF.Copy, scale=ohf[:, ee:ee + 1]).then_inc(C.sems["ac"], 1)
                e.wait_ge(C.sems["dv"], tg(("resid", i)))
                e.activation(out=gs[:, :], in_=yac[:, :], func=AF.Silu).then_inc(C.sems["ac"], 1)

        @blk.tensor
        def _(e):
            for i in range(NT):
                gi = i // 4
                e.wait_ge(C.sems["qa"], tg(("xg", gi)))
                for ee in range(NE):
                    if ee >= 2:
                        e.wait_ge(C.sems["ac"], tg(("ev", i, ee - 2)))
                    elif i >= 1:
                        e.wait_ge(C.sems["ac"], tg(("ev", i - 1, NE - 2 + ee)))
                    for half in range(2):
                        for dch in range(DC):
                            ins = e.matmul(
                                ps_e[:, (ee % 2) * 1024 + half * 512:(ee % 2) * 1024 + (half + 1) * 512],
                                xg[:, dch * 512 + (i % 4) * 128:dch * 512 + (i % 4) * 128 + 128],
                                wex[:, (ee * 8 + dch) * 1024 + half * 512:(ee * 8 + dch) * 1024 + (half + 1) * 512],
                                start=(dch == 0), stop=(dch == DC - 1))
                    ins.then_inc(C.sems["pe"], 1)

        @blk.vector
        def _(e):
            for i in range(NT):
                e.wait_ge(C.sems["qs"], tg(("oh", i)))
                e.tensor_copy(out=ohf[:, :], in_=ohb[:, :]).then_inc(C.sems["dv"], 1)
                for ee in range(1, NE):
                    e.wait_ge(C.sems["ac"], tg(("ev", i, ee)))
                    e.tensor_add(out=yac[:, :], in0=yac[:, :], in1=tmp[:, :]).then_inc(C.sems["dv"], 1)
                e.wait_ge(C.sems["qs"], tg(("xt", i)))
                i1 = e.tensor_tensor(out=yac[:, :], in0=yac[:, :], in1=xtok[:, :], op=ALU.add)
                C.settle(e, "dvx", i1)
                e.engine_nop().then_inc(C.sems["dv"], 1)
                e.wait_ge(C.sems["ac"], tg(("silu", i)))
                e.wait_ge(C.sems["qs"], tg(("x1", i)))
                if i >= 1:
                    e.wait_ge(C.sems["qs"], tg(("yo", i - 1)))
                i2 = e.tensor_tensor(out=gs[:, :], in0=st[:, :], in1=gs[:, :], op=ALU.mult)
                C.settle(e, "dvx", i2)
                e.tensor_tensor(out=gs[:, :], in0=gs[:, :], in1=x1t[:, :],
                                op=ALU.add).then_inc(C.sems["dv"], 1)

    C.base.update({k: cnt[k] for k in cnt})
    emit_fence(C)
    sc.close()



L0A_CONSTS = None
LB_CONSTS = ["ident_f", "ident_b"]


def _declare(C, hc, ha, const_names, tensors):
    hb = {}
    for nm in (const_names if const_names is not None else
               ["ident_f", "ident_b", "rope_lhsT", "cosk", "sink", "winp2a", "winm2a",
                "cos_tok", "sin_tok", "ltm4"]):
        v = hc[nm]
        dt = F32 if v.dtype == np.float32 else BF16
        hb[nm] = C.nc.dram_tensor(nm, list(v.shape), dt, kind="ExternalInput").ap()
    for nm, kind in tensors:
        v = ha.get(nm)
        if v is not None:
            dt = I32 if v.dtype == np.int32 else (F32 if v.dtype == np.float32 else BF16)
            hb[nm] = C.nc.dram_tensor(nm, list(v.shape), dt, kind=kind).ap()
    return hb


def _finalize(C):
    with C.nc.Block() as blk:
        @blk.gpsimd
        def _(e):
            for s2 in C.base:
                if C.base[s2] > 0:
                    e.wait_ge(C.sems[s2], C.base[s2])
            for h, v in C.pending_sw:
                e.wait_ge(h, v)
    C.pending_sw = []
    C.close()


_BUILD_CACHE = {}


def build_l0a(hc, ha):
    C = Ctx("l0a")
    hb = _declare(C, hc, ha, None,
                  [("tg", "ExternalInput"), ("wq", "ExternalInput"), ("wk", "ExternalInput"),
                   ("wv", "ExternalInput"), ("bkt", "ExternalInput"), ("bqt", "ExternalInput"),
                   ("bvb", "ExternalInput"), ("wo_ext", "ExternalInput")])
    x = C.nc.dram_tensor("x", [T, D], F32, kind="ExternalInput").ap()
    xnt = C.nc.dram_tensor("xnt", [D, T], BF16, kind="ExternalOutput").ap()
    s_hbm = C.nc.dram_tensor("s", [T, D], F32, kind="ExternalOutput").ap()
    sg_hbm = C.nc.dram_tensor("sg", [4, D], F32, kind="ExternalOutput").ap()
    emit_consts(C, hb)
    xnt_sb = emit_phase_a(C, x, xnt, layer=0, gate=None)
    emit_l0_kv(C, xnt_sb, hb)
    emit_l0_globals_pre(C, hb, x)
    w = emit_l0_attn(C, xnt_sb, hb, s_hbm)
    emit_l0_fixup(C, hb, sg_hbm, w["wo"], w["wo_row"])
    _finalize(C)
    return C


def build_l0b(hc, ha):
    C = Ctx("l0b")
    hb = _declare(C, hc, ha, LB_CONSTS,
                  [("tg", "ExternalInput"), ("wfcT", "ExternalInput"),
                   ("wprojT", "ExternalInput"), ("lnwT", "ExternalInput"),
                   ("lnbT", "ExternalInput")])
    x = C.nc.dram_tensor("x", [T, D], F32, kind="ExternalInput").ap()
    xnt = C.nc.dram_tensor("xnt", [D, T], BF16, kind="ExternalInput").ap()
    s_hbm = C.nc.dram_tensor("s", [T, D], F32, kind="ExternalInput").ap()
    sg_hbm = C.nc.dram_tensor("sg", [4, D], F32, kind="ExternalInput").ap()
    wfc_t = C.nc.dram_tensor("wfct", [1024, 4096], BF16, kind="Internal").ap()
    wproj_t = C.nc.dram_tensor("wpjt", [4096, 1024], BF16, kind="Internal").ap()
    x1 = C.nc.dram_tensor("x1", [T, D], F32, kind="ExternalOutput").ap()
    emit_consts(C, hb, names=LB_CONSTS)
    emit_l0_ternarize(C, hb, wfc_t, wproj_t)
    emit_l0_ffn(C, hb, wfc_t, wproj_t, xnt, x, s_hbm, sg_hbm, x1)
    _finalize(C)
    return C


def build_l1a(hc, ha):
    C = Ctx("l1a")
    hb = _declare(C, hc, ha, LB_CONSTS,
                  [("gateT", "ExternalInput"), ("wdq", "ExternalInput"), ("bdq", "ExternalInput"),
                   ("wuq", "ExternalInput"), ("buqt", "ExternalInput"),
                   ("wdkv", "ExternalInput"), ("bdkv", "ExternalInput"),
                   ("wkr", "ExternalInput"), ("bkr", "ExternalInput"),
                   ("wukT", "ExternalInput"), ("bukt", "ExternalInput"),
                   ("wuvT", "ExternalInput"), ("buvt", "ExternalInput"),
                   ("wqr", "ExternalInput"), ("bqrt", "ExternalInput"),
                   ("wo1", "ExternalInput"), ("bo1", "ExternalInput"),
                   ("bdkvb", "ExternalInput")])
    x1 = C.nc.dram_tensor("x1", [T, D], F32, kind="ExternalInput").ap()
    xnt = C.nc.dram_tensor("xnt1", [D, T], BF16, kind="ExternalOutput").ap()
    oh = C.nc.dram_tensor("oh", [T, 8], BF16, kind="ExternalOutput").ap()
    xtok = C.nc.dram_tensor("xtok", [T, D], BF16, kind="ExternalOutput").ap()
    s_hbm = C.nc.dram_tensor("s1", [T, D], F32, kind="ExternalOutput").ap()
    emit_consts(C, hb, names=LB_CONSTS)
    gd = {"gateT_hbm": hb["gateT"], "oh_hbm": oh, "xtok_hbm": xtok}
    xnt_sb = emit_phase_a(C, x1, xnt, layer=1, gate=gd)
    emit_l1_kv(C, xnt_sb, hb)
    emit_l1_attn(C, xnt_sb, hb, s_hbm)
    _finalize(C)
    return C


def build_l1b(hc, ha):
    C = Ctx("l1b")
    hb = _declare(C, hc, ha, LB_CONSTS, [("wexT", "ExternalInput")])
    xnt = C.nc.dram_tensor("xnt1", [D, T], BF16, kind="ExternalInput").ap()
    oh = C.nc.dram_tensor("oh", [T, 8], BF16, kind="ExternalInput").ap()
    xtok = C.nc.dram_tensor("xtok", [T, D], BF16, kind="ExternalInput").ap()
    s_hbm = C.nc.dram_tensor("s1", [T, D], F32, kind="ExternalInput").ap()
    x1 = C.nc.dram_tensor("x1", [T, D], F32, kind="ExternalInput").ap()
    y = C.nc.dram_tensor("y", [T, D], F32, kind="ExternalOutput").ap()
    emit_consts(C, hb, names=LB_CONSTS)
    emit_l1_moe(C, hb, xnt, xtok, oh, s_hbm, x1, y)
    _finalize(C)
    return C


def build_fused(hc, ha):
    C = Ctx("mk")
    hb = _declare(C, hc, ha, None,
                  [("tg", "ExternalInput"), ("wq", "ExternalInput"), ("wk", "ExternalInput"),
                   ("wv", "ExternalInput"), ("bkt", "ExternalInput"), ("bqt", "ExternalInput"),
                   ("bvb", "ExternalInput"), ("wo_ext", "ExternalInput"),
                   ("wfcT", "ExternalInput"), ("wprojT", "ExternalInput"),
                   ("lnwT", "ExternalInput"), ("lnbT", "ExternalInput"),
                   ("gateT", "ExternalInput"), ("wdq", "ExternalInput"), ("bdq", "ExternalInput"),
                   ("wuq", "ExternalInput"), ("buqt", "ExternalInput"),
                   ("wdkv", "ExternalInput"), ("bdkv", "ExternalInput"),
                   ("wkr", "ExternalInput"), ("bkr", "ExternalInput"),
                   ("wukT", "ExternalInput"), ("bukt", "ExternalInput"),
                   ("wuvT", "ExternalInput"), ("buvt", "ExternalInput"),
                   ("wqr", "ExternalInput"), ("bqrt", "ExternalInput"),
                   ("wo1", "ExternalInput"), ("bo1", "ExternalInput"),
                   ("bdkvb", "ExternalInput"), ("wexT", "ExternalInput")])
    x = C.nc.dram_tensor("x", [T, D], F32, kind="ExternalInput").ap()
    xnt0 = C.nc.dram_tensor("xnt0", [D, T], BF16, kind="Internal").ap()
    s0 = C.nc.dram_tensor("s0", [T, D], F32, kind="Internal").ap()
    sg0 = C.nc.dram_tensor("sg0", [4, D], F32, kind="Internal").ap()
    wfc_t = C.nc.dram_tensor("wfct", [1024, 4096], BF16, kind="Internal").ap()
    wproj_t = C.nc.dram_tensor("wpjt", [4096, 1024], BF16, kind="Internal").ap()
    x1 = C.nc.dram_tensor("x1", [T, D], F32, kind="Internal").ap()
    xnt1 = C.nc.dram_tensor("xnt1", [D, T], BF16, kind="Internal").ap()
    oh = C.nc.dram_tensor("oh", [T, 8], BF16, kind="Internal").ap()
    xtok = C.nc.dram_tensor("xtok", [T, D], BF16, kind="Internal").ap()
    s1 = C.nc.dram_tensor("s1", [T, D], F32, kind="Internal").ap()
    y = C.nc.dram_tensor("y", [T, D], F32, kind="ExternalOutput").ap()

    emit_consts(C, hb, names=["ident_f", "ident_b"])

    sc0 = C.scope(); C.arena = sc0
    C.stage = C.sbuf("stage0", [128, 4096], F32)
    emit_consts(C, hb, names=["rope", "cosk", "sink", "winp2a", "winm2a"])
    xnt_sb = emit_phase_a(C, x, xnt0, layer=0, gate=None)
    emit_l0_kv(C, xnt_sb, hb)
    emit_l0_globals_pre(C, hb, x)
    w = emit_l0_attn(C, xnt_sb, hb, s0)
    emit_l0_fixup(C, hb, sg0, w["wo"], w["wo_row"])
    emit_fence(C); C.arena = None; sc0.close(); del C.stage

    sc1 = C.scope(); C.arena = sc1
    emit_l0_ternarize(C, hb, wfc_t, wproj_t)
    emit_l0_ffn(C, hb, wfc_t, wproj_t, xnt0, x, s0, sg0, x1)
    emit_fence(C); C.arena = None; sc1.close()

    sc2 = C.scope(); C.arena = sc2
    C.stage = C.sbuf("stage1", [128, 4096], F32)
    gd = {"gateT_hbm": hb["gateT"], "oh_hbm": oh, "xtok_hbm": xtok}
    xnt_sb1 = emit_phase_a(C, x1, xnt1, layer=1, gate=gd)
    emit_l1_kv(C, xnt_sb1, hb)
    emit_l1_attn(C, xnt_sb1, hb, s1)
    emit_fence(C); C.arena = None; sc2.close(); del C.stage

    emit_l1_moe(C, hb, xnt1, xtok, oh, s1, x1, y)
    _finalize(C)
    return C


def kernel(**inputs):
    hc = host_consts()
    ha = host_layout(inputs)
    x_full = np.ascontiguousarray(np.asarray(inputs["L0_attn_x"], dtype=np.float32))
    n = 8

    weight_names = ["tg", "wq", "wk", "wv", "bkt", "bqt", "bvb", "wo_ext",
                    "wfcT", "wprojT", "lnwT", "lnbT",
                    "gateT", "wdq", "bdq", "wuq", "buqt", "wdkv", "bdkv",
                    "wkr", "bkr", "wukT", "bukt", "wuvT", "buvt",
                    "wqr", "bqrt", "wo1", "bo1", "bdkvb", "wexT"]
    consts_all = ["ident_f", "ident_b", "rope_lhsT", "cosk", "sink", "winp2a",
                  "winm2a", "cos_tok", "sin_tok", "ltm4"]
    base = {nm: hc[nm] for nm in consts_all}
    base.update({nm: ha[nm] for nm in weight_names})
    Cf = build_fused(hc, ha)
    maps = [dict(base, x=x_full[b]) for b in range(n)]
    rf = run_bass_kernel_spmd(Cf.nc, maps, core_ids=list(range(n))).results
    return np.stack([rf[b]["y"] for b in range(n)], axis=0).astype(np.float32)


def kernel_unfused(**inputs):
    hc = host_consts()
    ha = host_layout(inputs)
    x_full = np.ascontiguousarray(np.asarray(inputs["L0_attn_x"], dtype=np.float32))
    n = 8

    def shared(names):
        return {nm: (hc[nm] if nm in hc else ha[nm]) for nm in names}

    consts_all = ["ident_f", "ident_b", "rope_lhsT", "cosk", "sink", "winp2a",
                  "winm2a", "cos_tok", "sin_tok", "ltm4"]
    consts_lb = ["ident_f", "ident_b"]

    Ca = build_l0a(hc, ha)
    base_a = shared(consts_all + ["tg", "wq", "wk", "wv", "bkt", "bqt", "bvb", "wo_ext"])
    maps = [dict(base_a, x=x_full[b]) for b in range(n)]
    ra = run_bass_kernel_spmd(Ca.nc, maps, core_ids=list(range(n))).results

    Cb = build_l0b(hc, ha)
    base_b = shared(consts_lb + ["tg", "wfcT", "wprojT", "lnwT", "lnbT"])
    maps = [dict(base_b, x=x_full[b], xnt=ra[b]["xnt"], s=ra[b]["s"], sg=ra[b]["sg"])
            for b in range(n)]
    rb = run_bass_kernel_spmd(Cb.nc, maps, core_ids=list(range(n))).results

    Cc = build_l1a(hc, ha)
    base_c = shared(consts_lb + ["gateT", "wdq", "bdq", "wuq", "buqt", "wdkv", "bdkv",
                                 "wkr", "bkr", "wukT", "bukt", "wuvT", "buvt",
                                 "wqr", "bqrt", "wo1", "bo1", "bdkvb"])
    maps = [dict(base_c, x1=rb[b]["x1"]) for b in range(n)]
    rc = run_bass_kernel_spmd(Cc.nc, maps, core_ids=list(range(n))).results

    Cd = build_l1b(hc, ha)
    base_d = shared(consts_lb + ["wexT"])
    maps = [dict(base_d, xnt1=rc[b]["xnt1"], oh=rc[b]["oh"], xtok=rc[b]["xtok"],
                 s1=rc[b]["s1"], x1=rb[b]["x1"]) for b in range(n)]
    rd = run_bass_kernel_spmd(Cd.nc, maps, core_ids=list(range(n))).results

    out = np.stack([rd[b]["y"] for b in range(n)], axis=0).astype(np.float32)
    return out
